# version 07
# speedup vs baseline: 1.3368x; 1.0050x over previous
import math
from contextlib import ExitStack

import numpy as np
import concourse.bass as bass
import concourse.mybir as mybir
from concourse.bass_utils import run_bass_kernel_spmd

F32 = mybir.dt.float32
BF16 = mybir.dt.bfloat16
AF = mybir.ActivationFunctionType
ALU = mybir.AluOpType
AX = mybir.AxisListType

NCORES = 8
NTOK = 4096
NT = 32
D = 1024
EPS_LN = 1e-5

class Sem:
    def __init__(self, es, nc, name):
        self.h = es.enter_context(nc.semaphore(name))
        self.n = 0
        self.base = 0
    def inc(self, instr, k=1):
        instr.then_inc(self.h, k)
        self.n += k
        return instr
    def wait(self, eng, val=None):
        eng.wait_ge(self.h, self.n if val is None else val)

class Ctx:
    def __init__(self, nc, es):
        self.nc = nc
        self.es = es
        self.sem_es = es
        self.t = nc.tensor
        self.v = nc.vector
        self.s = nc.scalar
        self.g = nc.gpsimd
        self.sy = nc.sync
        self._semid = 0
        self._pools = {}
        self._active = []
        self._consts = {}
        self.sconst = self.sem("sconst")
        self.sconst.kind = "c"
    def sem(self, name=None, kind="c"):
        pool = self._pools.setdefault(kind, [])
        if pool:
            s = pool.pop()
        else:
            self._semid += 1
            s = Sem(self.sem_es, self.nc, name or f"{kind}{self._semid}")
            s.kind = kind
        s.base = s.n
        self._active.append(s)
        return s
    def release_sems(self):
        for s in self._active:
            if s is not self.sconst:
                self._pools.setdefault(s.kind, []).append(s)
        self._active = [s for s in self._active if s is self.sconst]
    def sb(self, name, shape, dtype, side=None):
        if side is None:
            return self.es.enter_context(self.nc.sbuf_tensor(name, shape, dtype))
        return self.es.enter_context(self.nc.sbuf_tensor(name, shape, dtype, side=side))
    def ps(self, name, shape, dtype):
        return self.es.enter_context(self.nc.psum_tensor(name, shape, dtype))
    def const(self, value, dtype=F32):
        key = (dtype, float(value))
        if key not in self._consts:
            tile = self.sem_es.enter_context(
                self.nc.sbuf_tensor(f"const{len(self._consts)}", [128, 1], dtype))
            self.sconst.inc(self.g.memset(tile[:], float(value)))
            self.nc.const_aps.register(dtype, float(value), tile[:])
            self._consts[key] = tile
        return float(value)

def emit_ln(c, x_ap, xn_bf, sc, sv, scratch):
    v, s = c.v, c.s
    sq, c0, c1_, c2, c3 = (scratch["sq"], scratch["c0"], scratch["c1"],
                           scratch["c2"], scratch["c3"])
    n = x_ap.shape[-1]
    sv.inc(v.tensor_reduce(c0[:], x_ap, AX.X, ALU.add))
    sc.inc(s.activation(sq[:, 0:n], x_ap, AF.Square))
    sc.wait(v)
    sv.inc(v.tensor_reduce(c1_[:], sq[:, 0:n], AX.X, ALU.add))
    sv.wait(v)
    sv.inc(v.tensor_scalar(c2[:], c0[:], 1.0 / n, None, ALU.mult))
    sv.inc(v.tensor_scalar(c1_[:], c1_[:], 1.0 / n, None, ALU.mult))
    sv.wait(v)
    sv.inc(v.tensor_tensor(c3[:], c2[:], c2[:], ALU.mult))
    sv.wait(v)
    sv.inc(v.tensor_tensor(c1_[:], c1_[:], c3[:], ALU.subtract))
    sv.wait(s)
    sc.inc(s.activation(c3[:], c1_[:], AF.Ln, bias=EPS_LN))
    sc.wait(s)
    sc.inc(s.activation(c3[:], c3[:], AF.Exp, scale=-0.5))
    sc.wait(v)
    sv.inc(v.tensor_scalar(xn_bf, x_ap, c2[:], c3[:], ALU.subtract, ALU.mult))
    return c3, c2


def emit_ln_stats(c, x_ap, sc, sv, scratch):
    v, s = c.v, c.s
    sq, c0, c1_, c2, c3 = (scratch["sq"], scratch["c0"], scratch["c1"],
                           scratch["c2"], scratch["c3"])
    n = x_ap.shape[-1]
    sv.wait(v); sv.wait(s); sc.wait(v); sc.wait(s)
    sv.inc(v.tensor_reduce(c0[:], x_ap, AX.X, ALU.add))
    sc.inc(s.activation(sq[:, 0:n], x_ap, AF.Square))
    sc.wait(v)
    sv.inc(v.tensor_reduce(c1_[:], sq[:, 0:n], AX.X, ALU.add))
    sv.wait(v)
    sv.inc(v.tensor_scalar(c2[:], c0[:], 1.0 / n, None, ALU.mult))
    sv.inc(v.tensor_scalar(c1_[:], c1_[:], 1.0 / n, None, ALU.mult))
    sv.wait(v)
    sv.inc(v.tensor_tensor(c3[:], c2[:], c2[:], ALU.mult))
    sv.wait(v)
    sv.inc(v.tensor_tensor(c1_[:], c1_[:], c3[:], ALU.subtract))
    sv.wait(s)
    sc.inc(s.activation(c3[:], c1_[:], AF.Ln, bias=EPS_LN))
    sc.wait(s)
    sc.inc(s.activation(c3[:], c3[:], AF.Exp, scale=-0.5))

def emit_ln_f32(c, x_ap, out_ap, sc, sv, scratch):
    emit_ln_stats(c, x_ap, sc, sv, scratch)
    sc.wait(c.v)
    sv.inc(c.v.tensor_scalar(out_ap, x_ap, scratch["c2"][:], scratch["c3"][:],
                             ALU.subtract, ALU.mult))

def emit_transpose_tiles(c, src_bf, dst_bf, ident, ps_tr, sp, sv, nk):
    t, v = c.t, c.v
    for k in range(nk):
        sp.inc(t.transpose(ps_tr[:, k * 128:(k + 1) * 128],
                           src_bf[:, k * 128:(k + 1) * 128], ident[:]))
    sp.wait(v)
    sv.inc(v.tensor_copy(dst_bf[:, 0:nk * 128], ps_tr[:, 0:nk * 128]))

def emit_identity(c, ident, sg):
    g = c.g
    sg.inc(g.memset(ident[:], 0.0))
    sg.wait(g)
    sg.inc(g.affine_select(out=ident[:], in_=ident[:], compare_op=ALU.not_equal,
                           fill=1.0, base=0, pattern=[[-1, ident.shape[1]]],
                           channel_multiplier=1))

def emit_L0FFN_weights(c, W, name):
    g = c.g
    bwt = c.sb(name + "_bwt", [128, 8, 1024], BF16)
    catT = c.sb(name + "_catT", [128, 8, 118], BF16)
    catB = c.sb(name + "_catB", [112, 1024], BF16)
    baseb = c.sb(name + "_bb", [1, 1024], BF16)
    routerb = c.sb(name + "_rb", [1, 6], BF16)
    onesr = c.sb(name + "_ones", [1, 128], BF16)
    ident = c.sb(name + "_id", [128, 128], BF16)
    sw = c.sem(kind="sw")
    sg = c.sem()
    sw.inc(g.dma_start(out=bwt[:], in_=W["baseWT"].rearrange("(a p) n -> p a n", p=128)), 16)
    sw.inc(g.dma_start(out=catT[:], in_=W["catT"].rearrange("(a p) n -> p a n", p=128)), 16)
    sw.inc(g.dma_start(out=catB[:], in_=W["catB"]), 16)
    sw.inc(g.dma_start(out=baseb[:], in_=W["baseb"]), 16)
    sw.inc(g.dma_start(out=routerb[:], in_=W["routerb"]), 16)
    sg.inc(g.memset(onesr[:], 1.0))
    emit_identity(c, ident, sg)
    return dict(bwt=bwt, catT=catT, catB=catB, baseb=baseb, routerb=routerb,
                onesr=onesr, ident=ident, sw=sw, sg=sg)

def emit_phase_L0FFN(c, Xin, Xout, W, ntiles, name, wsem=None):
    nc = c.nc
    t, v, s, g, sy = c.t, c.v, c.s, c.g, c.sy

    if wsem is None:
        wstate = emit_L0FFN_weights(c, W, name)
    else:
        wstate = wsem
        bwt, catT, catB, baseb, routerb, onesr, ident = (
            wstate["bwt"], wstate["catT"], wstate["catB"], wstate["baseb"],
            wstate["routerb"], wstate["onesr"], wstate["ident"])
        sw, sg = wstate["sw"], wstate["sg"]

    xin = [c.sb(name + f"_xin{i}", [128, 1024], F32) for i in range(2)]
    xn = c.sb(name + "_xn", [128, 1024], BF16)
    xnT = c.sb(name + "_xnT", [128, 1024], BF16)
    Hw = c.sb(name + "_Hw", [128, 112], BF16)
    HwT = c.sb(name + "_HwT", [112, 128], BF16)
    gate = c.sb(name + "_gate", [128, 6], F32)
    gw = c.sb(name + "_gw", [128, 6], F32)
    delta = c.sb(name + "_delta", [128, 1024], F32)
    xout = [c.sb(name + f"_xout{i}", [128, 1024], F32) for i in range(2)]
    scratch = {k: c.sb(name + "_" + k, [128, 1024] if k == "sq" else [128, 1], F32)
               for k in ("sq", "c0", "c1", "c2", "c3")}
    ps_tr = c.ps(name + "_pstr", [128, 1024], BF16)
    ps_h = c.ps(name + "_psh", [128, 128], F32)
    ps_o = [c.ps(name + f"_pso{i}", [128, 1024], F32) for i in range(2)]
    ps_t2 = c.ps(name + "_pst2", [128, 128], BF16)

    c.const(EPS_LN)
    c.sconst.wait(c.s)
    sx = [c.sem(kind="hw"), c.sem(kind="hw")]
    sst = [c.sem(kind="hw"), c.sem(kind="hw")]
    sp = c.sem()
    sv = c.sem()
    sc = c.sem()
    sdone = c.sem()

    sx[0].inc(sy.dma_start(out=xin[0][:], in_=Xin[0]), 16)
    if ntiles > 1:
        sx[1].inc(sy.dma_start(out=xin[1][:], in_=Xin[1]), 16)

    for ti in range(ntiles):
        xb = xin[ti % 2]
        ob = xout[ti % 2]
        po = ps_o[ti % 2]
        for eng in (v, s):
            eng.wait_ge(sx[ti % 2].h, sx[ti % 2].base + 16 * (ti // 2 + 1))
        emit_ln(c, xb[:], xn[:], sc, sv, scratch)
        sc.wait(t)
        if ti == 0:
            sg.wait(t)
            sw.wait(t)
        sv.wait(t)
        emit_transpose_tiles(c, xn, xnT, ident, ps_tr, sp, sv, 8)
        sv.wait(t)
        if ti >= 2:
            t.wait_ge(sdone.h, sdone.base + ti - 1)
        for k in range(8):
            lhs = xnT[:, k * 128:(k + 1) * 128]
            t.matmul(po[:, 0:512], lhs, bwt[:, k, 0:512], start=(k == 0), stop=False)
            t.matmul(po[:, 512:1024], lhs, bwt[:, k, 512:1024], start=(k == 0), stop=False)
            sp.inc(t.matmul(ps_h[:, 0:118], lhs, catT[:, k, :], start=(k == 0), stop=False))
        t.matmul(po[:, 0:512], onesr[:], baseb[:, 0:512], start=False, stop=False)
        t.matmul(po[:, 512:1024], onesr[:], baseb[:, 512:1024], start=False, stop=False)
        sp.inc(t.matmul(ps_h[:, 112:118], onesr[:], routerb[:], start=False, stop=True))
        sp.wait(v)
        sv.inc(v.tensor_reduce(scratch["c0"][:], ps_h[:, 112:118], AX.X, ALU.max))
        sv.wait(v)
        sv.inc(v.tensor_scalar(scratch["c1"][:], scratch["c0"][:], -1.0, None, ALU.mult))
        sv.wait(s)
        sp.wait(s)
        sc.inc(s.activation(gate[:], ps_h[:, 112:118], AF.Exp, bias=scratch["c1"][:]))
        sc.wait(v)
        sv.inc(v.tensor_reduce(scratch["c0"][:], gate[:], AX.X, ALU.add))
        sv.wait(v)
        sv.inc(v.reciprocal(scratch["c1"][:], scratch["c0"][:]))
        sv.wait(v)
        sv.inc(v.tensor_scalar(gate[:], gate[:], scratch["c1"][:], None, ALU.mult))
        sv.wait(v)
        sv.inc(v.tensor_reduce(scratch["c0"][:], gate[:], AX.X, ALU.max))
        sv.wait(v)
        sv.inc(v.tensor_scalar(gw[:], gate[:], scratch["c0"][:], None, ALU.is_ge))
        sv.wait(v)
        sv.inc(v.scalar_tensor_tensor(gw[:], gw[:], -1e30, gate[:], ALU.mult, ALU.add))
        sv.wait(v)
        sv.inc(v.tensor_reduce(scratch["c1"][:], gw[:], AX.X, ALU.max))
        sv.wait(v)
        sv.inc(v.tensor_scalar(gw[:], gate[:], scratch["c1"][:], None, ALU.is_ge))
        sv.wait(v)
        sv.inc(v.tensor_tensor(gw[:], gw[:], gate[:], ALU.mult))
        sv.wait(v)
        sv.inc(v.tensor_reduce(scratch["c2"][:], gw[:], AX.X, ALU.add))
        sv.wait(v)
        sv.inc(v.tensor_scalar(scratch["c2"][:], scratch["c2"][:], 1e-6, None, ALU.add))
        sv.wait(v)
        sv.inc(v.reciprocal(scratch["c2"][:], scratch["c2"][:]))
        sv.wait(v)
        sv.inc(v.tensor_scalar(gw[:], gw[:], scratch["c2"][:], None, ALU.mult))
        sv.wait(v)
        sv.inc(v.tensor_copy(Hw[:, 0:16], ps_h[:, 0:16]))
        for e in range(6):
            sv.inc(v.tensor_scalar(Hw[:, 16 + 16 * e:32 + 16 * e],
                                   ps_h[:, 16 + 16 * e:32 + 16 * e],
                                   gw[:, e:e + 1], None, ALU.mult))
        sv.wait(t)
        sp.inc(t.transpose(ps_t2[0:112, :], Hw[:], ident[:]))
        sp.wait(v)
        sv.inc(v.tensor_copy(HwT[:], ps_t2[0:112, :]))
        sv.wait(t)
        sp.inc(t.matmul(po[:, 0:512], HwT[:], catB[:, 0:512], start=False, stop=True))
        sp.inc(t.matmul(po[:, 512:1024], HwT[:], catB[:, 512:1024], start=False, stop=True))
        sp.wait(s)
        sc.inc(s.activation(delta[:], po[:], AF.Copy, scale=0.5))
        sc.wait(v)
        if ti >= 2:
            v.wait_ge(sst[ti % 2].h, sst[ti % 2].base + 16 * (ti // 2))
        sdone.inc(v.tensor_add(ob[:], delta[:], xb[:]))
        sy.wait_ge(sdone.h, sdone.base + ti + 1)
        sst[ti % 2].inc(sy.dma_start(out=Xout[ti], in_=ob[:]), 16)
        if ti + 2 < ntiles:
            sy.wait_ge(sdone.h, sdone.base + ti + 1)
            sx[ti % 2].inc(sy.dma_start(out=xin[ti % 2][:], in_=Xin[ti + 2]), 16)
    return dict(sst=sst, n=ntiles, wstate=wstate)

def _prep_weights_A(inputs):
    baseWT = np.ascontiguousarray(inputs["L0_ffn_base_W"].T)
    baseb = np.ascontiguousarray(inputs["L0_ffn_base_b"][None, :])
    routedA = inputs["L0_ffn_routed_A"].reshape(96, 1024)
    cat = np.concatenate([inputs["L0_ffn_shared_A"], routedA, inputs["L0_ffn_router_W"]], 0)
    catT = np.ascontiguousarray(cat.T)
    routedB = np.ascontiguousarray(np.transpose(inputs["L0_ffn_routed_B"], (0, 2, 1))).reshape(96, 1024)
    catB = np.ascontiguousarray(np.concatenate([inputs["L0_ffn_shared_B"].T, routedB], 0))
    routerb = np.ascontiguousarray(inputs["L0_ffn_router_b"][None, :])
    return dict(baseWT=baseWT, baseb=baseb, catT=catT, catB=catB, routerb=routerb)

def shard_stream(x):
    shards = []
    for cid in range(NCORES):
        b, h = cid // 2, cid % 2
        main = x[b, h * NTOK:(h + 1) * NTOK].reshape(NT, 128, D)
        halo = np.zeros((1, 128, D), x.dtype)
        if h == 1:
            halo[0, 0:3] = x[b, NTOK - 3:NTOK]
        shards.append(np.ascontiguousarray(np.concatenate([main, halo], 0)))
    return shards

A_W_SHAPES = dict(baseWT=(1024, 1024), baseb=(1, 1024), catT=(1024, 118),
                  catB=(112, 1024), routerb=(1, 6))

def build_A(ntiles=NT + 1):
    nc = bass.Bass("TRN2", target_bir_lowering=False)
    X = nc.dram_tensor("X", [ntiles, 128, D], F32, kind="ExternalInput").ap()
    XO = nc.dram_tensor("XO", [ntiles, 128, D], F32, kind="ExternalOutput").ap()
    Wap = {k: nc.dram_tensor("W_" + k, list(s), F32, kind="ExternalInput").ap()
           for k, s in A_W_SHAPES.items()}
    es = ExitStack()
    c = Ctx(nc, es)
    r = emit_phase_L0FFN(c, X, XO, Wap, ntiles, "A")
    for smm in r["sst"]:
        smm.wait(c.sy)
    es.close()
    return nc



B_W_SHAPES = dict(
    ipT=(1024, 4384),
    opT=(2048, 1024),
    lnw=(1, 1024), lnb=(1, 1024),
    convw=(2304, 4), convb=(128, 18),
    dtbias=(1, 32), alog=(1, 32), dvec=(1, 32),
    normw=(1, 2048),
    hmask=(128, 8),
    halom=(128, 1),
)
NCH = 32
B_SCRATCH = dict(u0=([33, 128, 1024], F32), xnT=([33, 128, 1024], BF16),
                 raw=([18, 128, 4099], F32), gate=([32, 128, 2048], F32),
                 act=([18, 128, 4096], BF16), y=([32, 128, 2048], F32),
                 ccin=([128, 2048], F32), ccout=([1024, 2048], F32))

def emit_phase_mamba(c, Xin, Xout, W, Ds, name="B", hgather=None, want_u0=True):
    nc = c.nc
    t, v, s, g, sy = c.t, c.v, c.s, c.g, c.sy
    u0d, xntd, rawd, gated, actd, yd = (Ds["u0"], Ds["xnT"], Ds["raw"],
                                        Ds["gate"], Ds["act"], Ds["y"])
    ccin, ccout = Ds["ccin"], Ds["ccout"]

    c.const(EPS_LN); c.const(1.0)
    c.sconst.wait(s)

    es_ip = ExitStack(); _sv = c.es; c.es = es_ip
    ipT = c.sb(name + "_ipT", [128, 8, 4384], BF16, side="right")
    c.es = _sv
    es_b0 = ExitStack(); _sv = c.es; c.es = es_b0
    onesr_dummy = None
    c.es = _sv
    onesr = c.sb(name + "_ones", [1, 128], BF16)
    onesrf = c.sb(name + "_onesf", [1, 128], F32)
    onescf = c.sb(name + "_onesc", [128, 1], F32)
    ident = c.sb(name + "_id", [128, 128], BF16)
    identf = c.sb(name + "_idf", [128, 128], F32)
    lmask = c.sb(name + "_lm", [128, 128], F32)
    _sv = c.es; c.es = es_b0
    lnw_bc = c.sb(name + "_lnw", [128, 1024], F32, side="right")
    lnb_bc = c.sb(name + "_lnb", [128, 1024], F32, side="right")
    rowtmp = c.sb(name + "_rowt", [1, 4224], F32, side="right")
    c.es = _sv
    normw_bc = c.sb(name + "_nw", [128, 2048], F32)
    dtb_bc = c.sb(name + "_dtb", [128, 32], F32)
    negA_bc = c.sb(name + "_negA", [128, 32], F32)
    D_bc = c.sb(name + "_Dbc", [128, 32], F32)
    convw_sb = c.sb(name + "_cw", [128, 18, 4], F32)
    convb_sb = c.sb(name + "_cb", [128, 18], F32)
    hmask_sb = c.sb(name + "_hm", [128, 8], F32)
    halom_sb = c.sb(name + "_ham", [128, 1], F32)
    dt_all = c.sb(name + "_dt", [128, NCH * 32], F32)
    s_all = c.sb(name + "_sa", [128, NCH * 32], F32)
    atot = c.sb(name + "_at", [1, NCH * 32], F32)
    pfx = c.sb(name + "_pfx", [1, NCH * 32], F32)
    H = [c.sb(name + f"_H{i}", [128, 2048], F32) for i in range(2)]
    Hb16 = c.sb(name + "_Hb", [128, 2048], BF16)
    Hinitb = c.sb(name + "_Hib", [128, 2048], BF16)

    ps_bc = c.ps(name + "_psbc", [128, 512], F32)
    ps_r = [c.ps(name + f"_psr{i}", [128, 512], F32) for i in range(2)]
    ps_tr = c.ps(name + "_pstr", [128, 1024], BF16)
    ps_big = c.ps(name + "_psbig", [128, 2048], F32)

    sg = c.sem(); sp = c.sem(); sv = c.sem(); sc = c.sem()
    sw = c.sem(kind="sw")
    sq = [c.sem(kind="hw") for _ in range(2)]
    so = [c.sem(kind="sw") for _ in range(4)]
    scc = c.sem(kind="cc")

    sw.inc(g.dma_start(out=ipT[:], in_=W["ipT"].rearrange("(a p) n -> p a n", p=128)), 16)
    sw.inc(g.dma_start(out=convw_sb[:], in_=W["convw"].rearrange("(m p) j -> p m j", p=128)), 16)
    sw.inc(g.dma_start(out=convb_sb[:], in_=W["convb"]), 16)
    sw.inc(g.dma_start(out=hmask_sb[:], in_=W["hmask"]), 16)
    sw.inc(g.dma_start(out=halom_sb[:], in_=W["halom"]), 16)
    sw.inc(g.dma_start(out=rowtmp[0:1, 0:1024], in_=W["lnw"]), 16)
    sw.inc(g.dma_start(out=rowtmp[0:1, 1024:2048], in_=W["lnb"]), 16)
    sw.inc(g.dma_start(out=rowtmp[0:1, 2048:2080], in_=W["dtbias"]), 16)
    sw.inc(g.dma_start(out=rowtmp[0:1, 2080:2112], in_=W["alog"]), 16)
    sw.inc(g.dma_start(out=rowtmp[0:1, 2112:2144], in_=W["dvec"]), 16)
    sw.inc(g.dma_start(out=rowtmp[0:1, 2144:4192], in_=W["normw"]), 16)
    sg.inc(g.memset(onesr[:], 1.0))
    sg.inc(g.memset(onesrf[:], 1.0))
    sg.inc(g.memset(onescf[:], 1.0))
    emit_identity(c, ident, sg)
    emit_identity(c, identf, sg)
    sg.inc(g.memset(lmask[:], 1.0))
    sg.wait(g)
    sg.inc(g.affine_select(out=lmask[:], in_=lmask[:], compare_op=ALU.is_ge,
                           fill=0.0, base=0, pattern=[[1, 128]], channel_multiplier=-1))
    sg.inc(g.memset(pfx[0:1, 0:32], 0.0))
    if hgather is None:
        sg.inc(g.memset(Hb16[:], 0.0))
        sg.inc(g.memset(H[1][:], 0.0))
    else:
        sw.wait(v)
        es_h = ExitStack(); _svh = c.es; c.es = es_h
        hsl0 = _Ring(c, name + "hg", 2, [128, 2048], F32, kind="hw")
        c.es = _svh
        sv0 = c.sem()
        for j in range(8):
            if j >= 2:
                sy.wait_ge(sv0.h, sv0.n)
            hsl0.dma(j, sy.dma_start(out=hsl0.buf(j)[:], in_=hgather[j]))
            hsl0.wait_loaded(v, j)
            dst, src_ = (H[1], H[0]) if j % 2 == 0 else (H[0], H[1])
            if j == 0:
                sv0.inc(v.tensor_scalar(H[1][:], hsl0.buf(0)[:], hmask_sb[:, 0:1], None, ALU.mult))
            else:
                sv0.wait(v)
                sv0.inc(v.scalar_tensor_tensor(dst[:], hsl0.buf(j)[:], hmask_sb[:, j:j + 1],
                                               src_[:], ALU.mult, ALU.add))
        sv0.wait(v)
        sv0.inc(v.tensor_copy(H[1][:], H[0][:]))
        sv0.wait(v)
        sv0.inc(v.tensor_copy(Hb16[:], H[1][:]))
        sv0.wait(t); sv0.wait(s); sv0.wait(g); sv0.wait(sy)
        es_h.close()

    sw.wait(s)
    sc.inc(s.activation(rowtmp[0:1, 2080:2112], rowtmp[0:1, 2080:2112], AF.Exp))

    def bcast(dst, col0, ncols, scale=None):
        for j0 in range(0, ncols, 512):
            w_ = min(512, ncols - j0)
            sv.wait(t); sc.wait(t)
            sp.inc(t.matmul(ps_bc[:, 0:w_], onesrf[:], rowtmp[0:1, col0 + j0:col0 + j0 + w_],
                            start=True, stop=True))
            sp.wait(v)
            if scale is None:
                sv.inc(v.tensor_copy(dst[:, j0:j0 + w_], ps_bc[:, 0:w_]))
            else:
                sv.inc(v.tensor_scalar(dst[:, j0:j0 + w_], ps_bc[:, 0:w_], scale, None, ALU.mult))
    sw.wait(t); sg.wait(t); sw.wait(v)
    bcast(lnw_bc, 0, 1024)
    bcast(lnb_bc, 1024, 1024)
    bcast(dtb_bc, 2048, 32)
    bcast(negA_bc, 2080, 32, scale=-1.0)
    bcast(D_bc, 2112, 32)
    bcast(normw_bc, 2144, 2048)

    scratch = {k: c.sb(name + "_" + k, [128, 2048] if k == "sq" else [128, 1], F32)
               for k in ("sq", "c0", "c1", "c2", "c3", "c4", "c5")}

    from contextlib import ExitStack as _ES
    es0 = _ES()
    c0ctx = c.es; c.es = es0
    xin = [c.sb(name + f"_b0x{i}", [128, 1024], F32) for i in range(2)]
    u0t = [c.sb(name + f"_b0u{i}", [128, 1024], F32) for i in range(2)]
    xc2 = c.sb(name + "_b0xc", [128, 1024], F32)
    xn1 = c.sb(name + "_b0n1", [128, 1024], F32)
    xnb = c.sb(name + "_b0xn", [128, 1024], BF16)
    xnT_t = [c.sb(name + f"_b0t{i}", [128, 1024], BF16) for i in range(2)]
    c.es = c0ctx

    for ti in range(33):
        bi = ti % 2
        if ti < 2:
            sq[bi].inc(sy.dma_start(out=xin[bi][:], in_=Xin[ti]), 16)
        v.wait_ge(sq[bi].h, sq[bi].base + 16 * (ti // 2 + 1))
        s.wait_ge(sq[bi].h, sq[bi].base + 16 * (ti // 2 + 1))
        emit_ln_f32(c, xin[bi][:], u0t[bi][:], sc, sv, scratch)
        if want_u0:
            sv.wait(g)
            so[bi].inc(g.dma_start(out=u0d[ti], in_=u0t[bi][:]), 16)
        emit_ln_stats(c, u0t[bi][:], sc, sv, scratch)
        sc.wait(v)
        sv.inc(v.tensor_scalar(xc2[:], u0t[bi][:], scratch["c2"][:], scratch["c3"][:],
                               ALU.subtract, ALU.mult))
        sv.wait(v)
        sv.inc(v.tensor_tensor(xn1[:], xc2[:], lnw_bc[:], ALU.mult))
        sv.wait(v)
        sv.inc(v.tensor_tensor(xnb[:], xn1[:], lnb_bc[:], ALU.add))
        sv.wait(t)
        emit_transpose_tiles(c, xnb, xnT_t[bi], ident, ps_tr, sp, sv, 8)
        sv.wait(g)
        so[2 + bi].inc(g.dma_start(out=xntd[ti], in_=xnT_t[bi][:]), 16)
        if ti + 2 < 33:
            sy.wait_ge(sv.h, sv.n)
            sy.wait_ge(so[bi].h, so[bi].n)
            sy.wait_ge(so[2 + bi].h, so[2 + bi].n)
            sq[bi].inc(sy.dma_start(out=xin[bi][:], in_=Xin[ti + 2]), 16)
    es0.close()
    env = dict(ipT=ipT, onesr=onesr, onesrf=onesrf, onescf=onescf,
               ident=ident, identf=identf, lmask=lmask,
               lnw_bc=lnw_bc, lnb_bc=lnb_bc, normw_bc=normw_bc,
               dtb_bc=dtb_bc, negA_bc=negA_bc, D_bc=D_bc,
               convw_sb=convw_sb, convb_sb=convb_sb, hmask_sb=hmask_sb,
               halom_sb=halom_sb, dt_all=dt_all, s_all=s_all, atot=atot, pfx=pfx,
               H=H, Hb16=Hb16, Hinitb=Hinitb, Wop=W["opT"],
               ps_bc=ps_bc, ps_r=ps_r, ps_tr=ps_tr, ps_big=ps_big,
               sw=sw, sg=sg, sp=sp, sv=sv, sc=sc, scc=scc,
               xntd=xntd, rawd=rawd, gated=gated, actd=actd, yd=yd, u0d=u0d,
               ccin=ccin, ccout=ccout, es_ip=es_ip, es_b0=es_b0,
               b0sems=list(sq) + list(so))
    return env

class _Ring:
    def __init__(self, c, name, n, shape, dtype, kind="hw"):
        self.c = c
        self.bufs = [c.sb(f"{name}{i}", shape, dtype) for i in range(n)]
        self.sems = [c.sem(kind=kind) for _ in range(n)]
        self.cnt = [0] * n
        self.n = n
    def buf(self, i):
        return self.bufs[i % self.n]
    def sem(self, i):
        return self.sems[i % self.n]
    def wait_loaded(self, eng, i):
        sm = self.sems[i % self.n]
        eng.wait_ge(sm.h, sm.base + 16 * (i // self.n + 1))
    def dma(self, i, instr):
        self.sems[i % self.n].inc(instr, 16)

def _emit_mamba_b1(c, env, want_z=True):
    t, v, s, g, sy = c.t, c.v, c.s, c.g, c.sy
    ipT, onesr = env["ipT"], env["onesr"]
    ps_big, ps_bc = env["ps_big"], env["ps_bc"]
    sp, sv, sc = env["sp"], env["sv"], env["sc"]
    xntd, rawd, gated = env["xntd"], env["rawd"], env["gated"]
    dt_all = env["dt_all"]
    es = ExitStack(); sv_es = c.es; c.es = es
    xg = _Ring(c, "b1xg", 2, [128, 4, 1024], BF16, kind="hw")
    stg = _Ring(c, "b1st", 4, [128, 512], F32, kind="sw")
    xh = c.sb("b1xh", [128, 1024], BF16)
    c.es = sv_es

    sqh = c.sem(kind="hw")
    sqh.inc(sy.dma_start(out=xh[:], in_=xntd[32]), 16)
    sqh.wait(t)
    hst = stg.buf(0)
    nmm = 0
    for m in range(18):
        pb = ps_big[:, 0:512] if (m % 2 == 0) else ps_big[:, 512:1024]
        if m >= 2:
            t.wait_ge(sv.h, sv.n)
        for k in range(8):
            ins = t.matmul(pb[:, 0:128], ipT[:, k, 2048 + 128 * m:2048 + 128 * (m + 1)],
                           xh[:, k * 128:(k + 1) * 128],
                           start=(k == 0), stop=(k == 7))
        sp.inc(ins)
        sp.wait(v)
        sv.inc(v.tensor_copy(hst[:, m * 3:m * 3 + 3], pb[:, 0:3]))
    sv.wait(g)
    stg.dma(0, g.dma_start(out=rawd[:, :, 0:3].rearrange("m p n -> p m n"),
                           in_=hst[:, 0:54].rearrange("p (m n) -> p m n", n=3)))

    si = 1
    for gi in range(8):
        bi = gi % 2
        if gi < 2:
            xg.dma(gi, sy.dma_start(out=xg.buf(gi)[:],
                                    in_=xntd[4 * gi:4 * gi + 4].rearrange("t p n -> p t n")))
        xg.wait_loaded(t, gi)
        for m in range(18):
            pb = ps_big[:, 0:512] if (m % 2 == 0) else ps_big[:, 512:1024]
            t.wait_ge(sv.h, sv.n)
            for k in range(8):
                ins = t.matmul(pb[:], ipT[:, k, 2048 + 128 * m:2048 + 128 * (m + 1)],
                               xg.buf(gi)[:, :, k * 128:(k + 1) * 128],
                               start=(k == 0), stop=(k == 7))
            sp.inc(ins)
            sp.wait(v)
            sb_ = stg.buf(si); sm = stg.sem(si)
            v.wait_ge(sm.h, sm.base + 16 * (si // 4))
            sv.inc(v.tensor_copy(sb_[:], pb[:]))
            sv.wait(g)
            stg.dma(si, g.dma_start(out=rawd[m, :, 3 + 512 * gi:3 + 512 * (gi + 1)], in_=sb_[:]))
            si += 1
        for tl in range(4 if want_z else 0):
            for nb in range(4):
                pz = ps_big[:, 1024:1536] if ((tl * 4 + nb) % 2 == 0) else ps_big[:, 1536:2048]
                t.wait_ge(sc.h, sc.n)
                for k in range(8):
                    ins = t.matmul(pz[:], xg.buf(gi)[:, tl, k * 128:(k + 1) * 128],
                                   ipT[:, k, 512 * nb:512 * (nb + 1)],
                                   start=(k == 0), stop=(k == 7))
                sp.inc(ins)
                sp.wait(s)
                sb_ = stg.buf(si); sm = stg.sem(si)
                s.wait_ge(sm.h, sm.base + 16 * (si // 4))
                sc.inc(s.activation(sb_[:], pz[:], AF.Silu))
                sc.wait(g)
                stg.dma(si, g.dma_start(out=gated[4 * gi + tl, :, 512 * nb:512 * (nb + 1)], in_=sb_[:]))
                si += 1
        for tl in range(4):
            t.wait_ge(sv.h, sv.n)
            for k in range(8):
                ins = t.matmul(ps_bc[:, 192:224], xg.buf(gi)[:, tl, k * 128:(k + 1) * 128],
                               ipT[:, k, 4352:4384], start=(k == 0), stop=(k == 7))
            sp.inc(ins)
            sp.wait(v)
            ti_ = 4 * gi + tl
            sv.inc(v.tensor_copy(dt_all[:, 32 * ti_:32 * (ti_ + 1)], ps_bc[:, 192:224]))
        if gi + 2 < 8:
            sy.wait_ge(sp.h, sp.n)
            xg.dma(gi + 2, sy.dma_start(out=xg.buf(gi)[:],
                                        in_=xntd[4 * (gi + 2):4 * (gi + 2) + 4].rearrange("t p n -> p t n")))
    for i in range(4):
        stg.sems[i].wait(g)
    es.close()
    return list(xg.sems) + list(stg.sems) + [sqh]

def _emit_mamba_conv(c, env):
    t, v, s, g, sy = c.t, c.v, c.s, c.g, c.sy
    sp, sv, sc = env["sp"], env["sv"], env["sc"]
    rawd, actd = env["rawd"], env["actd"]
    convw_sb, convb_sb, halom_sb = env["convw_sb"], env["convb_sb"], env["halom_sb"]
    es = ExitStack(); sv_es = c.es; c.es = es
    ext = _Ring(c, "cvext", 2, [128, 18, 515], F32, kind="hw")
    ast = _Ring(c, "cvast", 2, [128, 18 * 512], BF16, kind="sw")
    acc = [c.sb(f"cvacc{i}", [128, 512], F32) for i in range(2)]
    c.es = sv_es
    NCG = NCH // 4
    for cg in range(NCG):
        bi = cg % 2
        if cg < 2:
            ext.dma(cg, sy.dma_start(out=ext.buf(cg)[:],
                                     in_=rawd[:, :, 512 * cg:512 * cg + 515].rearrange("m p n -> p m n")))
        ext.wait_loaded(v, cg)
        if cg == 0:
            sv.inc(v.tensor_scalar(ext.buf(0)[:, :, 0:3], ext.buf(0)[:, :, 0:3],
                                   halom_sb[:, 0:1], None, ALU.mult))
            sv.wait(v)
        for m in range(18):
            sv.wait(v); sc.wait(v)
            sv.inc(v.tensor_scalar(acc[0][:], ext.buf(cg)[:, m, 0:512],
                                   convw_sb[:, m, 0:1], None, ALU.mult))
            sv.wait(v)
            sv.inc(v.scalar_tensor_tensor(acc[1][:], ext.buf(cg)[:, m, 1:513],
                                          convw_sb[:, m, 1:2], acc[0][:], ALU.mult, ALU.add))
            sv.wait(v)
            sv.inc(v.scalar_tensor_tensor(acc[0][:], ext.buf(cg)[:, m, 2:514],
                                          convw_sb[:, m, 2:3], acc[1][:], ALU.mult, ALU.add))
            sv.wait(v)
            sv.inc(v.scalar_tensor_tensor(acc[1][:], ext.buf(cg)[:, m, 3:515],
                                          convw_sb[:, m, 3:4], acc[0][:], ALU.mult, ALU.add))
            sv.wait(s)
            ab = ast.buf(cg)
            if m == 0:
                sm_ = ast.sem(cg)
                s.wait_ge(sm_.h, sm_.base + 16 * (cg // 2))
            sc.inc(s.activation(ab[:, m * 512:(m + 1) * 512], acc[1][:], AF.Silu,
                                bias=convb_sb[:, m:m + 1]))
        sc.wait(g)
        ast.dma(cg, g.dma_start(out=actd[:, :, 512 * cg:512 * (cg + 1)].rearrange("m p n -> p m n"),
                                in_=ast.buf(cg)[:].rearrange("p (m n) -> p m n", n=512)))
        if cg + 2 < NCG:
            sy.wait_ge(sc.h, sc.n)
            sy.wait_ge(sv.h, sv.n)
            ext.dma(cg + 2, sy.dma_start(out=ext.buf(cg)[:],
                                         in_=rawd[:, :, 512 * (cg + 2):512 * (cg + 2) + 515].rearrange("m p n -> p m n")))
    for i in range(2):
        ast.sems[i].wait(g)
    es.close()
    return list(ext.sems) + list(ast.sems)

def _emit_mamba_dt(c, env):
    t, v, s, g = c.t, c.v, c.s, c.g
    sp, sv, sc = env["sp"], env["sv"], env["sc"]
    dt_all, s_all, atot, pfx = env["dt_all"], env["s_all"], env["atot"], env["pfx"]
    dtb_bc, negA_bc, lmask = env["dtb_bc"], env["negA_bc"], env["lmask"]
    onescf, ps_bc = env["onescf"], env["ps_bc"]
    es = ExitStack(); sv_es = c.es; c.es = es
    ta = c.sb("dtta", [128, 32], F32)
    tb = c.sb("dttb", [128, 32], F32)
    tc_ = c.sb("dttc", [128, 32], F32)
    td = c.sb("dttd", [128, 32], F32)
    c.es = sv_es
    for cc in range(NCH):
        sl = slice(32 * cc, 32 * (cc + 1))
        sv.wait(v); sc.wait(v)
        sv.inc(v.tensor_tensor(ta[:], dt_all[:, sl], dtb_bc[:], ALU.add))
        sv.wait(s)
        sc.inc(s.activation(tb[:], ta[:], AF.Abs))
        sc.wait(s)
        sc.inc(s.activation(tc_[:], tb[:], AF.Exp, scale=-1.0))
        sc.wait(s)
        sc.inc(s.activation(tb[:], tc_[:], AF.Ln, bias=1.0))
        sc.wait(s)
        sc.inc(s.activation(tc_[:], ta[:], AF.Relu))
        sc.wait(v)
        sv.inc(v.tensor_tensor(dt_all[:, sl], tb[:], tc_[:], ALU.add))
        sv.wait(v)
        sv.inc(v.tensor_tensor(td[:], dt_all[:, sl], negA_bc[:], ALU.mult))
        sv.wait(t)
        sp.inc(t.matmul(ps_bc[:, 224:256], lmask[:], td[:], start=True, stop=True))
        sp.inc(t.matmul(ps_bc[0:1, 256:288], onescf[:], td[:], start=True, stop=True))
        sp.wait(v)
        sv.inc(v.tensor_copy(s_all[:, sl], ps_bc[:, 224:256]))
        sv.inc(v.tensor_copy(atot[0:1, sl], ps_bc[0:1, 256:288]))
        sv.wait(v)
        if cc + 1 < NCH:
            sv.inc(v.tensor_tensor(pfx[0:1, 32 * (cc + 1):32 * (cc + 2)],
                                   pfx[0:1, sl], atot[0:1, sl], ALU.add))
    es.close()
    return []


def _scan_state_update(c, env, cc, sl, L):
    t, v, s = c.t, c.v, c.s
    sp, sv, sc = env["sp"], env["sv"], env["sc"]
    ps_bc, ps_big = env["ps_bc"], env["ps_big"]
    onesrf = env["onesrf"]
    drow, dfac, b_tm, xtil, Htmp = L["drow"], L["dfac"], L["b_tm"], L["xtil"], L["Htmp"]
    hprev, hcur = L["hprev"], L["hcur"]
    Hb16 = env["Hb16"]
    sc.wait(t)
    sp.inc(t.matmul(ps_bc[:, 128:160], onesrf[:], drow[:], start=True, stop=True))
    sp.wait(v)
    sv.inc(v.tensor_copy(dfac[:], ps_bc[:, 128:160]))
    sv.wait(t)
    for j in range(4):
        sp.inc(t.matmul(ps_big[:, 512 * j:512 * (j + 1)], b_tm[:],
                        xtil[:, 512 * j:512 * (j + 1)], start=True, stop=True))
    sp.wait(v)
    sv.inc(v.tensor_tensor(Htmp[:].rearrange("p (h d) -> p h d", d=64),
                           hprev[:].rearrange("p (h d) -> p h d", d=64),
                           dfac[:].unsqueeze(2).broadcast_to([128, 32, 64]), ALU.mult))
    sv.wait(v)
    sv.inc(v.tensor_add(hcur[:], Htmp[:], ps_big[:]))
    sv.wait(v)
    sv.inc(v.tensor_copy(Hb16[:], hcur[:]))

def _emit_mamba_scan(c, env, want_y=True):
    t, v, s, g, sy = c.t, c.v, c.s, c.g, c.sy
    sp, sv, sc = env["sp"], env["sv"], env["sc"]
    actd, yd = env["actd"], env["yd"]
    s_all, dt_all, atot = env["s_all"], env["dt_all"], env["atot"]
    lmask, ident, identf = env["lmask"], env["ident"], env["identf"]
    onesrf, D_bc = env["onesrf"], env["D_bc"]
    H, Hb16 = env["H"], env["Hb16"]
    ps_bc, ps_r, ps_tr, ps_big = env["ps_bc"], env["ps_r"], env["ps_tr"], env["ps_big"]

    es = ExitStack(); sv_es = c.es; c.es = es
    actl = _Ring(c, "b3act", 2, [128, 18, 128], BF16, kind="hw")
    yst = _Ring(c, "b3yst", 2, [128, 2048], F32, kind="sw")
    x_tm = c.sb("b3xtm", [128, 2048], BF16)
    b_tm = c.sb("b3btm", [128, 128], BF16)
    xtil = c.sb("b3xtl", [128, 2048], BF16)
    gg = c.sb("b3g", [128, 128], F32)
    drow = c.sb("b3dr", [1, 32], F32)
    colb4 = [c.sb(f"b3cb{i}", [128, 512], F32) for i in range(2)]
    onesbig = c.sb("b3ob", [128, 512], F32)
    Es = c.sb("b3es", [128, 32], F32)
    e2 = c.sb("b3e2", [128, 32], F32)
    dfac = c.sb("b3df", [128, 32], F32)
    ehead = [c.sb(f"b3eh{i}", [128, 512], F32) for i in range(2)]
    ehead2 = [c.sb(f"b3eh2{i}", [128, 512], F32) for i in range(2)]
    mhead = [c.sb(f"b3mh{i}", [128, 512], BF16) for i in range(2)]
    Htmp = c.sb("b3htmp", [128, 2048], F32)
    ya = c.sb("b3ya", [128, 2048], F32)
    tmp32 = c.sb("b3t32", [128, 32], F32)
    c.es = sv_es
    sgo = c.sem()
    sgo.inc(c.g.memset(onesbig[:], 1.0))
    sgo.wait(c.v)

    for cc in range(NCH):
        sl = slice(32 * cc, 32 * (cc + 1))
        hprev = H[(cc + 1) % 2]
        hcur = H[cc % 2]
        if cc < 2:
            actl.dma(cc, sy.dma_start(out=actl.buf(cc)[:],
                                      in_=actd[:, :, 128 * cc:128 * (cc + 1)].rearrange("m p n -> p m n")))
        A = actl.buf(cc)
        sv.wait(s); sp.wait(s)
        if want_y:
            sc.inc(s.activation(Es[:], s_all[:, sl], AF.Exp))
        sv.wait(t); sc.wait(t)
        sp.inc(t.matmul(ps_bc[:, 128:160], onesrf[:], atot[0:1, sl], start=True, stop=True))
        sp.wait(v)
        sv.inc(v.tensor_tensor(tmp32[:], ps_bc[:, 128:160], s_all[:, sl], ALU.subtract))
        sv.wait(s)
        sc.inc(s.activation(tmp32[:], tmp32[:], AF.Exp))
        sc.wait(v)
        sv.inc(v.tensor_tensor(e2[:], tmp32[:], dt_all[:, sl], ALU.mult))
        sc.inc(s.activation(drow[:], atot[0:1, sl], AF.Exp))
        actl.wait_loaded(t, cc)
        for r in range(2):
            sv.wait(t)
            for j in range(8):
                sp.inc(t.transpose(ps_tr[:, j * 128:(j + 1) * 128],
                                   A[:, 8 * r + j, :], ident[:]))
            sp.wait(v)
            sv.inc(v.tensor_copy(x_tm[:, 1024 * r:1024 * (r + 1)], ps_tr[:]))
        sv.wait(t)
        sp.inc(t.transpose(ps_tr[:, 0:128], A[:, 16, :], ident[:]))
        sp.wait(v)
        sv.inc(v.tensor_copy(b_tm[:], ps_tr[:, 0:128]))
        sv.wait(v)
        sv.inc(v.tensor_tensor(xtil[:].rearrange("p (h d) -> p h d", d=64),
                               x_tm[:].rearrange("p (h d) -> p h d", d=64),
                               e2[:].unsqueeze(2).broadcast_to([128, 32, 64]), ALU.mult))
        if not want_y:
            _scan_state_update(c, env, cc, sl, locals())
            if cc + 2 < NCH:
                sy.wait_ge(sp.h, sp.n)
                actl.dma(cc + 2, sy.dma_start(out=actl.buf(cc)[:],
                                              in_=actd[:, :, 128 * (cc + 2):128 * (cc + 3)].rearrange("m p n -> p m n")))
            continue
        sv.wait(t)
        sp.inc(t.matmul(ps_bc[:, 0:128], A[:, 16, :], A[:, 17, :], start=True, stop=True))
        sp.wait(v)
        sv.inc(v.tensor_tensor(gg[:], ps_bc[:, 0:128], lmask[:], ALU.mult))
        sv.wait(t)
        for j in range(4):
            sp.inc(t.matmul(ps_big[:, 512 * j:512 * (j + 1)], A[:, 17, :],
                            Hb16[:, 512 * j:512 * (j + 1)], start=True, stop=True))
        sp.wait(v)
        ybuf = yst.buf(cc)
        v.wait_ge(yst.sem(cc).h, yst.sem(cc).base + 16 * (cc // 2))
        sv.inc(v.tensor_tensor(ybuf[:].rearrange("p (h d) -> p h d", d=64),
                               ps_big[:].rearrange("p (h d) -> p h d", d=64),
                               Es[:].unsqueeze(2).broadcast_to([128, 32, 64]), ALU.mult))
        sv.wait(t)
        for grp in range(8):
            par = grp % 2
            pr = ps_r[par]
            hsl_ = slice(32 * cc + 4 * grp, 32 * cc + 4 * grp + 4)
            s_v = s_all[:, hsl_].unsqueeze(2).broadcast_to([128, 4, 128])
            dt_v = dt_all[:, hsl_].unsqueeze(2).broadcast_to([128, 4, 128])
            gg_v = gg[:].unsqueeze(1).broadcast_to([128, 4, 128])
            cb4 = colb4[par]
            eb, eb2, mb = ehead[par], ehead2[par], mhead[par]
            if grp >= 2:
                v.wait_ge(sp.h, sp.n)
            sv.inc(v.tensor_tensor(cb4[:].rearrange("p (l n) -> p l n", n=128),
                                   s_v, onesbig[:].rearrange("p (l n) -> p l n", n=128),
                                   ALU.mult))
            if grp >= 2:
                t.wait_ge(sv.h, sv.n)
            sv.wait(t)
            for l4 in range(4):
                sp.inc(t.transpose(pr[:, 128 * l4:128 * (l4 + 1)],
                                   cb4[:, 128 * l4:128 * (l4 + 1)], identf[:]))
            sp.wait(v)
            ebv = eb[:].rearrange("p (l n) -> p l n", n=128)
            eb2v = eb2[:].rearrange("p (l n) -> p l n", n=128)
            mbv = mb[:].rearrange("p (l n) -> p l n", n=128)
            sv.inc(v.tensor_tensor(ebv, pr[:].rearrange("p (l n) -> p l n", n=128),
                                   s_v, ALU.min))
            sv.wait(v)
            sv.inc(v.tensor_tensor(ebv, ebv, s_v, ALU.subtract))
            sv.wait(s)
            sc.inc(s.activation(eb2[:], eb[:], AF.Exp))
            sc.wait(v)
            sv.inc(v.tensor_tensor(ebv, eb2v, dt_v, ALU.mult))
            sv.wait(v)
            sv.inc(v.tensor_tensor(mbv, ebv, gg_v, ALU.mult))
            sv.wait(t)
            for l4 in range(4):
                h = 4 * grp + l4
                sp.inc(t.matmul(ps_big[:, 64 * h:64 * (h + 1)], mb[:, 128 * l4:128 * (l4 + 1)],
                                x_tm[:, 64 * h:64 * (h + 1)],
                                start=(h % 8 == 0), stop=(h % 8 == 7)))
        sp.wait(v)
        sv.inc(v.tensor_add(ya[:], ybuf[:], ps_big[:]))
        sv.wait(v)
        sv.inc(v.tensor_tensor(ybuf[:].rearrange("p (h d) -> p h d", d=64),
                               x_tm[:].rearrange("p (h d) -> p h d", d=64),
                               D_bc[:].unsqueeze(2).broadcast_to([128, 32, 64]), ALU.mult))
        sv.wait(v)
        sv.inc(v.tensor_add(ya[:], ya[:], ybuf[:]))
        sv.wait(v)
        sv.inc(v.tensor_copy(ybuf[:], ya[:]))
        sv.wait(g)
        yst.dma(cc, g.dma_start(out=yd[cc], in_=ybuf[:]))
        _scan_state_update(c, env, cc, sl, locals())
        if cc + 2 < NCH:
            sy.wait_ge(sp.h, sp.n)
            actl.dma(cc + 2, sy.dma_start(out=actl.buf(cc)[:],
                                          in_=actd[:, :, 128 * (cc + 2):128 * (cc + 3)].rearrange("m p n -> p m n")))
    for i in range(2):
        yst.sems[i].wait(g)
    es.close()
    return list(actl.sems) + list(yst.sems)

def _emit_mamba_hexchange(c, env):
    t, v, s, g, sy = c.t, c.v, c.s, c.g, c.sy
    sp, sv, sc = env["sp"], env["sv"], env["sc"]
    H, Hinitb, hmask_sb = env["H"], env["Hinitb"], env["hmask_sb"]
    ccin, ccout = env["ccin"], env["ccout"]
    hfinal = H[(NCH - 1) % 2]
    scc = env["scc"]
    sst_cc = c.sem(kind="sw")
    sv.wait(g)
    sst_cc.inc(g.dma_start(out=ccin[:], in_=hfinal[:]), 16)
    sst_cc.wait(g)
    scc.inc(g.collective_compute(
        "AllGather", ALU.bypass, replica_groups=[list(range(NCORES))],
        ins=[ccin.opt()], outs=[ccout.opt()]), 1)
    scc.wait(sy)
    es = ExitStack(); sv_es = c.es; c.es = es
    hsl = _Ring(c, "hx", 2, [128, 2048], F32, kind="hw")
    Hinit = [c.sb(f"hxhi{i}", [128, 2048], F32) for i in range(2)]
    c.es = sv_es
    for j in range(8):
        if j >= 2:
            sy.wait_ge(sv.h, sv.n)
        hsl.dma(j, sy.dma_start(out=hsl.buf(j)[:], in_=ccout[128 * j:128 * (j + 1), :]))
        hsl.wait_loaded(v, j)
        if j == 0:
            sv.inc(v.tensor_scalar(Hinit[0][:], hsl.buf(0)[:], hmask_sb[:, 0:1], None, ALU.mult))
        else:
            sv.wait(v)
            sv.inc(v.scalar_tensor_tensor(Hinit[j % 2][:], hsl.buf(j)[:], hmask_sb[:, j:j + 1],
                                          Hinit[(j + 1) % 2][:], ALU.mult, ALU.add))
    sv.wait(v)
    sv.inc(v.tensor_copy(Hinitb[:], Hinit[7 % 2][:]))
    es.close()
    return list(hsl.sems)

def _emit_mamba_corr(c, env):
    t, v, s, g, sy = c.t, c.v, c.s, c.g, c.sy
    sp, sv, sc = env["sp"], env["sv"], env["sc"]
    actd, yd = env["actd"], env["yd"]
    s_all, pfx, onesrf = env["s_all"], env["pfx"], env["onesrf"]
    Hinitb = env["Hinitb"]
    ps_bc, ps_big = env["ps_bc"], env["ps_big"]
    es = ExitStack(); sv_es = c.es; c.es = es
    cl = _Ring(c, "b3cc", 2, [128, 128], BF16, kind="hw")
    dst = _Ring(c, "b3cd", 2, [128, 2048], F32, kind="sw")
    es2 = c.sb("b3ce", [128, 32], F32)
    c.es = sv_es
    for cc in range(NCH):
        if cc < 2:
            cl.dma(cc, sy.dma_start(out=cl.buf(cc)[:],
                                    in_=actd[17, :, 128 * cc:128 * (cc + 1)]))
        sv.wait(t); sc.wait(t)
        sp.inc(t.matmul(ps_bc[:, 128:160], onesrf[:], pfx[0:1, 32 * cc:32 * (cc + 1)],
                        start=True, stop=True))
        sp.wait(v)
        sv.inc(v.tensor_tensor(es2[:], ps_bc[:, 128:160], s_all[:, 32 * cc:32 * (cc + 1)], ALU.add))
        sv.wait(s)
        sc.inc(s.activation(es2[:], es2[:], AF.Exp))
        cl.wait_loaded(t, cc)
        sc.wait(t)
        for j in range(4):
            sp.inc(t.matmul(ps_big[:, 512 * j:512 * (j + 1)], cl.buf(cc)[:],
                            Hinitb[:, 512 * j:512 * (j + 1)], start=True, stop=True))
        sp.wait(v); sc.wait(v)
        db = dst.buf(cc)
        v.wait_ge(dst.sem(cc).h, dst.sem(cc).base + 16 * (cc // 2))
        sv.inc(v.tensor_tensor(db[:].rearrange("p (h d) -> p h d", d=64),
                               ps_big[:].rearrange("p (h d) -> p h d", d=64),
                               es2[:].unsqueeze(2).broadcast_to([128, 32, 64]), ALU.mult))
        sv.wait(g)
        dst.dma(cc, g.dma_start(out=yd[cc], in_=db[:], accum_op=ALU.add))
        if cc + 2 < NCH:
            sy.wait_ge(sp.h, sp.n)
            cl.dma(cc + 2, sy.dma_start(out=cl.buf(cc)[:],
                                        in_=actd[17, :, 128 * (cc + 2):128 * (cc + 3)]))
    for i in range(2):
        dst.sems[i].wait(g)
    es.close()
    return list(cl.sems) + list(dst.sems)

def _emit_mamba_out(c, env, Xin, Xout):
    t, v, s, g, sy = c.t, c.v, c.s, c.g, c.sy
    sp, sv, sc = env["sp"], env["sv"], env["sc"]
    yd, gated, u0d = env["yd"], env["gated"], env["u0d"]
    normw_bc, ident = env["normw_bc"], env["ident"]
    ps_tr, ps_big = env["ps_tr"], env["ps_big"]
    es = ExitStack(); sv_es = c.es; c.es = es
    opT = c.sb("b4opT", [128, 16, 1024], BF16)
    swo = c.sem(kind="sw")
    swo.inc(g.dma_start(out=opT[:], in_=env["Wop"].rearrange("(a p) n -> p a n", p=128)), 16)
    swo.wait(t)
    yl = _Ring(c, "b4y", 2, [128, 2048], F32, kind="hw")
    gl = _Ring(c, "b4g", 2, [128, 2048], F32, kind="hw")
    ul = _Ring(c, "b4u", 2, [128, 1024], F32, kind="hw")
    xl = _Ring(c, "b4x", 2, [128, 1024], F32, kind="hw")
    y2 = c.sb("b4y2", [128, 2048], F32)
    y4 = c.sb("b4y4", [128, 2048], BF16)
    y_fm = c.sb("b4yfm", [128, 2048], BF16)
    xo = _Ring(c, "b4o", 2, [128, 1024], F32, kind="sw")
    scr = {k: c.sb("b4" + k, [128, 2048] if k == "sq" else [128, 1], F32)
           for k in ("sq", "c0", "c1", "c2", "c3")}
    c.es = sv_es
    for ti in range(NT):
        bi = ti % 2
        if ti < 2:
            yl.dma(ti, sy.dma_start(out=yl.buf(ti)[:], in_=yd[ti]))
            gl.dma(ti, sy.dma_start(out=gl.buf(ti)[:], in_=gated[ti]))
            ul.dma(ti, sy.dma_start(out=ul.buf(ti)[:], in_=u0d[ti]))
            xl.dma(ti, sy.dma_start(out=xl.buf(ti)[:], in_=Xin[ti]))
        for r_ in (yl, gl, ul, xl):
            r_.wait_loaded(v, ti)
        sv.wait(v)
        sv.inc(v.tensor_tensor(y2[:], yl.buf(ti)[:], gl.buf(ti)[:], ALU.mult))
        sv.wait(s)
        sc.inc(s.activation(scr["sq"][:], y2[:], AF.Square))
        sc.wait(v)
        sv.inc(v.tensor_reduce(scr["c0"][:], scr["sq"][:], AX.X, ALU.add))
        sv.wait(v)
        sv.inc(v.tensor_scalar(scr["c1"][:], scr["c0"][:], 1.0 / 2048, None, ALU.mult))
        sv.wait(s)
        sc.inc(s.activation(scr["c2"][:], scr["c1"][:], AF.Ln, bias=EPS_LN))
        sc.wait(s)
        sc.inc(s.activation(scr["c2"][:], scr["c2"][:], AF.Exp, scale=-0.5))
        sc.wait(v)
        sv.inc(v.tensor_scalar(y2[:], y2[:], scr["c2"][:], None, ALU.mult))
        sv.wait(v)
        sv.inc(v.tensor_tensor(y4[:], y2[:], normw_bc[:], ALU.mult))
        for r in range(2):
            sv.wait(t)
            for j in range(8):
                sp.inc(t.transpose(ps_tr[:, j * 128:(j + 1) * 128],
                                   y4[:, 1024 * r + 128 * j:1024 * r + 128 * (j + 1)], ident[:]))
            sp.wait(v)
            sv.inc(v.tensor_copy(y_fm[:, 1024 * r:1024 * (r + 1)], ps_tr[:]))
        sv.wait(t)
        if ti >= 1:
            t.wait_ge(sv.h, sv.n)
        for k in range(16):
            i1 = t.matmul(ps_big[:, 0:512], y_fm[:, 128 * k:128 * (k + 1)],
                          opT[:, k, 0:512], start=(k == 0), stop=(k == 15))
            i2 = t.matmul(ps_big[:, 512:1024], y_fm[:, 128 * k:128 * (k + 1)],
                          opT[:, k, 512:1024], start=(k == 0), stop=(k == 15))
        sp.inc(i1); sp.inc(i2)
        sp.wait(v)
        ob = xo.buf(ti)
        v.wait_ge(xo.sem(ti).h, xo.sem(ti).base + 16 * (ti // 2))
        sv.inc(v.tensor_add(y2[:, 0:1024], ps_big[:, 0:1024], ul.buf(ti)[:]))
        sv.wait(v)
        sv.inc(v.tensor_add(ob[:], y2[:, 0:1024], xl.buf(ti)[:]))
        sv.wait(g)
        xo.dma(ti, g.dma_start(out=Xout[ti], in_=ob[:]))
        if ti + 2 < NT:
            sy.wait_ge(sv.h, sv.n)
            yl.dma(ti + 2, sy.dma_start(out=yl.buf(ti)[:], in_=yd[ti + 2]))
            gl.dma(ti + 2, sy.dma_start(out=gl.buf(ti)[:], in_=gated[ti + 2]))
            ul.dma(ti + 2, sy.dma_start(out=ul.buf(ti)[:], in_=u0d[ti + 2]))
            xl.dma(ti + 2, sy.dma_start(out=xl.buf(ti)[:], in_=Xin[ti + 2]))
    for i in range(2):
        xo.sems[i].wait(g)
    es.close()
    return list(xo.sems) + list(yl.sems) + list(gl.sems) + list(ul.sems) + list(xl.sems)

def _barrier_all(c, sems):
    for eng in (c.t, c.v, c.s, c.g, c.sy):
        for sm in sems:
            sm.wait(eng)

def emit_phase_mamba_main(c, Xin, Xout, W, Ds, name="B", mode="cc",
                          hgather=None, hout=None):
    env = emit_phase_mamba(c, Xin, Xout, W, Ds, name, hgather=hgather,
                           want_u0=(mode != "pre"))
    base = [env["sp"], env["sv"], env["sc"], env["sw"], env["sg"], env["scc"]]
    _barrier_all(c, base + env["b0sems"])
    env["es_b0"].close()
    r = _emit_mamba_b1(c, env, want_z=(mode != "pre"))
    _barrier_all(c, base + r)
    env["es_ip"].close()
    r = _emit_mamba_conv(c, env)
    _barrier_all(c, base + r)
    r = _emit_mamba_dt(c, env)
    _barrier_all(c, base + r)
    r = _emit_mamba_scan(c, env, want_y=(mode != "pre"))
    _barrier_all(c, base + r)
    if mode == "pre":
        sfin = c.sem(kind="sw")
        sfin.inc(c.g.dma_start(out=hout, in_=env["H"][1][:]), 16)
        sfin.wait(c.g)
        return env
    if mode == "cc":
        r = _emit_mamba_hexchange(c, env)
        _barrier_all(c, base + r)
        r = _emit_mamba_corr(c, env)
        _barrier_all(c, base + r)
    r = _emit_mamba_out(c, env, Xin, Xout)
    _barrier_all(c, base + r)
    return env

def _prep_weights_B(inputs):
    return dict(
        ipT=np.ascontiguousarray(inputs["L0_attn_in_proj_w"].T),
        opT=np.ascontiguousarray(inputs["L0_attn_out_proj_w"].T),
        lnw=np.ascontiguousarray(inputs["L0_attn_ln_w"][None, :]),
        lnb=np.ascontiguousarray(inputs["L0_attn_ln_b"][None, :]),
        convw=np.ascontiguousarray(inputs["L0_attn_conv_w"]),
        convb=np.ascontiguousarray(inputs["L0_attn_conv_b"].reshape(18, 128).T),
        dtbias=np.ascontiguousarray(inputs["L0_attn_dt_bias"][None, :]),
        alog=np.ascontiguousarray(inputs["L0_attn_A_log"][None, :]),
        dvec=np.ascontiguousarray(inputs["L0_attn_D"][None, :]),
        normw=np.ascontiguousarray(inputs["L0_attn_norm_w"][None, :]),
    )

def _core_masks_B(cid):
    hmask = np.zeros((128, 8), np.float32)
    halom = np.zeros((128, 1), np.float32)
    if cid % 2 == 1:
        hmask[:, cid - 1] = 1.0
        halom[:] = 1.0
    return dict(hmask=hmask, halom=halom)

def build_B(mode="cc"):
    nc = bass.Bass("TRN2", target_bir_lowering=False)
    X = nc.dram_tensor("X", [33, 128, D], F32, kind="ExternalInput").ap()
    XO = nc.dram_tensor("XO", [33, 128, D], F32, kind="ExternalOutput").ap()
    Wap = {k: nc.dram_tensor("W_" + k, list(s), F32, kind="ExternalInput").ap()
           for k, s in B_W_SHAPES.items()}
    Ds = {k: nc.dram_tensor("S_" + k, list(sh), dt_, kind="Internal",
                            addr_space=("Shared" if k == "ccout" else "Local")).ap()
          for k, (sh, dt_) in B_SCRATCH.items()}
    hg = hout = None
    if mode == "pre":
        hout = nc.dram_tensor("HF", [128, 2048], F32, kind="ExternalOutput").ap()
    if mode == "full":
        hg = nc.dram_tensor("HG", [8, 128, 2048], F32, kind="ExternalInput").ap()
    es = ExitStack()
    c = Ctx(nc, es)
    emit_phase_mamba_main(c, X, XO, Wap, Ds, "B", mode=mode, hgather=hg, hout=hout)
    es.close()
    return nc

D_W_SHAPES = dict(w1T=(9, 1024, 2816), w3T=(9, 1024, 2816), w2T=(9, 2816, 1024),
                  gateT=(1024, 16), gbias=(1, 16))

def emit_phase_L1FFN(c, Xin, Xout, W, xntd, name, inplace=False):
    nc = c.nc
    t, v, s, g, sy = c.t, c.v, c.s, c.g, c.sy
    NE = 9

    gateT = c.sb(name + "_gT", [128, 8, 16], BF16)
    gbias = c.sb(name + "_gb", [1, 16], BF16)
    onesr = c.sb(name + "_ones", [1, 128], BF16)
    ident = c.sb(name + "_id", [128, 128], BF16)
    cw = c.sb(name + "_cw", [128, NT, 16], F32)
    sg = c.sem(); sp = c.sem(); sv = c.sem(); sc = c.sem()
    sw = c.sem(kind="sw")
    sacc = [c.sem(kind="sw"), c.sem(kind="sw")]

    c.const(EPS_LN)
    c.sconst.wait(s)
    sw.inc(g.dma_start(out=gateT[:], in_=W["gateT"].rearrange("(a p) n -> p a n", p=128)), 16)
    sw.inc(g.dma_start(out=gbias[:], in_=W["gbias"]), 16)
    sg.inc(g.memset(onesr[:], 1.0))
    emit_identity(c, ident, sg)

    es0 = ExitStack(); _sv = c.es; c.es = es0
    xin = [c.sb(name + f"_p0x{i}", [128, 1024], F32) for i in range(2)]
    xnb = c.sb(name + "_p0n", [128, 1024], BF16)
    xnT = [c.sb(name + f"_p0t{i}", [128, 1024], BF16) for i in range(2)]
    gsc = c.sb(name + "_p0g", [128, 16], F32)
    gw2 = c.sb(name + "_p0w", [128, 16], F32)
    scratch = {k: c.sb(name + "_p0" + k, [128, 1024] if k == "sq" else [128, 1], F32)
               for k in ("sq", "c0", "c1", "c2", "c3")}
    ps_tr = c.ps(name + "_p0ptr", [128, 1024], BF16)
    ps_sc = c.ps(name + "_p0psc", [128, 16], F32)
    c.es = _sv
    sq_ = [c.sem(kind="hw"), c.sem(kind="hw")]
    so_ = [c.sem(kind="sw"), c.sem(kind="sw")]

    for ti in range(NT):
        bi = ti % 2
        if ti < 2:
            sq_[bi].inc(sy.dma_start(out=xin[bi][:], in_=Xin[ti]), 16)
        v.wait_ge(sq_[bi].h, sq_[bi].base + 16 * (ti // 2 + 1))
        s.wait_ge(sq_[bi].h, sq_[bi].base + 16 * (ti // 2 + 1))
        emit_ln_stats(c, xin[bi][:], sc, sv, scratch)
        sc.wait(v)
        sv.inc(v.tensor_scalar(xnb[:], xin[bi][:], scratch["c2"][:], scratch["c3"][:],
                               ALU.subtract, ALU.mult))
        if not inplace:
            sv.wait(g)
            so_[bi].inc(g.dma_start(out=Xout[ti], in_=xin[bi][:]), 16)
        sv.wait(t)
        if ti == 0:
            sg.wait(t); sw.wait(t)
        emit_transpose_tiles(c, xnb, xnT[bi], ident, ps_tr, sp, sv, 8)
        sv.wait(g)
        so_[bi].inc(g.dma_start(
            out=xntd[:, :, 128 * ti:128 * (ti + 1)].rearrange("k p j -> p k j"),
            in_=xnT[bi][:].rearrange("p (k j) -> p k j", j=128)), 16)
        if ti + 2 < NT:
            sy.wait_ge(sv.h, sv.n)
            sy.wait_ge(so_[bi].h, so_[bi].n)
            sq_[bi].inc(sy.dma_start(out=xin[bi][:], in_=Xin[ti + 2]), 16)
    _sv = c.es; c.es = es0
    xnt_r = _Ring(c, name + "_xr", 2, [128, 8, 128], BF16, kind="hw")
    c.es = _sv
    for ti in range(NT):
        bi = ti % 2
        if ti < 2:
            xnt_r.dma(ti, sy.dma_start(
                out=xnt_r.buf(ti)[:],
                in_=xntd[:, :, 128 * ti:128 * (ti + 1)].rearrange("k p j -> p k j")))
        xnt_r.wait_loaded(t, ti)
        sv.wait(t); sc.wait(t)
        for k in range(8):
            t.matmul(ps_sc[:], xnt_r.buf(ti)[:, k, :], gateT[:, k, :],
                     start=(k == 0), stop=False)
        sp.inc(t.matmul(ps_sc[:], onesr[:], gbias[:].broadcast_to([1, 16]), start=False, stop=True))
        sp.wait(s)
        sc.inc(s.activation(gsc[:], ps_sc[:], AF.Sigmoid))
        sc.wait(v)
        sv.inc(v.tensor_reduce(scratch["c0"][:], gsc[:, 0:9], AX.X, ALU.max))
        sv.wait(v)
        sv.inc(v.tensor_scalar(gw2[:, 0:9], gsc[:, 0:9], scratch["c0"][:], None, ALU.is_ge))
        sv.wait(v)
        sv.inc(v.scalar_tensor_tensor(gw2[:, 0:9], gw2[:, 0:9], -1e30, gsc[:, 0:9],
                                      ALU.mult, ALU.add))
        sv.wait(v)
        sv.inc(v.tensor_reduce(scratch["c1"][:], gw2[:, 0:9], AX.X, ALU.max))
        sv.wait(v)
        sv.inc(v.tensor_scalar(gw2[:, 0:9], gsc[:, 0:9], scratch["c1"][:], None, ALU.is_ge))
        sv.wait(v)
        sv.inc(v.tensor_tensor(gw2[:, 0:9], gw2[:, 0:9], gsc[:, 0:9], ALU.mult))
        sv.wait(v)
        sv.inc(v.tensor_reduce(scratch["c2"][:], gw2[:, 0:9], AX.X, ALU.add))
        sv.wait(v)
        sv.inc(v.tensor_scalar(scratch["c2"][:], scratch["c2"][:], 1e-20, None, ALU.add))
        sv.wait(v)
        sv.inc(v.reciprocal(scratch["c2"][:], scratch["c2"][:]))
        sv.wait(v)
        sv.inc(v.tensor_scalar(scratch["c2"][:], scratch["c2"][:], 0.5, None, ALU.mult))
        sv.wait(v)
        sv.inc(v.tensor_scalar(cw[:, ti, 0:9], gw2[:, 0:9], scratch["c2"][:], None, ALU.mult))
        if ti + 2 < NT:
            sy.wait_ge(sv.h, sv.n)
            xnt_r.dma(ti + 2, sy.dma_start(
                out=xnt_r.buf(ti)[:],
                in_=xntd[:, :, 128 * (ti + 2):128 * (ti + 3)].rearrange("k p j -> p k j")))
    for smx in sq_ + so_:
        smx.wait(g); smx.wait(sy); smx.wait(t); smx.wait(v); smx.wait(s)
    for smx in (sp, sv, sc, sw, sg):
        for eng in (t, v, s, g, sy):
            smx.wait(eng)
    es0.close()

    esE = ExitStack(); _sv = c.es; c.es = esE
    w1T = c.sb(name + "_w1", [128, 8, 2816], BF16)
    w3T = c.sb(name + "_w3", [128, 8, 2816], BF16)
    w2T = c.sb(name + "_w2", [128, 22, 1024], BF16)
    h_all = c.sb(name + "_h", [128, 22, 512], BF16)
    tsil = [c.sb(name + f"_ts{i}", [128, 512], F32) for i in range(2)]
    delta = [c.sb(name + f"_d{i}", [128, 1024], F32) for i in range(2)]
    xg = _Ring(c, name + "_xg", 2, [128, 8, 512], BF16, kind="hw")
    ps_s1 = [c.ps(name + f"_ps1{i}", [128, 512], F32) for i in range(4)]
    ps_o = [c.ps(name + f"_pso{i}", [128, 1024], F32) for i in range(2)]
    c.es = _sv
    swx = c.sem(kind="sw")
    sevac = [c.sem(), c.sem()]
    spv = [c.sem(), c.sem()]
    sevo = [c.sem(), c.sem()]

    for e in range(NE):
        g.wait_ge(sp.h, sp.n)
        if e == 0:
            swx.inc(g.dma_start(out=w1T[:], in_=W["w1T"][0].rearrange("(a p) n -> p a n", p=128)), 16)
            swx.inc(g.dma_start(out=w3T[:], in_=W["w3T"][0].rearrange("(a p) n -> p a n", p=128)), 16)
        swx.inc(g.dma_start(out=w2T[:], in_=W["w2T"][e].rearrange("(a p) n -> p a n", p=128)), 16)
        swx.wait(t)
        sacc[0].wait(g); sacc[1].wait(g)
        for gi in range(8):
            bx = gi % 2
            if gi < 2:
                if e > 0 or gi > 0:
                    sy.wait_ge(sp.h, sp.n)
                xg.dma(8 * e + gi, sy.dma_start(
                    out=xg.buf(gi)[:],
                    in_=xntd[:, :, 512 * gi:512 * (gi + 1)].rearrange("k p j -> p k j")))
            xg.wait_loaded(t, 8 * e + gi)
            for m in range(22):
                par = m % 2
                pa, pb = ps_s1[2 * par], ps_s1[2 * par + 1]
                if m >= 2:
                    t.wait_ge(sevac[par].h, sevac[par].n)
                for k in range(8):
                    ia = t.matmul(pa[:], w1T[:, k, 128 * m:128 * (m + 1)],
                                  xg.buf(gi)[:, k, :],
                                  start=(k == 0), stop=(k == 7))
                    ib = t.matmul(pb[:], w3T[:, k, 128 * m:128 * (m + 1)],
                                  xg.buf(gi)[:, k, :],
                                  start=(k == 0), stop=(k == 7))
                sp.inc(ia); spv[par].inc(ib)
                s.wait_ge(sp.h, sp.n)
                s.wait_ge(sevac[par].h, sevac[par].n)
                sc.inc(s.activation(tsil[par][:], pa[:], AF.Silu))
                v.wait_ge(sc.h, sc.n)
                v.wait_ge(spv[par].h, spv[par].n)
                sevac[par].inc(v.tensor_tensor(h_all[:, m, :], tsil[par][:], pb[:], ALU.mult))
            if gi == 7 and e + 1 < NE:
                g.wait_ge(sevac[0].h, sevac[0].n)
                g.wait_ge(sevac[1].h, sevac[1].n)
                swx.inc(g.dma_start(out=w1T[:], in_=W["w1T"][e + 1].rearrange("(a p) n -> p a n", p=128)), 16)
                swx.inc(g.dma_start(out=w3T[:], in_=W["w3T"][e + 1].rearrange("(a p) n -> p a n", p=128)), 16)
            for tt in range(4):
                po = ps_o[tt % 2]
                tile = 4 * gi + tt
                if tt == 0:
                    t.wait_ge(sevac[0].h, sevac[0].n)
                    t.wait_ge(sevac[1].h, sevac[1].n)
                t.wait_ge(sevo[tt % 2].h, sevo[tt % 2].n)
                for m in range(22):
                    i1 = t.matmul(po[:, 0:512], h_all[:, m, 128 * tt:128 * (tt + 1)],
                                  w2T[:, m, 0:512], start=(m == 0), stop=(m == 21))
                    i2 = t.matmul(po[:, 512:1024], h_all[:, m, 128 * tt:128 * (tt + 1)],
                                  w2T[:, m, 512:1024], start=(m == 0), stop=(m == 21))
                sp.inc(i1); sp.inc(i2)
                sp.wait(s)
                db = delta[tt % 2]
                sacc[tt % 2].wait(s)
                sevo[tt % 2].inc(s.activation(db[:], po[:], AF.Copy, scale=cw[:, tile, e:e + 1]))
                g.wait_ge(sevo[tt % 2].h, sevo[tt % 2].n)
                sacc[tt % 2].inc(g.dma_start(out=Xout[tile], in_=db[:], accum_op=ALU.add), 16)
            if gi + 2 < 8:
                sy.wait_ge(sp.h, sp.n)
                xg.dma(8 * e + gi + 2, sy.dma_start(
                    out=xg.buf(gi)[:],
                    in_=xntd[:, :, 512 * (gi + 2):512 * (gi + 3)].rearrange("k p j -> p k j")))
    sacc[0].wait(g); sacc[1].wait(g); sacc[0].wait(sy); sacc[1].wait(sy)
    for smx in (sp, sv, sc, swx):
        for eng in (t, v, s, g, sy):
            smx.wait(eng)
    esE.close()
    return dict(sacc=sacc)

def _prep_weights_D(inputs):
    w1 = np.concatenate([inputs["L1_ffn_w1"], inputs["L1_ffn_new_w1"]], 0)
    w3 = np.concatenate([inputs["L1_ffn_w3"], inputs["L1_ffn_new_w3"]], 0)
    w2 = np.concatenate([inputs["L1_ffn_w2"], inputs["L1_ffn_new_w2"]], 0)
    gate = np.concatenate([inputs["L1_ffn_gate_weight"], inputs["L1_ffn_new_gate_weight"]], 0)
    gb = np.zeros((1, 16), np.float32)
    gb[0, 8] = inputs["L1_ffn_gate_bias"][0]
    gb[0, 9:] = -1e30
    gateTpad = np.zeros((1024, 16), np.float32)
    gateTpad[:, 0:9] = gate.T
    return dict(
        w1T=np.ascontiguousarray(np.transpose(w1, (0, 2, 1))),
        w3T=np.ascontiguousarray(np.transpose(w3, (0, 2, 1))),
        w2T=np.ascontiguousarray(np.transpose(w2, (0, 2, 1))),
        gateT=np.ascontiguousarray(gateTpad),
        gbias=gb,
    )

def build_D():
    nc = bass.Bass("TRN2", target_bir_lowering=False)
    X = nc.dram_tensor("X", [33, 128, D], F32, kind="ExternalInput").ap()
    XO = nc.dram_tensor("XO", [33, 128, D], F32, kind="ExternalOutput").ap()
    Wap = {k: nc.dram_tensor("W_" + k, list(s), F32, kind="ExternalInput").ap()
           for k, s in D_W_SHAPES.items()}
    xntd = nc.dram_tensor("S_xnT", [8, 128, NTOK], BF16, kind="Internal").ap()
    es = ExitStack()
    c = Ctx(nc, es)
    emit_phase_L1FFN(c, X, XO, Wap, xntd, "Dp")
    es.close()
    return nc

E_W_SHAPES = dict(WqkT=(1024, 2048), bqk=(1, 2048), lepw=(128, 8, 3), lepb=(128, 8),
                  kidx=(1, 2048), sgn=(1, 2048), mask67=(67, 8))
LN1E4 = float(np.log(10000.0))
NSEQ = 8192.0
E_SKIP = set()

def emit_phase_linattn(c, Xin, Xout, W, name, mode="full", ccg=None, ccpart=None,
                       hrowd=None, qeD=None, qrD=None, uTD=None):
    nc = c.nc
    t, v, s, g, sy = c.t, c.v, c.s, c.g, c.sy
    pre = (mode == "pre")

    WqkT = c.sb(name + "_wq", [128, 8, 2048], BF16)
    bqk = c.sb(name + "_bq", [1, 2048], BF16)
    onesr = c.sb(name + "_ones", [1, 128], BF16)
    ident = c.sb(name + "_id", [128, 128], BF16)
    cos_bc = c.sb(name + "_cos", [128, 2048], BF16)
    sin_bc = c.sb(name + "_sin", [128, 2048], BF16)
    es_bc = ExitStack(); _svb = c.es; c.es = es_bc
    r_th = c.sb(name + "_rth", [1, 2048], F32, side="right")
    r_sin = c.sb(name + "_rsin", [1, 2048], F32, side="right")
    r_cos = c.sb(name + "_rcos", [1, 2048], F32, side="right")
    r_sgn = c.sb(name + "_rsgn", [1, 2048], F32, side="right")
    c.es = _svb
    rowb = c.sb(name + "_rowb", [1, 2048], BF16)
    lepw = c.sb(name + "_lw", [128, 8, 3], F32)
    lepb = c.sb(name + "_lb", [128, 8], F32)
    m67 = c.sb(name + "_m67", [67, 8], F32)
    sg = c.sem(); sp = c.sem(); sv = c.sem(); sc = c.sem()
    sw = c.sem(kind="sw")

    c.const(EPS_LN); c.const(float(np.pi / 2)); c.const(1.0)
    c.sconst.wait(s)
    sw.inc(g.dma_start(out=WqkT[:], in_=W["WqkT"].rearrange("(a p) n -> p a n", p=128)), 16)
    sw.inc(g.dma_start(out=bqk[:], in_=W["bqk"]), 16)
    sw.inc(g.dma_start(out=r_th[:], in_=W["kidx"]), 16)
    sw.inc(g.dma_start(out=r_sgn[:], in_=W["sgn"]), 16)
    sw.inc(g.dma_start(out=lepw[:], in_=W["lepw"]), 16)
    sw.inc(g.dma_start(out=lepb[:], in_=W["lepb"]), 16)
    if not pre:
        sw.inc(g.dma_start(out=m67[:], in_=W["mask67"]), 16)
    sg.inc(g.memset(onesr[:], 1.0))
    emit_identity(c, ident, sg)

    _svb = c.es; c.es = es_bc
    ps_bc = c.ps(name + "_psbc", [128, 512], F32)
    c.es = _svb
    sw.wait(s)
    sc.inc(s.activation(r_th[:], r_th[:], AF.Exp, scale=-LN1E4))
    sc.wait(s)
    sc.inc(s.activation(r_sin[:], r_th[:], AF.Sin))
    sc.wait(s)
    sc.inc(s.activation(r_cos[:], r_th[:], AF.Sin, bias=float(np.pi / 2)))
    sc.wait(v); sw.wait(v)
    sv.inc(v.tensor_tensor(r_sin[:], r_sin[:], r_sgn[:], ALU.mult))
    for which, src_row, dst in ((0, r_cos, cos_bc), (1, r_sin, sin_bc)):
        sv.wait(v)
        sv.inc(v.tensor_copy(rowb[:], src_row[:]))
        for j in range(4):
            sv.wait(t); sg.wait(t)
            sp.inc(t.matmul(ps_bc[:], onesr[:], rowb[0:1, 512 * j:512 * (j + 1)],
                            start=True, stop=True))
            sp.wait(v)
            sv.inc(v.tensor_copy(dst[:, 512 * j:512 * (j + 1)], ps_bc[:]))

    if (not pre) and ("combine" in E_SKIP):
        kv_sb = c.sb(name + "_kv", [128, 1024], BF16)
        kvp = c.sb(name + "_kvp", [128, 1024], BF16)
        ksum_bc = c.sb(name + "_ksb", [128, 1024], BF16)
        halocols = c.sb(name + "_hc", [128, 2, 8], F32)
        sg.inc(g.memset(kv_sb[:], 0.001))
        sg.inc(g.memset(kvp[:], 0.001))
        sg.inc(g.memset(ksum_bc[:], 0.001))
        sg.inc(g.memset(halocols[:], 0.0))
        sg.wait(v); sg.wait(s); sg.wait(t)
    elif not pre:
        kv_sb = c.sb(name + "_kv", [128, 1024], BF16)
        kvp = c.sb(name + "_kvp", [128, 1024], BF16)
        ksum_bc = c.sb(name + "_ksb", [128, 1024], BF16)
        acc67 = [c.sb(name + f"_a67{i}", [67, 1024], F32) for i in range(2)]
        halocols = c.sb(name + "_hc", [128, 2, 8], F32)
        es_cc = ExitStack(); _sv = c.es; c.es = es_cc
        slot = _Ring(c, name + "_sl", 2, [67, 1024], F32, kind="hw")
        c.es = _sv
        sv2 = c.sem()
        sw.wait(v)
        for j in range(8):
            if j >= 2:
                sy.wait_ge(sv2.h, sv2.n)
            slot.dma(j, sy.dma_start(out=slot.buf(j)[:], in_=ccg[j]))
            slot.wait_loaded(v, j)
            dst, src_ = acc67[(j + 1) % 2], acc67[j % 2]
            if j == 0:
                sv2.inc(v.tensor_scalar(acc67[1][:], slot.buf(0)[:], m67[:, 0:1], None, ALU.mult))
            else:
                sv2.wait(v)
                sv2.inc(v.scalar_tensor_tensor(dst[:], slot.buf(j)[:], m67[:, j:j + 1],
                                               src_[:], ALU.mult, ALU.add))
        sv2.wait(v)
        sv2.inc(v.tensor_copy(kv_sb[0:64, :], acc67[0][0:64, :]))
        sv2.inc(v.tensor_copy(kv_sb[64:128, :], acc67[0][0:64, :]))
        sv2.wait(g)
        sg.inc(g.memset(kvp[:], 0.0))
        sg.wait(v)
        sv2.wait(v)
        for j_ in range(8):
            sv2.inc(v.tensor_copy(kvp[0:64, 128 * j_:128 * j_ + 64],
                                  kv_sb[0:64, 128 * j_:128 * j_ + 64]))
            sv2.inc(v.tensor_copy(kvp[64:128, 128 * j_ + 64:128 * (j_ + 1)],
                                  kv_sb[64:128, 128 * j_ + 64:128 * (j_ + 1)]))
        sv2.wait(v)
        sv2.inc(v.tensor_copy(rowb[:, 0:1024], acc67[0][64:65, :]))
        sv2.wait(t)
        for j in range(2):
            sp.inc(t.matmul(ps_bc[:], onesr[:], rowb[0:1, 512 * j:512 * (j + 1)],
                            start=True, stop=True))
            sp.wait(v)
            sv2.inc(v.tensor_copy(ksum_bc[:, 512 * j:512 * (j + 1)], ps_bc[:]))
            sv2.wait(t)
        sv2.wait(g)
        shh = c.sem(kind="sw")
        shh.inc(g.dma_start(out=hrowd[:], in_=acc67[0][65:67, :]), 16)
        shh.wait(g)
        with nc.allow_non_contiguous_dma(reason="2KB halo gather"):
            shh.inc(g.dma_start(out=halocols[:],
                                in_=hrowd[:].rearrange("r (m p) -> p r m", p=128)), 16)
        shh.wait(v); shh.wait(s)
        for smx in (sv2, shh):
            for eng in (t, v, s, g, sy):
                smx.wait(eng)
        es_cc.close()

    es_bc.close()
    es0 = ExitStack(); _sv = c.es; c.es = es0
    xin = [c.sb(name + f"_x{i}", [128, 1024], F32) for i in range(2)]
    ub = c.sb(name + "_u", [128, 1024], BF16)
    uT = [c.sb(name + f"_uT{i}", [128, 1024], BF16) for i in range(2)]
    ulastc = [c.sb(name + f"_ulc{i}", [128, 8], BF16) for i in range(2)]
    e0 = c.sb(name + "_e0", [128, 2048], F32)
    e1 = c.sb(name + "_e1", [128, 2048], F32)
    qke = c.sb(name + "_qke", [128, 2048], BF16)
    t1b = c.sb(name + "_t1", [128, 2048], BF16)
    t2b = c.sb(name + "_t2", [128, 2048], BF16)
    qkr = c.sb(name + "_qkr", [128, 2048], BF16)
    qz = c.sb(name + "_qz", [128, 1024], BF16)
    qzT = c.sb(name + "_qzT", [128, 1024], BF16)
    zt = c.sb(name + "_zt", [128, 1024], F32)
    zden = c.sb(name + "_zd", [128, 16], F32)
    pef = [c.sb(name + f"_pef{i}", [128, 1024], F32) for i in range(2)]
    pe_b = c.sb(name + "_peb", [128, 1024], BF16)
    x1s = [c.sb(name + f"_x1{i}", [128, 1024], F32) for i in range(2)]
    xo = _Ring(c, name + "_xo", 2, [128, 1024], F32, kind="sw")
    scratch = {k: c.sb(name + "_s" + k, [128, 1024] if k == "sq" else [128, 1], F32)
               for k in ("sq", "c0", "c1", "c2", "c3")}
    ps_tr = c.ps(name + "_pstr", [128, 1024], BF16)
    ps_qk = c.ps(name + "_psqk", [128, 2048], F32)
    if pre:
        ps_kv = c.ps(name + "_pskv", [64, 1024], F32)
        urow = c.sb(name + "_ur", [1, 2048], F32)
        ks_acc = [c.sb(name + f"_ka{i}", [1, 1024], F32) for i in range(2)]
    else:
        ps_at = c.ps(name + "_psat", [128, 1024], F32)
        ps_pe = c.ps(name + "_pspe", [128, 1024], BF16)
    c.es = _sv
    sq_ = [c.sem(kind="hw"), c.sem(kind="hw")]
    sst = [c.sem(kind="hw"), c.sem(kind="hw")]
    surow = c.sem(kind="sw")

    _sv = c.es; c.es = es0
    onescb = c.sb(name + "_oc", [128, 1], BF16)
    kvf = c.sb(name + "_kvf", [64, 1024], F32)
    ksf = c.sb(name + "_ksf", [1, 1024], F32)
    c.es = _sv
    sg.inc(g.memset(onescb[:], 1.0))
    if not pre:
        env_loc = dict(sp=sp, sv=sv, sc=sc, pef=pef, x1s=x1s, xo=xo, lepw=lepw,
                       pe_b=pe_b, ps_pe=ps_pe, ident=ident, Xout=Xout)

    if pre:
        st3 = [c.sem(kind="sw"), c.sem(kind="sw")]
        for ti in range(NT):
            bi = ti % 2
            if ti < 2:
                sq_[bi].inc(sy.dma_start(out=xin[bi][:], in_=Xin[ti]), 16)
            v.wait_ge(sq_[bi].h, sq_[bi].base + 16 * (ti // 2 + 1))
            s.wait_ge(sq_[bi].h, sq_[bi].base + 16 * (ti // 2 + 1))
            emit_ln_stats(c, xin[bi][:], sc, sv, scratch)
            sc.wait(v)
            v.wait_ge(sp.h, sp.n)
            surow.wait(v)
            st3[bi].wait(v)
            sv.inc(v.tensor_scalar(ub[:], xin[bi][:], scratch["c2"][:], scratch["c3"][:],
                                   ALU.subtract, ALU.mult))
            if ti == 0:
                sv.wait(g)
                surow.inc(g.dma_start(out=urow[0:1, 0:1024], in_=ub[0:1, :]), 16)
            sv.wait(t)
            if ti == 0:
                sg.wait(t); sw.wait(t)
            emit_transpose_tiles(c, ub, uT[bi], ident, ps_tr, sp, sv, 8)
            sv.wait(t)
            if ti >= 1:
                t.wait_ge(sv.h, sv.n)
            for nb in range(4):
                for k in range(8):
                    t.matmul(ps_qk[:, 512 * nb:512 * (nb + 1)], uT[bi][:, 128 * k:128 * (k + 1)],
                             WqkT[:, k, 512 * nb:512 * (nb + 1)], start=(k == 0), stop=False)
                sp.inc(t.matmul(ps_qk[:, 512 * nb:512 * (nb + 1)], onesr[:],
                                bqk[0:1, 512 * nb:512 * (nb + 1)], start=False, stop=True))
            sp.wait(v)
            sv.inc(v.tensor_scalar(e0[:], ps_qk[:], 0.0, None, ALU.min))
            sv.wait(s)
            sc.inc(s.activation(e1[:], e0[:], AF.Exp))
            sv.wait(v); sc.wait(v)
            sv.inc(v.scalar_tensor_tensor(e0[:], e0[:], -1.0, ps_qk[:], ALU.mult, ALU.add))
            sc.wait(v); sv.wait(v)
            sv.inc(v.tensor_add(qke[:], e0[:], e1[:]))
            sv.wait(v)
            sv.inc(v.tensor_tensor(t1b[:], qke[:], cos_bc[:], ALU.mult))
            qv = qke[:].rearrange("p (b s j) -> p b s j", s=2, j=32)
            sv_ = sin_bc[:].rearrange("p (b s j) -> p b s j", s=2, j=32)
            t2v = t2b[:].rearrange("p (b s j) -> p b s j", s=2, j=32)
            sv.wait(v)
            sv.inc(v.tensor_tensor(t2v[:, :, 0, :], qv[:, :, 1, :], sv_[:, :, 0, :], ALU.mult))
            sv.inc(v.tensor_tensor(t2v[:, :, 1, :], qv[:, :, 0, :], sv_[:, :, 1, :], ALU.mult))
            sv.wait(v)
            sv.inc(v.tensor_add(qkr[:], t1b[:], t2b[:]))
            sv.wait(g)
            st3[bi].inc(g.dma_start(out=uTD[ti], in_=uT[bi][:]), 16)
            st3[bi].inc(g.dma_start(out=qeD[ti], in_=qke[:, 0:1024]), 16)
            st3[bi].inc(g.dma_start(out=qrD[ti], in_=qkr[:, 0:1024]), 16)
            sv.wait(t)
            for h in range(16):
                ins = t.matmul(ps_kv[0:64, 64 * h:64 * (h + 1)],
                               qkr[:, 1024 + 64 * h:1024 + 64 * (h + 1)],
                               ub[:, 64 * h:64 * (h + 1)],
                               start=(ti == 0 and h % 8 == 0), stop=(ti == NT - 1 and h % 8 == 7))
            sp.inc(ins)
            i2 = t.matmul(ps_qk[0:1, 0:512], onescb[:], qke[:, 1024:1536], start=True, stop=True)
            sp.inc(i2)
            i3 = t.matmul(ps_qk[0:1, 512:1024], onescb[:], qke[:, 1536:2048], start=True, stop=True)
            sp.inc(i3)
            sp.wait(v)
            if ti == 0:
                sv.inc(v.tensor_copy(ks_acc[0][:], ps_qk[0:1, 0:1024]))
            else:
                sv.inc(v.tensor_tensor(ks_acc[ti % 2][:], ks_acc[(ti + 1) % 2][:],
                                       ps_qk[0:1, 0:1024], ALU.add))
            if ti == NT - 1:
                sv.wait(g)
                surow.inc(g.dma_start(out=urow[0:1, 1024:2048], in_=ub[127:128, :]), 16)
            if ti + 2 < NT:
                sy.wait_ge(sv.h, sv.n)
                sy.wait_ge(sp.h, sp.n)
                sq_[bi].inc(sy.dma_start(out=xin[bi][:], in_=Xin[ti + 2]), 16)
    else:
        _sv = c.es; c.es = es0
        qel = _Ring(c, name + "_qel", 2, [128, 1024], BF16, kind="hw")
        qrl = _Ring(c, name + "_qrl", 2, [128, 1024], BF16, kind="hw")
        utl = _Ring(c, name + "_utl", 2, [128, 1024], BF16, kind="hw")
        c.es = _sv
        for ti in range(NT):
            bi = ti % 2
            if ti < 2:
                sq_[bi].inc(sy.dma_start(out=xin[bi][:], in_=Xin[ti]), 16)
                qel.dma(ti, sy.dma_start(out=qel.buf(ti)[:], in_=qeD[ti]))
                qrl.dma(ti, sy.dma_start(out=qrl.buf(ti)[:], in_=qrD[ti]))
                utl.dma(ti, sy.dma_start(out=utl.buf(ti)[:], in_=uTD[ti]))
            v.wait_ge(sq_[bi].h, sq_[bi].base + 16 * (ti // 2 + 1))
            for r_ in (qel, qrl, utl):
                r_.wait_loaded(v, ti)
                r_.wait_loaded(t, ti)
            uTb = utl.buf(ti)
            sv.wait(v)
            sv.inc(v.tensor_tensor(zt[:], qel.buf(ti)[:], ksum_bc[:], ALU.mult))
            sv.wait(v)
            sv.inc(v.tensor_reduce(zden[:].rearrange("p (h j) -> p h j", j=1),
                                   zt[:].rearrange("p (h j) -> p h j", j=64), AX.X, ALU.add))
            sv.wait(v)
            sv.inc(v.tensor_scalar(zden[:], zden[:], 1.0 / NSEQ, 1e-6, ALU.mult, ALU.add))
            sv.wait(v)
            sv.inc(v.reciprocal(zden[:], zden[:]))
            sv.wait(v)
            sv.inc(v.tensor_scalar(zden[:], zden[:], 1.0 / NSEQ, None, ALU.mult))
            sv.wait(v)
            sv.inc(v.tensor_tensor(qz[:].rearrange("p (h d) -> p h d", d=64),
                                   qrl.buf(ti)[:].rearrange("p (h d) -> p h d", d=64),
                                   zden[:].unsqueeze(2).broadcast_to([128, 16, 64]), ALU.mult))
            sv.wait(t)
            if ti == 0:
                sg.wait(t); sw.wait(t)
            emit_transpose_tiles(c, qz, qzT, ident, ps_tr, sp, sv, 8)
            sv.wait(t)
            if ti >= 1:
                t.wait_ge(sv.h, sv.n)
            for j_ in range(8):
                ins = t.matmul(ps_at[:, 128 * j_:128 * (j_ + 1)],
                               qzT[:, 128 * j_:128 * (j_ + 1)],
                               kvp[:, 128 * j_:128 * (j_ + 1)],
                               start=(j_ % 4 == 0), stop=(j_ % 4 == 3))
                if j_ % 4 == 3:
                    sp.inc(ins)
            lcp = ulastc[(ti + 1) % 2]
            pf = pef[bi]
            for m in range(8):
                um = uTb[:, 128 * m:128 * (m + 1)]
                sv.wait(v); sc.wait(v)
                sv.inc(v.tensor_scalar(pf[:, 128 * m:128 * (m + 1)], um, lepw[:, m, 1:2], None, ALU.mult))
                sv.wait(v)
                sv.inc(v.scalar_tensor_tensor(pf[:, 128 * m:128 * m + 127], um[:, 1:128],
                                              lepw[:, m, 2:3], pf[:, 128 * m:128 * m + 127],
                                              ALU.mult, ALU.add))
                sv.wait(v)
                sv.inc(v.scalar_tensor_tensor(pf[:, 128 * m + 1:128 * (m + 1)], um[:, 0:127],
                                              lepw[:, m, 0:1], pf[:, 128 * m + 1:128 * (m + 1)],
                                              ALU.mult, ALU.add))
                sv.wait(v)
                lsrc = halocols[:, 1, m:m + 1] if ti == 0 else lcp[:, m:m + 1]
                sv.inc(v.scalar_tensor_tensor(pf[:, 128 * m:128 * m + 1], lsrc,
                                              lepw[:, m, 0:1], pf[:, 128 * m:128 * m + 1],
                                              ALU.mult, ALU.add))
                sv.wait(v)
                sv.inc(v.tensor_scalar(pf[:, 128 * m:128 * (m + 1)], pf[:, 128 * m:128 * (m + 1)],
                                       lepb[:, m:m + 1], None, ALU.add))
            sv.wait(v)
            for m in range(8):
                sv.inc(v.tensor_copy(ulastc[bi][:, m:m + 1], uTb[:, 128 * m + 127:128 * (m + 1)]))
            sp.wait(v)
            sv.wait(v)
            sv.inc(v.tensor_add(x1s[bi][:], ps_at[:], xin[bi][:]))
            if ti >= 1:
                _finish_lepe_tile(c, env_loc, ti - 1, uTb, None)
            if ti + 2 < NT:
                sy.wait_ge(sv.h, sv.n)
                sq_[bi].inc(sy.dma_start(out=xin[bi][:], in_=Xin[ti + 2]), 16)
                qel.dma(ti + 2, sy.dma_start(out=qel.buf(ti)[:], in_=qeD[ti + 2]))
                qrl.dma(ti + 2, sy.dma_start(out=qrl.buf(ti)[:], in_=qrD[ti + 2]))
                utl.dma(ti + 2, sy.dma_start(out=utl.buf(ti)[:], in_=uTD[ti + 2]))
    if not pre:
        _finish_lepe_tile(c, env_loc, NT - 1, None, halocols)
        for i_ in range(2):
            xo.sems[i_].wait(g); xo.sems[i_].wait(sy)
    else:
        sp.wait(v); sv.wait(v)
        sv.inc(v.tensor_copy(kvf[:], ps_kv[0:64, :]))
        sv.inc(v.tensor_copy(ksf[:], ks_acc[(NT - 1) % 2][:]))
        sv.wait(g)
        surow.wait(g)
        sfin = c.sem(kind="sw")
        sfin.inc(g.dma_start(out=ccpart[0:64, :], in_=kvf[:]), 16)
        sfin.inc(g.dma_start(out=ccpart[64:65, :], in_=ksf[:]), 16)
        sfin.inc(g.dma_start(out=ccpart[65:66, :], in_=urow[0:1, 0:1024]), 16)
        sfin.inc(g.dma_start(out=ccpart[66:67, :], in_=urow[0:1, 1024:2048]), 16)
        sfin.wait(g); sfin.wait(sy)
    es0.close()
    return None

def _finish_lepe_tile(c, L, tj, u_next, halocols):
    t, v, s, g, sy = c.t, c.v, c.s, c.g, c.sy
    sp, sv, sc = L["sp"], L["sv"], L["sc"]
    pef, x1s, xo, lepw = L["pef"], L["x1s"], L["xo"], L["lepw"]
    pe_b, ps_pe, ident, Xout = L["pe_b"], L["ps_pe"], L["ident"], L["Xout"]
    bj = tj % 2
    pf = pef[bj]
    for m in range(8):
        rsrc = (halocols[:, 0, m:m + 1] if u_next is None
                else u_next[:, 128 * m:128 * m + 1])
        sv.wait(v)
        sv.inc(v.scalar_tensor_tensor(pf[:, 128 * m + 127:128 * (m + 1)], rsrc,
                                      lepw[:, m, 2:3], pf[:, 128 * m + 127:128 * (m + 1)],
                                      ALU.mult, ALU.add))
    sv.wait(v)
    sv.inc(v.tensor_copy(pe_b[:], pf[:]))
    sv.wait(t)
    t.wait_ge(sv.h, sv.n)
    for m in range(8):
        ins = t.transpose(ps_pe[:, 128 * m:128 * (m + 1)], pe_b[:, 128 * m:128 * (m + 1)], ident[:])
    sp.inc(ins)
    sp.wait(v)
    ob = xo.buf(tj)
    v.wait_ge(xo.sem(tj).h, xo.sem(tj).base + 16 * (tj // 2))
    sv.inc(v.tensor_add(ob[:], x1s[bj][:], ps_pe[:]))
    sv.wait(g)
    xo.dma(tj, g.dma_start(out=Xout[tj], in_=ob[:]))


def _perm_qk():
    idx = np.zeros(2048, np.int64)
    for half in range(2):
        for h in range(16):
            for kk in range(32):
                for s_ in range(2):
                    new = half * 1024 + h * 64 + 32 * s_ + kk
                    old = half * 1024 + h * 64 + 2 * kk + s_
                    idx[new] = old
    return idx

def _prep_weights_E(inputs):
    perm = _perm_qk()
    WqkT = np.ascontiguousarray(inputs["L1_attn_W_qk"][perm].T)
    bqk = np.ascontiguousarray(inputs["L1_attn_b_qk"][perm][None, :])
    lepw = np.ascontiguousarray(
        inputs["L1_attn_lepe_w"][:, 0, :].reshape(8, 128, 3).transpose(1, 0, 2))
    lepb = np.ascontiguousarray(inputs["L1_attn_lepe_b"].reshape(8, 128).T)
    kidx = np.zeros((1, 2048), np.float32)
    sgn = np.zeros((1, 2048), np.float32)
    for half in range(2):
        for h in range(16):
            for s_ in range(2):
                for kk in range(32):
                    pos = half * 1024 + h * 64 + 32 * s_ + kk
                    kidx[0, pos] = (h * 32 + kk) / 512.0
                    sgn[0, pos] = -1.0 if s_ == 0 else 1.0
    return dict(WqkT=WqkT, bqk=bqk, lepw=lepw, lepb=lepb, kidx=kidx, sgn=sgn)

def _core_masks_E(cid):
    b = cid // 2
    m = np.zeros((67, 8), np.float32)
    m[0:65, 2 * b] = 1.0
    m[0:65, 2 * b + 1] = 1.0
    if cid % 2 == 0:
        m[65, cid + 1] = 1.0
    else:
        m[66, cid - 1] = 1.0
    return dict(mask67=m)

def build_E(mode="full"):
    nc = bass.Bass("TRN2", target_bir_lowering=False)
    X = nc.dram_tensor("X", [33, 128, D], F32, kind="ExternalInput").ap()
    Wap = {k: nc.dram_tensor("W_" + k, list(s), F32, kind="ExternalInput").ap()
           for k, s in E_W_SHAPES.items()}
    es = ExitStack()
    c = Ctx(nc, es)
    if mode == "pre":
        ccpart = nc.dram_tensor("CP", [67, 1024], F32, kind="ExternalOutput").ap()
        emit_phase_linattn(c, X, None, Wap, "Ep", mode="pre", ccpart=ccpart)
    else:
        XO = nc.dram_tensor("XO", [33, 128, D], F32, kind="ExternalOutput").ap()
        ccg = nc.dram_tensor("CG", [8, 67, 1024], F32, kind="ExternalInput").ap()
        hrowd = nc.dram_tensor("S_hrow", [2, 1024], F32, kind="Internal").ap()
        emit_phase_linattn(c, X, XO, Wap, "Ef", mode="full", ccg=ccg, hrowd=hrowd)
    es.close()
    return nc

def _global_barrier(c):
    for eng in (c.t, c.v, c.s, c.g, c.sy):
        for sm in c._active:
            if sm.n > 0:
                sm.wait(eng)

def build_full():
    nc = bass.Bass("TRN2", target_bir_lowering=False)
    X = nc.dram_tensor("X", [33, 128, D], F32, kind="ExternalInput").ap()
    XO = nc.dram_tensor("XO", [33, 128, D], F32, kind="ExternalOutput").ap()
    Xs = nc.dram_tensor("S_X", [33, 128, D], F32, kind="Internal").ap()
    WapA = {k: nc.dram_tensor("WA_" + k, list(s), F32, kind="ExternalInput").ap()
            for k, s in A_W_SHAPES.items()}
    WapB = {k: nc.dram_tensor("WB_" + k, list(s), F32, kind="ExternalInput").ap()
            for k, s in B_W_SHAPES.items()}
    WapD = {k: nc.dram_tensor("WD_" + k, list(s), F32, kind="ExternalInput").ap()
            for k, s in D_W_SHAPES.items()}
    WapE = {k: nc.dram_tensor("WE_" + k, list(s), F32, kind="ExternalInput").ap()
            for k, s in E_W_SHAPES.items()}
    DsB = {k: nc.dram_tensor("SB_" + k, list(sh), dt_, kind="Internal",
                             addr_space=("Shared" if k == "ccout" else "Local")).ap()
           for k, (sh, dt_) in B_SCRATCH.items()}
    xntd = nc.dram_tensor("S_xnT", [8, 128, NTOK], BF16, kind="Internal").ap()
    ccinE = nc.dram_tensor("S_ccinE", [67, 1024], F32, kind="Internal").ap()
    ccoutE = nc.dram_tensor("S_ccoutE", [536, 1024], F32, kind="Internal",
                            addr_space="Shared").ap()
    hrowd = nc.dram_tensor("S_hrow", [2, 1024], F32, kind="Internal").ap()
    qeD = nc.dram_tensor("S_qeD", [NT, 128, D], BF16, kind="Internal").ap()
    qrD = nc.dram_tensor("S_qrD", [NT, 128, D], BF16, kind="Internal").ap()
    uTD = nc.dram_tensor("S_uTD", [NT, 128, D], BF16, kind="Internal").ap()

    es = ExitStack()
    c = Ctx(nc, es)
    c.const(EPS_LN); c.const(1.0); c.const(float(np.pi / 2))

    def phase(fn):
        es_p = ExitStack()
        old = c.es
        c.es = es_p
        fn()
        c.es = old
        _global_barrier(c)
        es_p.close()
        c.release_sems()

    wstateA = emit_L0FFN_weights(c, WapA, "AW")
    phase(lambda: emit_phase_L0FFN(c, X, Xs, WapA, NT + 1, "A", wsem=wstateA))
    phase(lambda: emit_phase_mamba_main(c, Xs, Xs, WapB, DsB, "B", mode="cc"))
    phase(lambda: emit_phase_L0FFN(c, Xs, Xs, WapA, NT, "C", wsem=wstateA))
    phase(lambda: emit_phase_L1FFN(c, Xs, Xs, WapD, xntd, "Dp", inplace=True))
    phase(lambda: emit_phase_linattn(c, Xs, None, WapE, "Ep", mode="pre", ccpart=ccinE,
                                     qeD=qeD, qrD=qrD, uTD=uTD))
    def _ecc():
        scc = c.sem(kind="cc")
        scc.inc(c.g.collective_compute(
            "AllGather", ALU.bypass, replica_groups=[list(range(NCORES))],
            ins=[ccinE.opt()], outs=[ccoutE.opt()]), 1)
        scc.wait(c.g)
    phase(_ecc)
    ccgv = ccoutE.rearrange("(s p) f -> s p f", p=67)
    phase(lambda: emit_phase_linattn(c, Xs, Xs, WapE, "Ef", mode="full",
                                     ccg=ccgv, hrowd=hrowd, qeD=qeD, qrD=qrD, uTD=uTD))
    phase(lambda: emit_phase_L1FFN(c, Xs, XO, WapD, xntd, "Fp", inplace=False))
    es.close()
    return nc

def kernel(**inputs):
    x = np.ascontiguousarray(inputs["L0_attn_x"], np.float32)
    shards = shard_stream(x)
    WA = _prep_weights_A(inputs)
    WB = _prep_weights_B(inputs)
    WD = _prep_weights_D(inputs)
    WE = _prep_weights_E(inputs)
    cores = list(range(NCORES))
    nc = build_full()
    in_maps = []
    for cid in cores:
        m = dict(X=shards[cid])
        m.update({"WA_" + k: v for k, v in WA.items()})
        m.update({"WB_" + k: v for k, v in WB.items()})
        m.update({"WB_" + k: v for k, v in _core_masks_B(cid).items()})
        m.update({"WD_" + k: v for k, v in WD.items()})
        m.update({"WE_" + k: v for k, v in WE.items()})
        m.update({"WE_" + k: v for k, v in _core_masks_E(cid).items()})
        in_maps.append(m)
    res = run_bass_kernel_spmd(nc, in_maps, core_ids=cores)
    out = np.empty((4, 8192, D), np.float32)
    for cid in cores:
        b, h = cid // 2, cid % 2
        out[b, h * NTOK:(h + 1) * NTOK] = (
            np.asarray(res.results[cid]["XO"]).reshape(NT + 1, 128, D)[:NT].reshape(NTOK, D))
    return out


# version 08
# speedup vs baseline: 1.3436x; 1.0051x over previous
import math
from contextlib import ExitStack

import numpy as np
import concourse.bass as bass
import concourse.mybir as mybir
from concourse.bass_utils import run_bass_kernel_spmd

F32 = mybir.dt.float32
BF16 = mybir.dt.bfloat16
AF = mybir.ActivationFunctionType
ALU = mybir.AluOpType
AX = mybir.AxisListType

NCORES = 8
NTOK = 4096
NT = 32
D = 1024
EPS_LN = 1e-5

class Sem:
    def __init__(self, es, nc, name):
        self.h = es.enter_context(nc.semaphore(name))
        self.n = 0
        self.base = 0
    def inc(self, instr, k=1):
        instr.then_inc(self.h, k)
        self.n += k
        return instr
    def wait(self, eng, val=None):
        eng.wait_ge(self.h, self.n if val is None else val)

class Ctx:
    def __init__(self, nc, es):
        self.nc = nc
        self.es = es
        self.sem_es = es
        self.t = nc.tensor
        self.v = nc.vector
        self.s = nc.scalar
        self.g = nc.gpsimd
        self.sy = nc.sync
        self._semid = 0
        self._pools = {}
        self._active = []
        self._consts = {}
        self.sconst = self.sem("sconst")
        self.sconst.kind = "c"
    def sem(self, name=None, kind="c"):
        pool = self._pools.setdefault(kind, [])
        if pool:
            s = pool.pop()
        else:
            self._semid += 1
            s = Sem(self.sem_es, self.nc, name or f"{kind}{self._semid}")
            s.kind = kind
        s.base = s.n
        self._active.append(s)
        return s
    def release_sems(self):
        for s in self._active:
            if s is not self.sconst:
                self._pools.setdefault(s.kind, []).append(s)
        self._active = [s for s in self._active if s is self.sconst]
    def sb(self, name, shape, dtype, side=None):
        if side is None:
            return self.es.enter_context(self.nc.sbuf_tensor(name, shape, dtype))
        return self.es.enter_context(self.nc.sbuf_tensor(name, shape, dtype, side=side))
    def ps(self, name, shape, dtype):
        return self.es.enter_context(self.nc.psum_tensor(name, shape, dtype))
    def const(self, value, dtype=F32):
        key = (dtype, float(value))
        if key not in self._consts:
            tile = self.sem_es.enter_context(
                self.nc.sbuf_tensor(f"const{len(self._consts)}", [128, 1], dtype))
            self.sconst.inc(self.g.memset(tile[:], float(value)))
            self.nc.const_aps.register(dtype, float(value), tile[:])
            self._consts[key] = tile
        return float(value)

def emit_ln(c, x_ap, xn_bf, sc, sv, scratch):
    v, s = c.v, c.s
    sq, c0, c1_, c2, c3 = (scratch["sq"], scratch["c0"], scratch["c1"],
                           scratch["c2"], scratch["c3"])
    n = x_ap.shape[-1]
    sv.inc(v.tensor_reduce(c0[:], x_ap, AX.X, ALU.add))
    sc.inc(s.activation(sq[:, 0:n], x_ap, AF.Square))
    sc.wait(v)
    sv.inc(v.tensor_reduce(c1_[:], sq[:, 0:n], AX.X, ALU.add))
    sv.wait(v)
    sv.inc(v.tensor_scalar(c2[:], c0[:], 1.0 / n, None, ALU.mult))
    sv.inc(v.tensor_scalar(c1_[:], c1_[:], 1.0 / n, None, ALU.mult))
    sv.wait(v)
    sv.inc(v.tensor_tensor(c3[:], c2[:], c2[:], ALU.mult))
    sv.wait(v)
    sv.inc(v.tensor_tensor(c1_[:], c1_[:], c3[:], ALU.subtract))
    sv.wait(s)
    sc.inc(s.activation(c3[:], c1_[:], AF.Ln, bias=EPS_LN))
    sc.wait(s)
    sc.inc(s.activation(c3[:], c3[:], AF.Exp, scale=-0.5))
    sc.wait(v)
    sv.inc(v.tensor_scalar(xn_bf, x_ap, c2[:], c3[:], ALU.subtract, ALU.mult))
    return c3, c2


def emit_ln_stats(c, x_ap, sc, sv, scratch):
    v, s = c.v, c.s
    sq, c0, c1_, c2, c3 = (scratch["sq"], scratch["c0"], scratch["c1"],
                           scratch["c2"], scratch["c3"])
    n = x_ap.shape[-1]
    sv.wait(v); sv.wait(s); sc.wait(v); sc.wait(s)
    sv.inc(v.tensor_reduce(c0[:], x_ap, AX.X, ALU.add))
    sc.inc(s.activation(sq[:, 0:n], x_ap, AF.Square))
    sc.wait(v)
    sv.inc(v.tensor_reduce(c1_[:], sq[:, 0:n], AX.X, ALU.add))
    sv.wait(v)
    sv.inc(v.tensor_scalar(c2[:], c0[:], 1.0 / n, None, ALU.mult))
    sv.inc(v.tensor_scalar(c1_[:], c1_[:], 1.0 / n, None, ALU.mult))
    sv.wait(v)
    sv.inc(v.tensor_tensor(c3[:], c2[:], c2[:], ALU.mult))
    sv.wait(v)
    sv.inc(v.tensor_tensor(c1_[:], c1_[:], c3[:], ALU.subtract))
    sv.wait(s)
    sc.inc(s.activation(c3[:], c1_[:], AF.Ln, bias=EPS_LN))
    sc.wait(s)
    sc.inc(s.activation(c3[:], c3[:], AF.Exp, scale=-0.5))

def emit_ln_f32(c, x_ap, out_ap, sc, sv, scratch):
    emit_ln_stats(c, x_ap, sc, sv, scratch)
    sc.wait(c.v)
    sv.inc(c.v.tensor_scalar(out_ap, x_ap, scratch["c2"][:], scratch["c3"][:],
                             ALU.subtract, ALU.mult))

def emit_transpose_tiles(c, src_bf, dst_bf, ident, ps_tr, sp, sv, nk):
    t, v = c.t, c.v
    for k in range(nk):
        sp.inc(t.transpose(ps_tr[:, k * 128:(k + 1) * 128],
                           src_bf[:, k * 128:(k + 1) * 128], ident[:]))
    sp.wait(v)
    sv.inc(v.tensor_copy(dst_bf[:, 0:nk * 128], ps_tr[:, 0:nk * 128]))

def emit_identity(c, ident, sg):
    g = c.g
    sg.inc(g.memset(ident[:], 0.0))
    sg.wait(g)
    sg.inc(g.affine_select(out=ident[:], in_=ident[:], compare_op=ALU.not_equal,
                           fill=1.0, base=0, pattern=[[-1, ident.shape[1]]],
                           channel_multiplier=1))

def emit_L0FFN_weights(c, W, name):
    g = c.g
    bwt = c.sb(name + "_bwt", [128, 8, 1024], BF16)
    catT = c.sb(name + "_catT", [128, 8, 118], BF16)
    catB = c.sb(name + "_catB", [112, 1024], BF16)
    baseb = c.sb(name + "_bb", [1, 1024], BF16)
    routerb = c.sb(name + "_rb", [1, 6], BF16)
    onesr = c.sb(name + "_ones", [1, 128], BF16)
    ident = c.sb(name + "_id", [128, 128], BF16)
    sw = c.sem(kind="sw")
    sg = c.sem()
    sw.inc(g.dma_start(out=bwt[:], in_=W["baseWT"].rearrange("(a p) n -> p a n", p=128)), 16)
    sw.inc(g.dma_start(out=catT[:], in_=W["catT"].rearrange("(a p) n -> p a n", p=128)), 16)
    sw.inc(g.dma_start(out=catB[:], in_=W["catB"]), 16)
    sw.inc(g.dma_start(out=baseb[:], in_=W["baseb"]), 16)
    sw.inc(g.dma_start(out=routerb[:], in_=W["routerb"]), 16)
    sg.inc(g.memset(onesr[:], 1.0))
    emit_identity(c, ident, sg)
    return dict(bwt=bwt, catT=catT, catB=catB, baseb=baseb, routerb=routerb,
                onesr=onesr, ident=ident, sw=sw, sg=sg)

def emit_phase_L0FFN(c, Xin, Xout, W, ntiles, name, wsem=None):
    nc = c.nc
    t, v, s, g, sy = c.t, c.v, c.s, c.g, c.sy

    if wsem is None:
        wstate = emit_L0FFN_weights(c, W, name)
    else:
        wstate = wsem
    bwt, catT, catB, baseb, routerb, onesr, ident = (
        wstate["bwt"], wstate["catT"], wstate["catB"], wstate["baseb"],
        wstate["routerb"], wstate["onesr"], wstate["ident"])
    sw, sg = wstate["sw"], wstate["sg"]

    xin = [c.sb(name + f"_xin{i}", [128, 1024], F32) for i in range(2)]
    xn = c.sb(name + "_xn", [128, 1024], BF16)
    xnT = c.sb(name + "_xnT", [128, 1024], BF16)
    Hw = c.sb(name + "_Hw", [128, 112], BF16)
    HwT = c.sb(name + "_HwT", [112, 128], BF16)
    gate = c.sb(name + "_gate", [128, 6], F32)
    gw = c.sb(name + "_gw", [128, 6], F32)
    delta = c.sb(name + "_delta", [128, 1024], F32)
    xout = [c.sb(name + f"_xout{i}", [128, 1024], F32) for i in range(2)]
    scratch2 = [{k: c.sb(name + f"_{k}{i}", [128, 1024] if k == "sq" else [128, 1], F32)
                 for k in ("sq", "c0", "c1", "c2", "c3")} for i in range(2)]
    scratch = scratch2[0]
    ps_tr = c.ps(name + "_pstr", [128, 1024], BF16)
    ps_h = c.ps(name + "_psh", [128, 128], F32)
    ps_o = [c.ps(name + f"_pso{i}", [128, 1024], F32) for i in range(2)]
    ps_t2 = c.ps(name + "_pst2", [128, 128], BF16)

    c.const(EPS_LN)
    c.sconst.wait(c.s)
    sx = [c.sem(kind="hw"), c.sem(kind="hw")]
    sst = [c.sem(kind="hw"), c.sem(kind="hw")]
    sp = c.sem()
    sv = c.sem()
    sc = c.sem()
    sdone = c.sem()

    sx[0].inc(sy.dma_start(out=xin[0][:], in_=Xin[0]), 16)
    if ntiles > 1:
        sx[1].inc(sy.dma_start(out=xin[1][:], in_=Xin[1]), 16)

    for ti in range(ntiles):
        xb = xin[ti % 2]
        ob = xout[ti % 2]
        po = ps_o[ti % 2]
        scratch = scratch2[ti % 2]
        if ti == 0:
            for eng in (v, s):
                eng.wait_ge(sx[0].h, sx[0].base + 16)
            emit_ln_stats(c, xb[:], sc, sv, scratch)
        sc.wait(v)
        sv.inc(v.tensor_scalar(xn[:], xb[:], scratch["c2"][:], scratch["c3"][:],
                               ALU.subtract, ALU.mult))
        sc.wait(t)
        if ti == 0:
            sg.wait(t)
            sw.wait(t)
        sv.wait(t)
        emit_transpose_tiles(c, xn, xnT, ident, ps_tr, sp, sv, 8)
        sv.wait(t)
        if ti >= 2:
            t.wait_ge(sdone.h, sdone.base + ti - 1)
        for k in range(8):
            lhs = xnT[:, k * 128:(k + 1) * 128]
            t.matmul(po[:, 0:512], lhs, bwt[:, k, 0:512], start=(k == 0), stop=False)
            t.matmul(po[:, 512:1024], lhs, bwt[:, k, 512:1024], start=(k == 0), stop=False)
            sp.inc(t.matmul(ps_h[:, 0:118], lhs, catT[:, k, :], start=(k == 0), stop=False))
        t.matmul(po[:, 0:512], onesr[:], baseb[:, 0:512], start=False, stop=False)
        t.matmul(po[:, 512:1024], onesr[:], baseb[:, 512:1024], start=False, stop=False)
        sp.inc(t.matmul(ps_h[:, 112:118], onesr[:], routerb[:], start=False, stop=True))
        if ti + 1 < ntiles:
            nb_ = (ti + 1) % 2
            for eng in (v, s):
                eng.wait_ge(sx[nb_].h, sx[nb_].base + 16 * ((ti + 1) // 2 + 1))
            emit_ln_stats(c, xin[nb_][:], sc, sv, scratch2[nb_])
        sp.wait(v)
        sv.inc(v.tensor_reduce(scratch["c0"][:], ps_h[:, 112:118], AX.X, ALU.max))
        sv.wait(v)
        sv.inc(v.tensor_scalar(scratch["c1"][:], scratch["c0"][:], -1.0, None, ALU.mult))
        sv.wait(s)
        sp.wait(s)
        sc.inc(s.activation(gate[:], ps_h[:, 112:118], AF.Exp, bias=scratch["c1"][:]))
        sc.wait(v)
        sv.inc(v.tensor_reduce(scratch["c0"][:], gate[:], AX.X, ALU.add))
        sv.wait(v)
        sv.inc(v.reciprocal(scratch["c1"][:], scratch["c0"][:]))
        sv.wait(v)
        sv.inc(v.tensor_scalar(gate[:], gate[:], scratch["c1"][:], None, ALU.mult))
        sv.wait(v)
        sv.inc(v.tensor_reduce(scratch["c0"][:], gate[:], AX.X, ALU.max))
        sv.wait(v)
        sv.inc(v.tensor_scalar(gw[:], gate[:], scratch["c0"][:], None, ALU.is_ge))
        sv.wait(v)
        sv.inc(v.scalar_tensor_tensor(gw[:], gw[:], -1e30, gate[:], ALU.mult, ALU.add))
        sv.wait(v)
        sv.inc(v.tensor_reduce(scratch["c1"][:], gw[:], AX.X, ALU.max))
        sv.wait(v)
        sv.inc(v.tensor_scalar(gw[:], gate[:], scratch["c1"][:], None, ALU.is_ge))
        sv.wait(v)
        sv.inc(v.tensor_tensor(gw[:], gw[:], gate[:], ALU.mult))
        sv.wait(v)
        sv.inc(v.tensor_reduce(scratch["c2"][:], gw[:], AX.X, ALU.add))
        sv.wait(v)
        sv.inc(v.tensor_scalar(scratch["c2"][:], scratch["c2"][:], 1e-6, None, ALU.add))
        sv.wait(v)
        sv.inc(v.reciprocal(scratch["c2"][:], scratch["c2"][:]))
        sv.wait(v)
        sv.inc(v.tensor_scalar(gw[:], gw[:], scratch["c2"][:], None, ALU.mult))
        sv.wait(v)
        sv.inc(v.tensor_copy(Hw[:, 0:16], ps_h[:, 0:16]))
        for e in range(6):
            sv.inc(v.tensor_scalar(Hw[:, 16 + 16 * e:32 + 16 * e],
                                   ps_h[:, 16 + 16 * e:32 + 16 * e],
                                   gw[:, e:e + 1], None, ALU.mult))
        sv.wait(t)
        sp.inc(t.transpose(ps_t2[0:112, :], Hw[:], ident[:]))
        sp.wait(v)
        sv.inc(v.tensor_copy(HwT[:], ps_t2[0:112, :]))
        sv.wait(t)
        sp.inc(t.matmul(po[:, 0:512], HwT[:], catB[:, 0:512], start=False, stop=True))
        sp.inc(t.matmul(po[:, 512:1024], HwT[:], catB[:, 512:1024], start=False, stop=True))
        sp.wait(s)
        sc.inc(s.activation(delta[:], po[:], AF.Copy, scale=0.5))
        sc.wait(v)
        if ti >= 2:
            v.wait_ge(sst[ti % 2].h, sst[ti % 2].base + 16 * (ti // 2))
        sdone.inc(v.tensor_add(ob[:], delta[:], xb[:]))
        sy.wait_ge(sdone.h, sdone.base + ti + 1)
        sst[ti % 2].inc(sy.dma_start(out=Xout[ti], in_=ob[:]), 16)
        if ti + 2 < ntiles:
            sy.wait_ge(sdone.h, sdone.base + ti + 1)
            sx[ti % 2].inc(sy.dma_start(out=xin[ti % 2][:], in_=Xin[ti + 2]), 16)
    return dict(sst=sst, n=ntiles, wstate=wstate)

def _prep_weights_A(inputs):
    baseWT = np.ascontiguousarray(inputs["L0_ffn_base_W"].T)
    baseb = np.ascontiguousarray(inputs["L0_ffn_base_b"][None, :])
    routedA = inputs["L0_ffn_routed_A"].reshape(96, 1024)
    cat = np.concatenate([inputs["L0_ffn_shared_A"], routedA, inputs["L0_ffn_router_W"]], 0)
    catT = np.ascontiguousarray(cat.T)
    routedB = np.ascontiguousarray(np.transpose(inputs["L0_ffn_routed_B"], (0, 2, 1))).reshape(96, 1024)
    catB = np.ascontiguousarray(np.concatenate([inputs["L0_ffn_shared_B"].T, routedB], 0))
    routerb = np.ascontiguousarray(inputs["L0_ffn_router_b"][None, :])
    return dict(baseWT=baseWT, baseb=baseb, catT=catT, catB=catB, routerb=routerb)

def shard_stream(x):
    shards = []
    for cid in range(NCORES):
        b, h = cid // 2, cid % 2
        main = x[b, h * NTOK:(h + 1) * NTOK].reshape(NT, 128, D)
        halo = np.zeros((1, 128, D), x.dtype)
        if h == 1:
            halo[0, 0:3] = x[b, NTOK - 3:NTOK]
        shards.append(np.ascontiguousarray(np.concatenate([main, halo], 0)))
    return shards

A_W_SHAPES = dict(baseWT=(1024, 1024), baseb=(1, 1024), catT=(1024, 118),
                  catB=(112, 1024), routerb=(1, 6))

def build_A(ntiles=NT + 1):
    nc = bass.Bass("TRN2", target_bir_lowering=False)
    X = nc.dram_tensor("X", [ntiles, 128, D], F32, kind="ExternalInput").ap()
    XO = nc.dram_tensor("XO", [ntiles, 128, D], F32, kind="ExternalOutput").ap()
    Wap = {k: nc.dram_tensor("W_" + k, list(s), F32, kind="ExternalInput").ap()
           for k, s in A_W_SHAPES.items()}
    es = ExitStack()
    c = Ctx(nc, es)
    r = emit_phase_L0FFN(c, X, XO, Wap, ntiles, "A")
    for smm in r["sst"]:
        smm.wait(c.sy)
    es.close()
    return nc



B_W_SHAPES = dict(
    ipT=(1024, 4384),
    opT=(2048, 1024),
    lnw=(1, 1024), lnb=(1, 1024),
    convw=(2304, 4), convb=(128, 18),
    dtbias=(1, 32), alog=(1, 32), dvec=(1, 32),
    normw=(1, 2048),
    hmask=(128, 8),
    halom=(128, 1),
)
NCH = 32
B_SCRATCH = dict(u0=([33, 128, 1024], F32), xnT=([33, 128, 1024], BF16),
                 raw=([18, 128, 4099], F32), gate=([32, 128, 2048], F32),
                 act=([18, 128, 4096], BF16), y=([32, 128, 2048], F32),
                 ccin=([128, 2048], F32), ccout=([1024, 2048], F32))

def emit_phase_mamba(c, Xin, Xout, W, Ds, name="B", hgather=None, want_u0=True):
    nc = c.nc
    t, v, s, g, sy = c.t, c.v, c.s, c.g, c.sy
    u0d, xntd, rawd, gated, actd, yd = (Ds["u0"], Ds["xnT"], Ds["raw"],
                                        Ds["gate"], Ds["act"], Ds["y"])
    ccin, ccout = Ds["ccin"], Ds["ccout"]

    c.const(EPS_LN); c.const(1.0)
    c.sconst.wait(s)

    es_ip = ExitStack(); _sv = c.es; c.es = es_ip
    ipT = c.sb(name + "_ipT", [128, 8, 4384], BF16, side="right")
    c.es = _sv
    es_b0 = ExitStack(); _sv = c.es; c.es = es_b0
    onesr_dummy = None
    c.es = _sv
    onesr = c.sb(name + "_ones", [1, 128], BF16)
    onesrf = c.sb(name + "_onesf", [1, 128], F32)
    onescf = c.sb(name + "_onesc", [128, 1], F32)
    ident = c.sb(name + "_id", [128, 128], BF16)
    identf = c.sb(name + "_idf", [128, 128], F32)
    lmask = c.sb(name + "_lm", [128, 128], F32)
    _sv = c.es; c.es = es_b0
    lnw_bc = c.sb(name + "_lnw", [128, 1024], F32, side="right")
    lnb_bc = c.sb(name + "_lnb", [128, 1024], F32, side="right")
    rowtmp = c.sb(name + "_rowt", [1, 4224], F32, side="right")
    c.es = _sv
    normw_bc = c.sb(name + "_nw", [128, 2048], F32)
    dtb_bc = c.sb(name + "_dtb", [128, 32], F32)
    negA_bc = c.sb(name + "_negA", [128, 32], F32)
    D_bc = c.sb(name + "_Dbc", [128, 32], F32)
    convw_sb = c.sb(name + "_cw", [128, 18, 4], F32)
    convb_sb = c.sb(name + "_cb", [128, 18], F32)
    hmask_sb = c.sb(name + "_hm", [128, 8], F32)
    halom_sb = c.sb(name + "_ham", [128, 1], F32)
    dt_all = c.sb(name + "_dt", [128, NCH * 32], F32)
    s_all = c.sb(name + "_sa", [128, NCH * 32], F32)
    atot = c.sb(name + "_at", [1, NCH * 32], F32)
    pfx = c.sb(name + "_pfx", [1, NCH * 32], F32)
    H = [c.sb(name + f"_H{i}", [128, 2048], F32) for i in range(2)]
    Hb16 = c.sb(name + "_Hb", [128, 2048], BF16)
    Hinitb = c.sb(name + "_Hib", [128, 2048], BF16)

    ps_bc = c.ps(name + "_psbc", [128, 512], F32)
    ps_r = [c.ps(name + f"_psr{i}", [128, 512], F32) for i in range(2)]
    ps_tr = c.ps(name + "_pstr", [128, 1024], BF16)
    ps_big = c.ps(name + "_psbig", [128, 2048], F32)

    sg = c.sem(); sp = c.sem(); sv = c.sem(); sc = c.sem()
    sw = c.sem(kind="sw")
    sq = [c.sem(kind="hw") for _ in range(2)]
    so = [c.sem(kind="sw") for _ in range(4)]
    scc = c.sem(kind="cc")

    sw.inc(g.dma_start(out=ipT[:], in_=W["ipT"].rearrange("(a p) n -> p a n", p=128)), 16)
    sw.inc(g.dma_start(out=convw_sb[:], in_=W["convw"].rearrange("(m p) j -> p m j", p=128)), 16)
    sw.inc(g.dma_start(out=convb_sb[:], in_=W["convb"]), 16)
    sw.inc(g.dma_start(out=hmask_sb[:], in_=W["hmask"]), 16)
    sw.inc(g.dma_start(out=halom_sb[:], in_=W["halom"]), 16)
    sw.inc(g.dma_start(out=rowtmp[0:1, 0:1024], in_=W["lnw"]), 16)
    sw.inc(g.dma_start(out=rowtmp[0:1, 1024:2048], in_=W["lnb"]), 16)
    sw.inc(g.dma_start(out=rowtmp[0:1, 2048:2080], in_=W["dtbias"]), 16)
    sw.inc(g.dma_start(out=rowtmp[0:1, 2080:2112], in_=W["alog"]), 16)
    sw.inc(g.dma_start(out=rowtmp[0:1, 2112:2144], in_=W["dvec"]), 16)
    sw.inc(g.dma_start(out=rowtmp[0:1, 2144:4192], in_=W["normw"]), 16)
    sg.inc(g.memset(onesr[:], 1.0))
    sg.inc(g.memset(onesrf[:], 1.0))
    sg.inc(g.memset(onescf[:], 1.0))
    emit_identity(c, ident, sg)
    emit_identity(c, identf, sg)
    sg.inc(g.memset(lmask[:], 1.0))
    sg.wait(g)
    sg.inc(g.affine_select(out=lmask[:], in_=lmask[:], compare_op=ALU.is_ge,
                           fill=0.0, base=0, pattern=[[1, 128]], channel_multiplier=-1))
    sg.inc(g.memset(pfx[0:1, 0:32], 0.0))
    if hgather is None:
        sg.inc(g.memset(Hb16[:], 0.0))
        sg.inc(g.memset(H[1][:], 0.0))
    else:
        sw.wait(v)
        es_h = ExitStack(); _svh = c.es; c.es = es_h
        hsl0 = _Ring(c, name + "hg", 2, [128, 2048], F32, kind="hw")
        c.es = _svh
        sv0 = c.sem()
        for j in range(8):
            if j >= 2:
                sy.wait_ge(sv0.h, sv0.n)
            hsl0.dma(j, sy.dma_start(out=hsl0.buf(j)[:], in_=hgather[j]))
            hsl0.wait_loaded(v, j)
            dst, src_ = (H[1], H[0]) if j % 2 == 0 else (H[0], H[1])
            if j == 0:
                sv0.inc(v.tensor_scalar(H[1][:], hsl0.buf(0)[:], hmask_sb[:, 0:1], None, ALU.mult))
            else:
                sv0.wait(v)
                sv0.inc(v.scalar_tensor_tensor(dst[:], hsl0.buf(j)[:], hmask_sb[:, j:j + 1],
                                               src_[:], ALU.mult, ALU.add))
        sv0.wait(v)
        sv0.inc(v.tensor_copy(H[1][:], H[0][:]))
        sv0.wait(v)
        sv0.inc(v.tensor_copy(Hb16[:], H[1][:]))
        sv0.wait(t); sv0.wait(s); sv0.wait(g); sv0.wait(sy)
        es_h.close()

    sw.wait(s)
    sc.inc(s.activation(rowtmp[0:1, 2080:2112], rowtmp[0:1, 2080:2112], AF.Exp))

    def bcast(dst, col0, ncols, scale=None):
        for j0 in range(0, ncols, 512):
            w_ = min(512, ncols - j0)
            sv.wait(t); sc.wait(t)
            sp.inc(t.matmul(ps_bc[:, 0:w_], onesrf[:], rowtmp[0:1, col0 + j0:col0 + j0 + w_],
                            start=True, stop=True))
            sp.wait(v)
            if scale is None:
                sv.inc(v.tensor_copy(dst[:, j0:j0 + w_], ps_bc[:, 0:w_]))
            else:
                sv.inc(v.tensor_scalar(dst[:, j0:j0 + w_], ps_bc[:, 0:w_], scale, None, ALU.mult))
    sw.wait(t); sg.wait(t); sw.wait(v)
    bcast(lnw_bc, 0, 1024)
    bcast(lnb_bc, 1024, 1024)
    bcast(dtb_bc, 2048, 32)
    bcast(negA_bc, 2080, 32, scale=-1.0)
    bcast(D_bc, 2112, 32)
    bcast(normw_bc, 2144, 2048)

    scratch = {k: c.sb(name + "_" + k, [128, 2048] if k == "sq" else [128, 1], F32)
               for k in ("sq", "c0", "c1", "c2", "c3", "c4", "c5")}

    from contextlib import ExitStack as _ES
    es0 = _ES()
    c0ctx = c.es; c.es = es0
    xin = [c.sb(name + f"_b0x{i}", [128, 1024], F32) for i in range(2)]
    u0t = [c.sb(name + f"_b0u{i}", [128, 1024], F32) for i in range(2)]
    xc2 = c.sb(name + "_b0xc", [128, 1024], F32)
    xn1 = c.sb(name + "_b0n1", [128, 1024], F32)
    xnb = c.sb(name + "_b0xn", [128, 1024], BF16)
    xnT_t = [c.sb(name + f"_b0t{i}", [128, 1024], BF16) for i in range(2)]
    c.es = c0ctx

    for ti in range(33):
        bi = ti % 2
        if ti < 2:
            sq[bi].inc(sy.dma_start(out=xin[bi][:], in_=Xin[ti]), 16)
        v.wait_ge(sq[bi].h, sq[bi].base + 16 * (ti // 2 + 1))
        s.wait_ge(sq[bi].h, sq[bi].base + 16 * (ti // 2 + 1))
        emit_ln_f32(c, xin[bi][:], u0t[bi][:], sc, sv, scratch)
        if want_u0:
            sv.wait(g)
            so[bi].inc(g.dma_start(out=u0d[ti], in_=u0t[bi][:]), 16)
        emit_ln_stats(c, u0t[bi][:], sc, sv, scratch)
        sc.wait(v)
        sv.inc(v.tensor_scalar(xc2[:], u0t[bi][:], scratch["c2"][:], scratch["c3"][:],
                               ALU.subtract, ALU.mult))
        sv.wait(v)
        sv.inc(v.tensor_tensor(xn1[:], xc2[:], lnw_bc[:], ALU.mult))
        sv.wait(v)
        sv.inc(v.tensor_tensor(xnb[:], xn1[:], lnb_bc[:], ALU.add))
        sv.wait(t)
        emit_transpose_tiles(c, xnb, xnT_t[bi], ident, ps_tr, sp, sv, 8)
        sv.wait(g)
        so[2 + bi].inc(g.dma_start(out=xntd[ti], in_=xnT_t[bi][:]), 16)
        if ti + 2 < 33:
            sy.wait_ge(sv.h, sv.n)
            sy.wait_ge(so[bi].h, so[bi].n)
            sy.wait_ge(so[2 + bi].h, so[2 + bi].n)
            sq[bi].inc(sy.dma_start(out=xin[bi][:], in_=Xin[ti + 2]), 16)
    es0.close()
    env = dict(ipT=ipT, onesr=onesr, onesrf=onesrf, onescf=onescf,
               ident=ident, identf=identf, lmask=lmask,
               lnw_bc=lnw_bc, lnb_bc=lnb_bc, normw_bc=normw_bc,
               dtb_bc=dtb_bc, negA_bc=negA_bc, D_bc=D_bc,
               convw_sb=convw_sb, convb_sb=convb_sb, hmask_sb=hmask_sb,
               halom_sb=halom_sb, dt_all=dt_all, s_all=s_all, atot=atot, pfx=pfx,
               H=H, Hb16=Hb16, Hinitb=Hinitb, Wop=W["opT"],
               ps_bc=ps_bc, ps_r=ps_r, ps_tr=ps_tr, ps_big=ps_big,
               sw=sw, sg=sg, sp=sp, sv=sv, sc=sc, scc=scc,
               xntd=xntd, rawd=rawd, gated=gated, actd=actd, yd=yd, u0d=u0d,
               ccin=ccin, ccout=ccout, es_ip=es_ip, es_b0=es_b0,
               b0sems=list(sq) + list(so))
    return env

class _Ring:
    def __init__(self, c, name, n, shape, dtype, kind="hw"):
        self.c = c
        self.bufs = [c.sb(f"{name}{i}", shape, dtype) for i in range(n)]
        self.sems = [c.sem(kind=kind) for _ in range(n)]
        self.cnt = [0] * n
        self.n = n
    def buf(self, i):
        return self.bufs[i % self.n]
    def sem(self, i):
        return self.sems[i % self.n]
    def wait_loaded(self, eng, i):
        sm = self.sems[i % self.n]
        eng.wait_ge(sm.h, sm.base + 16 * (i // self.n + 1))
    def dma(self, i, instr):
        self.sems[i % self.n].inc(instr, 16)

def _emit_mamba_b1(c, env, want_z=True):
    t, v, s, g, sy = c.t, c.v, c.s, c.g, c.sy
    ipT, onesr = env["ipT"], env["onesr"]
    ps_big, ps_bc = env["ps_big"], env["ps_bc"]
    sp, sv, sc = env["sp"], env["sv"], env["sc"]
    xntd, rawd, gated = env["xntd"], env["rawd"], env["gated"]
    dt_all = env["dt_all"]
    es = ExitStack(); sv_es = c.es; c.es = es
    xg = _Ring(c, "b1xg", 2, [128, 4, 1024], BF16, kind="hw")
    stg = _Ring(c, "b1st", 4, [128, 512], F32, kind="sw")
    xh = c.sb("b1xh", [128, 1024], BF16)
    c.es = sv_es

    sqh = c.sem(kind="hw")
    sqh.inc(sy.dma_start(out=xh[:], in_=xntd[32]), 16)
    sqh.wait(t)
    hst = stg.buf(0)
    nmm = 0
    for m in range(18):
        pb = ps_big[:, 0:512] if (m % 2 == 0) else ps_big[:, 512:1024]
        if m >= 2:
            t.wait_ge(sv.h, sv.n)
        for k in range(8):
            ins = t.matmul(pb[:, 0:128], ipT[:, k, 2048 + 128 * m:2048 + 128 * (m + 1)],
                           xh[:, k * 128:(k + 1) * 128],
                           start=(k == 0), stop=(k == 7))
        sp.inc(ins)
        sp.wait(v)
        sv.inc(v.tensor_copy(hst[:, m * 3:m * 3 + 3], pb[:, 0:3]))
    sv.wait(g)
    stg.dma(0, g.dma_start(out=rawd[:, :, 0:3].rearrange("m p n -> p m n"),
                           in_=hst[:, 0:54].rearrange("p (m n) -> p m n", n=3)))

    si = 1
    for gi in range(8):
        bi = gi % 2
        if gi < 2:
            xg.dma(gi, sy.dma_start(out=xg.buf(gi)[:],
                                    in_=xntd[4 * gi:4 * gi + 4].rearrange("t p n -> p t n")))
        xg.wait_loaded(t, gi)
        for m in range(18):
            pb = ps_big[:, 0:512] if (m % 2 == 0) else ps_big[:, 512:1024]
            t.wait_ge(sv.h, sv.n)
            for k in range(8):
                ins = t.matmul(pb[:], ipT[:, k, 2048 + 128 * m:2048 + 128 * (m + 1)],
                               xg.buf(gi)[:, :, k * 128:(k + 1) * 128],
                               start=(k == 0), stop=(k == 7))
            sp.inc(ins)
            sp.wait(v)
            sb_ = stg.buf(si); sm = stg.sem(si)
            v.wait_ge(sm.h, sm.base + 16 * (si // 4))
            sv.inc(v.tensor_copy(sb_[:], pb[:]))
            sv.wait(g)
            stg.dma(si, g.dma_start(out=rawd[m, :, 3 + 512 * gi:3 + 512 * (gi + 1)], in_=sb_[:]))
            si += 1
        for tl in range(4 if want_z else 0):
            for nb in range(4):
                pz = ps_big[:, 1024:1536] if ((tl * 4 + nb) % 2 == 0) else ps_big[:, 1536:2048]
                t.wait_ge(sc.h, sc.n)
                for k in range(8):
                    ins = t.matmul(pz[:], xg.buf(gi)[:, tl, k * 128:(k + 1) * 128],
                                   ipT[:, k, 512 * nb:512 * (nb + 1)],
                                   start=(k == 0), stop=(k == 7))
                sp.inc(ins)
                sp.wait(s)
                sb_ = stg.buf(si); sm = stg.sem(si)
                s.wait_ge(sm.h, sm.base + 16 * (si // 4))
                sc.inc(s.activation(sb_[:], pz[:], AF.Silu))
                sc.wait(g)
                stg.dma(si, g.dma_start(out=gated[4 * gi + tl, :, 512 * nb:512 * (nb + 1)], in_=sb_[:]))
                si += 1
        for tl in range(4):
            t.wait_ge(sv.h, sv.n)
            for k in range(8):
                ins = t.matmul(ps_bc[:, 192:224], xg.buf(gi)[:, tl, k * 128:(k + 1) * 128],
                               ipT[:, k, 4352:4384], start=(k == 0), stop=(k == 7))
            sp.inc(ins)
            sp.wait(v)
            ti_ = 4 * gi + tl
            sv.inc(v.tensor_copy(dt_all[:, 32 * ti_:32 * (ti_ + 1)], ps_bc[:, 192:224]))
        if gi + 2 < 8:
            sy.wait_ge(sp.h, sp.n)
            xg.dma(gi + 2, sy.dma_start(out=xg.buf(gi)[:],
                                        in_=xntd[4 * (gi + 2):4 * (gi + 2) + 4].rearrange("t p n -> p t n")))
    for i in range(4):
        stg.sems[i].wait(g)
    es.close()
    return list(xg.sems) + list(stg.sems) + [sqh]

def _emit_mamba_conv(c, env):
    t, v, s, g, sy = c.t, c.v, c.s, c.g, c.sy
    sp, sv, sc = env["sp"], env["sv"], env["sc"]
    rawd, actd = env["rawd"], env["actd"]
    convw_sb, convb_sb, halom_sb = env["convw_sb"], env["convb_sb"], env["halom_sb"]
    es = ExitStack(); sv_es = c.es; c.es = es
    ext = _Ring(c, "cvext", 2, [128, 18, 515], F32, kind="hw")
    ast = _Ring(c, "cvast", 2, [128, 18 * 512], BF16, kind="sw")
    acc = [c.sb(f"cvacc{i}", [128, 512], F32) for i in range(2)]
    c.es = sv_es
    NCG = NCH // 4
    for cg in range(NCG):
        bi = cg % 2
        if cg < 2:
            ext.dma(cg, sy.dma_start(out=ext.buf(cg)[:],
                                     in_=rawd[:, :, 512 * cg:512 * cg + 515].rearrange("m p n -> p m n")))
        ext.wait_loaded(v, cg)
        if cg == 0:
            sv.inc(v.tensor_scalar(ext.buf(0)[:, :, 0:3], ext.buf(0)[:, :, 0:3],
                                   halom_sb[:, 0:1], None, ALU.mult))
            sv.wait(v)
        for m in range(18):
            sv.wait(v); sc.wait(v)
            sv.inc(v.tensor_scalar(acc[0][:], ext.buf(cg)[:, m, 0:512],
                                   convw_sb[:, m, 0:1], None, ALU.mult))
            sv.wait(v)
            sv.inc(v.scalar_tensor_tensor(acc[1][:], ext.buf(cg)[:, m, 1:513],
                                          convw_sb[:, m, 1:2], acc[0][:], ALU.mult, ALU.add))
            sv.wait(v)
            sv.inc(v.scalar_tensor_tensor(acc[0][:], ext.buf(cg)[:, m, 2:514],
                                          convw_sb[:, m, 2:3], acc[1][:], ALU.mult, ALU.add))
            sv.wait(v)
            sv.inc(v.scalar_tensor_tensor(acc[1][:], ext.buf(cg)[:, m, 3:515],
                                          convw_sb[:, m, 3:4], acc[0][:], ALU.mult, ALU.add))
            sv.wait(s)
            ab = ast.buf(cg)
            if m == 0:
                sm_ = ast.sem(cg)
                s.wait_ge(sm_.h, sm_.base + 16 * (cg // 2))
            sc.inc(s.activation(ab[:, m * 512:(m + 1) * 512], acc[1][:], AF.Silu,
                                bias=convb_sb[:, m:m + 1]))
        sc.wait(g)
        ast.dma(cg, g.dma_start(out=actd[:, :, 512 * cg:512 * (cg + 1)].rearrange("m p n -> p m n"),
                                in_=ast.buf(cg)[:].rearrange("p (m n) -> p m n", n=512)))
        if cg + 2 < NCG:
            sy.wait_ge(sc.h, sc.n)
            sy.wait_ge(sv.h, sv.n)
            ext.dma(cg + 2, sy.dma_start(out=ext.buf(cg)[:],
                                         in_=rawd[:, :, 512 * (cg + 2):512 * (cg + 2) + 515].rearrange("m p n -> p m n")))
    for i in range(2):
        ast.sems[i].wait(g)
    es.close()
    return list(ext.sems) + list(ast.sems)

def _emit_mamba_dt(c, env):
    t, v, s, g = c.t, c.v, c.s, c.g
    sp, sv, sc = env["sp"], env["sv"], env["sc"]
    dt_all, s_all, atot, pfx = env["dt_all"], env["s_all"], env["atot"], env["pfx"]
    dtb_bc, negA_bc, lmask = env["dtb_bc"], env["negA_bc"], env["lmask"]
    onescf, ps_bc = env["onescf"], env["ps_bc"]
    es = ExitStack(); sv_es = c.es; c.es = es
    ta = c.sb("dtta", [128, 32], F32)
    tb = c.sb("dttb", [128, 32], F32)
    tc_ = c.sb("dttc", [128, 32], F32)
    td = c.sb("dttd", [128, 32], F32)
    c.es = sv_es
    for cc in range(NCH):
        sl = slice(32 * cc, 32 * (cc + 1))
        sv.wait(v); sc.wait(v)
        sv.inc(v.tensor_tensor(ta[:], dt_all[:, sl], dtb_bc[:], ALU.add))
        sv.wait(s)
        sc.inc(s.activation(tb[:], ta[:], AF.Abs))
        sc.wait(s)
        sc.inc(s.activation(tc_[:], tb[:], AF.Exp, scale=-1.0))
        sc.wait(s)
        sc.inc(s.activation(tb[:], tc_[:], AF.Ln, bias=1.0))
        sc.wait(s)
        sc.inc(s.activation(tc_[:], ta[:], AF.Relu))
        sc.wait(v)
        sv.inc(v.tensor_tensor(dt_all[:, sl], tb[:], tc_[:], ALU.add))
        sv.wait(v)
        sv.inc(v.tensor_tensor(td[:], dt_all[:, sl], negA_bc[:], ALU.mult))
        sv.wait(t)
        sp.inc(t.matmul(ps_bc[:, 224:256], lmask[:], td[:], start=True, stop=True))
        sp.inc(t.matmul(ps_bc[0:1, 256:288], onescf[:], td[:], start=True, stop=True))
        sp.wait(v)
        sv.inc(v.tensor_copy(s_all[:, sl], ps_bc[:, 224:256]))
        sv.inc(v.tensor_copy(atot[0:1, sl], ps_bc[0:1, 256:288]))
        sv.wait(v)
        if cc + 1 < NCH:
            sv.inc(v.tensor_tensor(pfx[0:1, 32 * (cc + 1):32 * (cc + 2)],
                                   pfx[0:1, sl], atot[0:1, sl], ALU.add))
    es.close()
    return []


def _scan_state_update(c, env, cc, sl, L):
    t, v, s = c.t, c.v, c.s
    sp, sv, sc = env["sp"], env["sv"], env["sc"]
    ps_bc, ps_big = env["ps_bc"], env["ps_big"]
    onesrf = env["onesrf"]
    drow, dfac, b_tm, xtil, Htmp = L["drow"], L["dfac"], L["b_tm"], L["xtil"], L["Htmp"]
    hprev, hcur = L["hprev"], L["hcur"]
    Hb16 = env["Hb16"]
    sc.wait(t)
    sp.inc(t.matmul(ps_bc[:, 128:160], onesrf[:], drow[:], start=True, stop=True))
    sp.wait(v)
    sv.inc(v.tensor_copy(dfac[:], ps_bc[:, 128:160]))
    sv.wait(t)
    for j in range(4):
        sp.inc(t.matmul(ps_big[:, 512 * j:512 * (j + 1)], b_tm[:],
                        xtil[:, 512 * j:512 * (j + 1)], start=True, stop=True))
    sp.wait(v)
    sv.inc(v.tensor_tensor(Htmp[:].rearrange("p (h d) -> p h d", d=64),
                           hprev[:].rearrange("p (h d) -> p h d", d=64),
                           dfac[:].unsqueeze(2).broadcast_to([128, 32, 64]), ALU.mult))
    sv.wait(v)
    sv.inc(v.tensor_add(hcur[:], Htmp[:], ps_big[:]))
    sv.wait(v)
    sv.inc(v.tensor_copy(Hb16[:], hcur[:]))

def _emit_mamba_scan(c, env, want_y=True):
    t, v, s, g, sy = c.t, c.v, c.s, c.g, c.sy
    sp, sv, sc = env["sp"], env["sv"], env["sc"]
    actd, yd = env["actd"], env["yd"]
    s_all, dt_all, atot = env["s_all"], env["dt_all"], env["atot"]
    lmask, ident, identf = env["lmask"], env["ident"], env["identf"]
    onesrf, D_bc = env["onesrf"], env["D_bc"]
    H, Hb16 = env["H"], env["Hb16"]
    ps_bc, ps_r, ps_tr, ps_big = env["ps_bc"], env["ps_r"], env["ps_tr"], env["ps_big"]

    es = ExitStack(); sv_es = c.es; c.es = es
    actl = _Ring(c, "b3act", 2, [128, 18, 128], BF16, kind="hw")
    yst = _Ring(c, "b3yst", 2, [128, 2048], F32, kind="sw")
    x_tm = c.sb("b3xtm", [128, 2048], BF16)
    b_tm = c.sb("b3btm", [128, 128], BF16)
    xtil = c.sb("b3xtl", [128, 2048], BF16)
    gg = c.sb("b3g", [128, 128], F32)
    drow = c.sb("b3dr", [1, 32], F32)
    colb4 = [c.sb(f"b3cb{i}", [128, 512], F32) for i in range(2)]
    onesbig = c.sb("b3ob", [128, 512], F32)
    Es = c.sb("b3es", [128, 32], F32)
    e2 = c.sb("b3e2", [128, 32], F32)
    dfac = c.sb("b3df", [128, 32], F32)
    ehead = [c.sb(f"b3eh{i}", [128, 512], F32) for i in range(2)]
    ehead2 = [c.sb(f"b3eh2{i}", [128, 512], F32) for i in range(2)]
    mhead = [c.sb(f"b3mh{i}", [128, 512], BF16) for i in range(2)]
    Htmp = c.sb("b3htmp", [128, 2048], F32)
    ya = c.sb("b3ya", [128, 2048], F32)
    tmp32 = c.sb("b3t32", [128, 32], F32)
    c.es = sv_es
    sgo = c.sem()
    sgo.inc(c.g.memset(onesbig[:], 1.0))
    sgo.wait(c.v)

    for cc in range(NCH):
        sl = slice(32 * cc, 32 * (cc + 1))
        hprev = H[(cc + 1) % 2]
        hcur = H[cc % 2]
        if cc < 2:
            actl.dma(cc, sy.dma_start(out=actl.buf(cc)[:],
                                      in_=actd[:, :, 128 * cc:128 * (cc + 1)].rearrange("m p n -> p m n")))
        A = actl.buf(cc)
        sv.wait(s); sp.wait(s)
        if want_y:
            sc.inc(s.activation(Es[:], s_all[:, sl], AF.Exp))
        sv.wait(t); sc.wait(t)
        sp.inc(t.matmul(ps_bc[:, 128:160], onesrf[:], atot[0:1, sl], start=True, stop=True))
        sp.wait(v)
        sv.inc(v.tensor_tensor(tmp32[:], ps_bc[:, 128:160], s_all[:, sl], ALU.subtract))
        sv.wait(s)
        sc.inc(s.activation(tmp32[:], tmp32[:], AF.Exp))
        sc.wait(v)
        sv.inc(v.tensor_tensor(e2[:], tmp32[:], dt_all[:, sl], ALU.mult))
        sc.inc(s.activation(drow[:], atot[0:1, sl], AF.Exp))
        actl.wait_loaded(t, cc)
        for r in range(2):
            sv.wait(t)
            for j in range(8):
                sp.inc(t.transpose(ps_tr[:, j * 128:(j + 1) * 128],
                                   A[:, 8 * r + j, :], ident[:]))
            sp.wait(v)
            sv.inc(v.tensor_copy(x_tm[:, 1024 * r:1024 * (r + 1)], ps_tr[:]))
        sv.wait(t)
        sp.inc(t.transpose(ps_tr[:, 0:128], A[:, 16, :], ident[:]))
        sp.wait(v)
        sv.inc(v.tensor_copy(b_tm[:], ps_tr[:, 0:128]))
        sv.wait(v)
        sv.inc(v.tensor_tensor(xtil[:].rearrange("p (h d) -> p h d", d=64),
                               x_tm[:].rearrange("p (h d) -> p h d", d=64),
                               e2[:].unsqueeze(2).broadcast_to([128, 32, 64]), ALU.mult))
        if not want_y:
            _scan_state_update(c, env, cc, sl, locals())
            if cc + 2 < NCH:
                sy.wait_ge(sp.h, sp.n)
                actl.dma(cc + 2, sy.dma_start(out=actl.buf(cc)[:],
                                              in_=actd[:, :, 128 * (cc + 2):128 * (cc + 3)].rearrange("m p n -> p m n")))
            continue
        sv.wait(t)
        sp.inc(t.matmul(ps_bc[:, 0:128], A[:, 16, :], A[:, 17, :], start=True, stop=True))
        sp.wait(v)
        sv.inc(v.tensor_tensor(gg[:], ps_bc[:, 0:128], lmask[:], ALU.mult))
        sv.wait(t)
        for j in range(4):
            sp.inc(t.matmul(ps_big[:, 512 * j:512 * (j + 1)], A[:, 17, :],
                            Hb16[:, 512 * j:512 * (j + 1)], start=True, stop=True))
        sp.wait(v)
        ybuf = yst.buf(cc)
        v.wait_ge(yst.sem(cc).h, yst.sem(cc).base + 16 * (cc // 2))
        sv.inc(v.tensor_tensor(ybuf[:].rearrange("p (h d) -> p h d", d=64),
                               ps_big[:].rearrange("p (h d) -> p h d", d=64),
                               Es[:].unsqueeze(2).broadcast_to([128, 32, 64]), ALU.mult))
        sv.wait(t)
        for grp in range(8):
            par = grp % 2
            pr = ps_r[par]
            hsl_ = slice(32 * cc + 4 * grp, 32 * cc + 4 * grp + 4)
            s_v = s_all[:, hsl_].unsqueeze(2).broadcast_to([128, 4, 128])
            dt_v = dt_all[:, hsl_].unsqueeze(2).broadcast_to([128, 4, 128])
            gg_v = gg[:].unsqueeze(1).broadcast_to([128, 4, 128])
            cb4 = colb4[par]
            eb, eb2, mb = ehead[par], ehead2[par], mhead[par]
            if grp >= 2:
                v.wait_ge(sp.h, sp.n)
            sv.inc(v.tensor_tensor(cb4[:].rearrange("p (l n) -> p l n", n=128),
                                   s_v, onesbig[:].rearrange("p (l n) -> p l n", n=128),
                                   ALU.mult))
            if grp >= 2:
                t.wait_ge(sv.h, sv.n)
            sv.wait(t)
            for l4 in range(4):
                sp.inc(t.transpose(pr[:, 128 * l4:128 * (l4 + 1)],
                                   cb4[:, 128 * l4:128 * (l4 + 1)], identf[:]))
            sp.wait(v)
            ebv = eb[:].rearrange("p (l n) -> p l n", n=128)
            eb2v = eb2[:].rearrange("p (l n) -> p l n", n=128)
            mbv = mb[:].rearrange("p (l n) -> p l n", n=128)
            sv.inc(v.tensor_tensor(ebv, pr[:].rearrange("p (l n) -> p l n", n=128),
                                   s_v, ALU.min))
            sv.wait(v)
            sv.inc(v.tensor_tensor(ebv, ebv, s_v, ALU.subtract))
            sv.wait(s)
            sc.inc(s.activation(eb2[:], eb[:], AF.Exp))
            sc.wait(v)
            sv.inc(v.tensor_tensor(ebv, eb2v, dt_v, ALU.mult))
            sv.wait(v)
            sv.inc(v.tensor_tensor(mbv, ebv, gg_v, ALU.mult))
            sv.wait(t)
            for l4 in range(4):
                h = 4 * grp + l4
                sp.inc(t.matmul(ps_big[:, 64 * h:64 * (h + 1)], mb[:, 128 * l4:128 * (l4 + 1)],
                                x_tm[:, 64 * h:64 * (h + 1)],
                                start=(h % 8 == 0), stop=(h % 8 == 7)))
        sp.wait(v)
        sv.inc(v.tensor_add(ya[:], ybuf[:], ps_big[:]))
        sv.wait(v)
        sv.inc(v.tensor_tensor(ybuf[:].rearrange("p (h d) -> p h d", d=64),
                               x_tm[:].rearrange("p (h d) -> p h d", d=64),
                               D_bc[:].unsqueeze(2).broadcast_to([128, 32, 64]), ALU.mult))
        sv.wait(v)
        sv.inc(v.tensor_add(ya[:], ya[:], ybuf[:]))
        sv.wait(v)
        sv.inc(v.tensor_copy(ybuf[:], ya[:]))
        sv.wait(g)
        yst.dma(cc, g.dma_start(out=yd[cc], in_=ybuf[:]))
        _scan_state_update(c, env, cc, sl, locals())
        if cc + 2 < NCH:
            sy.wait_ge(sp.h, sp.n)
            actl.dma(cc + 2, sy.dma_start(out=actl.buf(cc)[:],
                                          in_=actd[:, :, 128 * (cc + 2):128 * (cc + 3)].rearrange("m p n -> p m n")))
    for i in range(2):
        yst.sems[i].wait(g)
    es.close()
    return list(actl.sems) + list(yst.sems)

def _emit_mamba_hexchange(c, env):
    t, v, s, g, sy = c.t, c.v, c.s, c.g, c.sy
    sp, sv, sc = env["sp"], env["sv"], env["sc"]
    H, Hinitb, hmask_sb = env["H"], env["Hinitb"], env["hmask_sb"]
    ccin, ccout = env["ccin"], env["ccout"]
    hfinal = H[(NCH - 1) % 2]
    scc = env["scc"]
    sst_cc = c.sem(kind="sw")
    sv.wait(g)
    sst_cc.inc(g.dma_start(out=ccin[:], in_=hfinal[:]), 16)
    sst_cc.wait(g)
    scc.inc(g.collective_compute(
        "AllGather", ALU.bypass, replica_groups=[list(range(NCORES))],
        ins=[ccin.opt()], outs=[ccout.opt()]), 1)
    scc.wait(sy)
    es = ExitStack(); sv_es = c.es; c.es = es
    hsl = _Ring(c, "hx", 2, [128, 2048], F32, kind="hw")
    Hinit = [c.sb(f"hxhi{i}", [128, 2048], F32) for i in range(2)]
    c.es = sv_es
    for j in range(8):
        if j >= 2:
            sy.wait_ge(sv.h, sv.n)
        hsl.dma(j, sy.dma_start(out=hsl.buf(j)[:], in_=ccout[128 * j:128 * (j + 1), :]))
        hsl.wait_loaded(v, j)
        if j == 0:
            sv.inc(v.tensor_scalar(Hinit[0][:], hsl.buf(0)[:], hmask_sb[:, 0:1], None, ALU.mult))
        else:
            sv.wait(v)
            sv.inc(v.scalar_tensor_tensor(Hinit[j % 2][:], hsl.buf(j)[:], hmask_sb[:, j:j + 1],
                                          Hinit[(j + 1) % 2][:], ALU.mult, ALU.add))
    sv.wait(v)
    sv.inc(v.tensor_copy(Hinitb[:], Hinit[7 % 2][:]))
    es.close()
    return list(hsl.sems)

def _emit_mamba_corr(c, env):
    t, v, s, g, sy = c.t, c.v, c.s, c.g, c.sy
    sp, sv, sc = env["sp"], env["sv"], env["sc"]
    actd, yd = env["actd"], env["yd"]
    s_all, pfx, onesrf = env["s_all"], env["pfx"], env["onesrf"]
    Hinitb = env["Hinitb"]
    ps_bc, ps_big = env["ps_bc"], env["ps_big"]
    es = ExitStack(); sv_es = c.es; c.es = es
    cl = _Ring(c, "b3cc", 2, [128, 128], BF16, kind="hw")
    dst = _Ring(c, "b3cd", 2, [128, 2048], F32, kind="sw")
    es2 = c.sb("b3ce", [128, 32], F32)
    c.es = sv_es
    for cc in range(NCH):
        if cc < 2:
            cl.dma(cc, sy.dma_start(out=cl.buf(cc)[:],
                                    in_=actd[17, :, 128 * cc:128 * (cc + 1)]))
        sv.wait(t); sc.wait(t)
        sp.inc(t.matmul(ps_bc[:, 128:160], onesrf[:], pfx[0:1, 32 * cc:32 * (cc + 1)],
                        start=True, stop=True))
        sp.wait(v)
        sv.inc(v.tensor_tensor(es2[:], ps_bc[:, 128:160], s_all[:, 32 * cc:32 * (cc + 1)], ALU.add))
        sv.wait(s)
        sc.inc(s.activation(es2[:], es2[:], AF.Exp))
        cl.wait_loaded(t, cc)
        sc.wait(t)
        for j in range(4):
            sp.inc(t.matmul(ps_big[:, 512 * j:512 * (j + 1)], cl.buf(cc)[:],
                            Hinitb[:, 512 * j:512 * (j + 1)], start=True, stop=True))
        sp.wait(v); sc.wait(v)
        db = dst.buf(cc)
        v.wait_ge(dst.sem(cc).h, dst.sem(cc).base + 16 * (cc // 2))
        sv.inc(v.tensor_tensor(db[:].rearrange("p (h d) -> p h d", d=64),
                               ps_big[:].rearrange("p (h d) -> p h d", d=64),
                               es2[:].unsqueeze(2).broadcast_to([128, 32, 64]), ALU.mult))
        sv.wait(g)
        dst.dma(cc, g.dma_start(out=yd[cc], in_=db[:], accum_op=ALU.add))
        if cc + 2 < NCH:
            sy.wait_ge(sp.h, sp.n)
            cl.dma(cc + 2, sy.dma_start(out=cl.buf(cc)[:],
                                        in_=actd[17, :, 128 * (cc + 2):128 * (cc + 3)]))
    for i in range(2):
        dst.sems[i].wait(g)
    es.close()
    return list(cl.sems) + list(dst.sems)

def _emit_mamba_out(c, env, Xin, Xout):
    t, v, s, g, sy = c.t, c.v, c.s, c.g, c.sy
    sp, sv, sc = env["sp"], env["sv"], env["sc"]
    yd, gated, u0d = env["yd"], env["gated"], env["u0d"]
    normw_bc, ident = env["normw_bc"], env["ident"]
    ps_tr, ps_big = env["ps_tr"], env["ps_big"]
    es = ExitStack(); sv_es = c.es; c.es = es
    opT = c.sb("b4opT", [128, 16, 1024], BF16)
    swo = c.sem(kind="sw")
    swo.inc(g.dma_start(out=opT[:], in_=env["Wop"].rearrange("(a p) n -> p a n", p=128)), 16)
    swo.wait(t)
    yl = _Ring(c, "b4y", 2, [128, 2048], F32, kind="hw")
    gl = _Ring(c, "b4g", 2, [128, 2048], F32, kind="hw")
    ul = _Ring(c, "b4u", 2, [128, 1024], F32, kind="hw")
    xl = _Ring(c, "b4x", 2, [128, 1024], F32, kind="hw")
    y2 = c.sb("b4y2", [128, 2048], F32)
    y4 = c.sb("b4y4", [128, 2048], BF16)
    y_fm = c.sb("b4yfm", [128, 2048], BF16)
    xo = _Ring(c, "b4o", 2, [128, 1024], F32, kind="sw")
    scr = {k: c.sb("b4" + k, [128, 2048] if k == "sq" else [128, 1], F32)
           for k in ("sq", "c0", "c1", "c2", "c3")}
    c.es = sv_es
    for ti in range(NT):
        bi = ti % 2
        if ti < 2:
            yl.dma(ti, sy.dma_start(out=yl.buf(ti)[:], in_=yd[ti]))
            gl.dma(ti, sy.dma_start(out=gl.buf(ti)[:], in_=gated[ti]))
            ul.dma(ti, sy.dma_start(out=ul.buf(ti)[:], in_=u0d[ti]))
            xl.dma(ti, sy.dma_start(out=xl.buf(ti)[:], in_=Xin[ti]))
        for r_ in (yl, gl, ul, xl):
            r_.wait_loaded(v, ti)
        sv.wait(v)
        sv.inc(v.tensor_tensor(y2[:], yl.buf(ti)[:], gl.buf(ti)[:], ALU.mult))
        sv.wait(s)
        sc.inc(s.activation(scr["sq"][:], y2[:], AF.Square))
        sc.wait(v)
        sv.inc(v.tensor_reduce(scr["c0"][:], scr["sq"][:], AX.X, ALU.add))
        sv.wait(v)
        sv.inc(v.tensor_scalar(scr["c1"][:], scr["c0"][:], 1.0 / 2048, None, ALU.mult))
        sv.wait(s)
        sc.inc(s.activation(scr["c2"][:], scr["c1"][:], AF.Ln, bias=EPS_LN))
        sc.wait(s)
        sc.inc(s.activation(scr["c2"][:], scr["c2"][:], AF.Exp, scale=-0.5))
        sc.wait(v)
        sv.inc(v.tensor_scalar(y2[:], y2[:], scr["c2"][:], None, ALU.mult))
        sv.wait(v)
        sv.inc(v.tensor_tensor(y4[:], y2[:], normw_bc[:], ALU.mult))
        for r in range(2):
            sv.wait(t)
            for j in range(8):
                sp.inc(t.transpose(ps_tr[:, j * 128:(j + 1) * 128],
                                   y4[:, 1024 * r + 128 * j:1024 * r + 128 * (j + 1)], ident[:]))
            sp.wait(v)
            sv.inc(v.tensor_copy(y_fm[:, 1024 * r:1024 * (r + 1)], ps_tr[:]))
        sv.wait(t)
        if ti >= 1:
            t.wait_ge(sv.h, sv.n)
        for k in range(16):
            i1 = t.matmul(ps_big[:, 0:512], y_fm[:, 128 * k:128 * (k + 1)],
                          opT[:, k, 0:512], start=(k == 0), stop=(k == 15))
            i2 = t.matmul(ps_big[:, 512:1024], y_fm[:, 128 * k:128 * (k + 1)],
                          opT[:, k, 512:1024], start=(k == 0), stop=(k == 15))
        sp.inc(i1); sp.inc(i2)
        sp.wait(v)
        ob = xo.buf(ti)
        v.wait_ge(xo.sem(ti).h, xo.sem(ti).base + 16 * (ti // 2))
        sv.inc(v.tensor_add(y2[:, 0:1024], ps_big[:, 0:1024], ul.buf(ti)[:]))
        sv.wait(v)
        sv.inc(v.tensor_add(ob[:], y2[:, 0:1024], xl.buf(ti)[:]))
        sv.wait(g)
        xo.dma(ti, g.dma_start(out=Xout[ti], in_=ob[:]))
        if ti + 2 < NT:
            sy.wait_ge(sv.h, sv.n)
            yl.dma(ti + 2, sy.dma_start(out=yl.buf(ti)[:], in_=yd[ti + 2]))
            gl.dma(ti + 2, sy.dma_start(out=gl.buf(ti)[:], in_=gated[ti + 2]))
            ul.dma(ti + 2, sy.dma_start(out=ul.buf(ti)[:], in_=u0d[ti + 2]))
            xl.dma(ti + 2, sy.dma_start(out=xl.buf(ti)[:], in_=Xin[ti + 2]))
    for i in range(2):
        xo.sems[i].wait(g)
    es.close()
    return list(xo.sems) + list(yl.sems) + list(gl.sems) + list(ul.sems) + list(xl.sems)

def _barrier_all(c, sems):
    for eng in (c.t, c.v, c.s, c.g, c.sy):
        for sm in sems:
            sm.wait(eng)

def emit_phase_mamba_main(c, Xin, Xout, W, Ds, name="B", mode="cc",
                          hgather=None, hout=None):
    env = emit_phase_mamba(c, Xin, Xout, W, Ds, name, hgather=hgather,
                           want_u0=(mode != "pre"))
    base = [env["sp"], env["sv"], env["sc"], env["sw"], env["sg"], env["scc"]]
    _barrier_all(c, base + env["b0sems"])
    env["es_b0"].close()
    r = _emit_mamba_b1(c, env, want_z=(mode != "pre"))
    _barrier_all(c, base + r)
    env["es_ip"].close()
    r = _emit_mamba_conv(c, env)
    _barrier_all(c, base + r)
    r = _emit_mamba_dt(c, env)
    _barrier_all(c, base + r)
    r = _emit_mamba_scan(c, env, want_y=(mode != "pre"))
    _barrier_all(c, base + r)
    if mode == "pre":
        sfin = c.sem(kind="sw")
        sfin.inc(c.g.dma_start(out=hout, in_=env["H"][1][:]), 16)
        sfin.wait(c.g)
        return env
    if mode == "cc":
        r = _emit_mamba_hexchange(c, env)
        _barrier_all(c, base + r)
        r = _emit_mamba_corr(c, env)
        _barrier_all(c, base + r)
    r = _emit_mamba_out(c, env, Xin, Xout)
    _barrier_all(c, base + r)
    return env

def _prep_weights_B(inputs):
    return dict(
        ipT=np.ascontiguousarray(inputs["L0_attn_in_proj_w"].T),
        opT=np.ascontiguousarray(inputs["L0_attn_out_proj_w"].T),
        lnw=np.ascontiguousarray(inputs["L0_attn_ln_w"][None, :]),
        lnb=np.ascontiguousarray(inputs["L0_attn_ln_b"][None, :]),
        convw=np.ascontiguousarray(inputs["L0_attn_conv_w"]),
        convb=np.ascontiguousarray(inputs["L0_attn_conv_b"].reshape(18, 128).T),
        dtbias=np.ascontiguousarray(inputs["L0_attn_dt_bias"][None, :]),
        alog=np.ascontiguousarray(inputs["L0_attn_A_log"][None, :]),
        dvec=np.ascontiguousarray(inputs["L0_attn_D"][None, :]),
        normw=np.ascontiguousarray(inputs["L0_attn_norm_w"][None, :]),
    )

def _core_masks_B(cid):
    hmask = np.zeros((128, 8), np.float32)
    halom = np.zeros((128, 1), np.float32)
    if cid % 2 == 1:
        hmask[:, cid - 1] = 1.0
        halom[:] = 1.0
    return dict(hmask=hmask, halom=halom)

def build_B(mode="cc"):
    nc = bass.Bass("TRN2", target_bir_lowering=False)
    X = nc.dram_tensor("X", [33, 128, D], F32, kind="ExternalInput").ap()
    XO = nc.dram_tensor("XO", [33, 128, D], F32, kind="ExternalOutput").ap()
    Wap = {k: nc.dram_tensor("W_" + k, list(s), F32, kind="ExternalInput").ap()
           for k, s in B_W_SHAPES.items()}
    Ds = {k: nc.dram_tensor("S_" + k, list(sh), dt_, kind="Internal",
                            addr_space=("Shared" if k == "ccout" else "Local")).ap()
          for k, (sh, dt_) in B_SCRATCH.items()}
    hg = hout = None
    if mode == "pre":
        hout = nc.dram_tensor("HF", [128, 2048], F32, kind="ExternalOutput").ap()
    if mode == "full":
        hg = nc.dram_tensor("HG", [8, 128, 2048], F32, kind="ExternalInput").ap()
    es = ExitStack()
    c = Ctx(nc, es)
    emit_phase_mamba_main(c, X, XO, Wap, Ds, "B", mode=mode, hgather=hg, hout=hout)
    es.close()
    return nc

D_W_SHAPES = dict(w1T=(9, 1024, 2816), w3T=(9, 1024, 2816), w2T=(9, 2816, 1024),
                  gateT=(1024, 16), gbias=(1, 16))

def emit_phase_L1FFN(c, Xin, Xout, W, xntd, name, inplace=False):
    nc = c.nc
    t, v, s, g, sy = c.t, c.v, c.s, c.g, c.sy
    NE = 9

    gateT = c.sb(name + "_gT", [128, 8, 16], BF16)
    gbias = c.sb(name + "_gb", [1, 16], BF16)
    onesr = c.sb(name + "_ones", [1, 128], BF16)
    ident = c.sb(name + "_id", [128, 128], BF16)
    cw = c.sb(name + "_cw", [128, NT, 16], F32)
    sg = c.sem(); sp = c.sem(); sv = c.sem(); sc = c.sem()
    sw = c.sem(kind="sw")
    sacc = [c.sem(kind="sw"), c.sem(kind="sw")]

    c.const(EPS_LN)
    c.sconst.wait(s)
    sw.inc(g.dma_start(out=gateT[:], in_=W["gateT"].rearrange("(a p) n -> p a n", p=128)), 16)
    sw.inc(g.dma_start(out=gbias[:], in_=W["gbias"]), 16)
    sg.inc(g.memset(onesr[:], 1.0))
    emit_identity(c, ident, sg)

    es0 = ExitStack(); _sv = c.es; c.es = es0
    xin = [c.sb(name + f"_p0x{i}", [128, 1024], F32) for i in range(2)]
    xnb = c.sb(name + "_p0n", [128, 1024], BF16)
    xnT = [c.sb(name + f"_p0t{i}", [128, 1024], BF16) for i in range(2)]
    gsc = c.sb(name + "_p0g", [128, 16], F32)
    gw2 = c.sb(name + "_p0w", [128, 16], F32)
    scratch2 = [{k: c.sb(name + f"_p0{k}{i}", [128, 1024] if k == "sq" else [128, 1], F32)
                 for k in ("sq", "c0", "c1", "c2", "c3")} for i in range(2)]
    scratch = scratch2[0]
    ps_tr = c.ps(name + "_p0ptr", [128, 1024], BF16)
    ps_sc = c.ps(name + "_p0psc", [128, 16], F32)
    c.es = _sv
    sq_ = [c.sem(kind="hw"), c.sem(kind="hw")]
    so_ = [c.sem(kind="sw"), c.sem(kind="sw")]

    sq_[0].inc(sy.dma_start(out=xin[0][:], in_=Xin[0]), 16)
    sq_[1].inc(sy.dma_start(out=xin[1][:], in_=Xin[1]), 16)
    for ti in range(NT):
        bi = ti % 2
        scratch = scratch2[bi]
        if ti == 0:
            v.wait_ge(sq_[0].h, sq_[0].base + 16)
            s.wait_ge(sq_[0].h, sq_[0].base + 16)
            emit_ln_stats(c, xin[0][:], sc, sv, scratch)
        sc.wait(v)
        sv.inc(v.tensor_scalar(xnb[:], xin[bi][:], scratch["c2"][:], scratch["c3"][:],
                               ALU.subtract, ALU.mult))
        if not inplace:
            sv.wait(g)
            so_[bi].inc(g.dma_start(out=Xout[ti], in_=xin[bi][:]), 16)
        sv.wait(t)
        if ti == 0:
            sg.wait(t); sw.wait(t)
        emit_transpose_tiles(c, xnb, xnT[bi], ident, ps_tr, sp, sv, 8)
        sv.wait(g)
        so_[bi].inc(g.dma_start(
            out=xntd[:, :, 128 * ti:128 * (ti + 1)].rearrange("k p j -> p k j"),
            in_=xnT[bi][:].rearrange("p (k j) -> p k j", j=128)), 16)
        if ti + 1 < NT:
            nb_ = (ti + 1) % 2
            v.wait_ge(sq_[nb_].h, sq_[nb_].base + 16 * ((ti + 1) // 2 + 1))
            s.wait_ge(sq_[nb_].h, sq_[nb_].base + 16 * ((ti + 1) // 2 + 1))
            emit_ln_stats(c, xin[nb_][:], sc, sv, scratch2[nb_])
        if ti + 2 < NT:
            sy.wait_ge(sv.h, sv.n)
            sy.wait_ge(so_[bi].h, so_[bi].n)
            sq_[bi].inc(sy.dma_start(out=xin[bi][:], in_=Xin[ti + 2]), 16)
    _sv = c.es; c.es = es0
    xnt_r = _Ring(c, name + "_xr", 2, [128, 8, 128], BF16, kind="hw")
    c.es = _sv
    so_[0].wait(sy); so_[1].wait(sy)
    for ti in range(NT):
        bi = ti % 2
        if ti < 2:
            xnt_r.dma(ti, sy.dma_start(
                out=xnt_r.buf(ti)[:],
                in_=xntd[:, :, 128 * ti:128 * (ti + 1)].rearrange("k p j -> p k j")))
        xnt_r.wait_loaded(t, ti)
        sv.wait(t); sc.wait(t)
        for k in range(8):
            t.matmul(ps_sc[:], xnt_r.buf(ti)[:, k, :], gateT[:, k, :],
                     start=(k == 0), stop=False)
        sp.inc(t.matmul(ps_sc[:], onesr[:], gbias[:].broadcast_to([1, 16]), start=False, stop=True))
        sp.wait(s)
        sc.inc(s.activation(gsc[:], ps_sc[:], AF.Sigmoid))
        sc.wait(v)
        sv.inc(v.tensor_reduce(scratch["c0"][:], gsc[:, 0:9], AX.X, ALU.max))
        sv.wait(v)
        sv.inc(v.tensor_scalar(gw2[:, 0:9], gsc[:, 0:9], scratch["c0"][:], None, ALU.is_ge))
        sv.wait(v)
        sv.inc(v.scalar_tensor_tensor(gw2[:, 0:9], gw2[:, 0:9], -1e30, gsc[:, 0:9],
                                      ALU.mult, ALU.add))
        sv.wait(v)
        sv.inc(v.tensor_reduce(scratch["c1"][:], gw2[:, 0:9], AX.X, ALU.max))
        sv.wait(v)
        sv.inc(v.tensor_scalar(gw2[:, 0:9], gsc[:, 0:9], scratch["c1"][:], None, ALU.is_ge))
        sv.wait(v)
        sv.inc(v.tensor_tensor(gw2[:, 0:9], gw2[:, 0:9], gsc[:, 0:9], ALU.mult))
        sv.wait(v)
        sv.inc(v.tensor_reduce(scratch["c2"][:], gw2[:, 0:9], AX.X, ALU.add))
        sv.wait(v)
        sv.inc(v.tensor_scalar(scratch["c2"][:], scratch["c2"][:], 1e-20, None, ALU.add))
        sv.wait(v)
        sv.inc(v.reciprocal(scratch["c2"][:], scratch["c2"][:]))
        sv.wait(v)
        sv.inc(v.tensor_scalar(scratch["c2"][:], scratch["c2"][:], 0.5, None, ALU.mult))
        sv.wait(v)
        sv.inc(v.tensor_scalar(cw[:, ti, 0:9], gw2[:, 0:9], scratch["c2"][:], None, ALU.mult))
        if ti + 2 < NT:
            sy.wait_ge(sv.h, sv.n)
            xnt_r.dma(ti + 2, sy.dma_start(
                out=xnt_r.buf(ti)[:],
                in_=xntd[:, :, 128 * (ti + 2):128 * (ti + 3)].rearrange("k p j -> p k j")))
    for smx in sq_ + so_:
        smx.wait(g); smx.wait(sy); smx.wait(t); smx.wait(v); smx.wait(s)
    for smx in (sp, sv, sc, sw, sg):
        for eng in (t, v, s, g, sy):
            smx.wait(eng)
    es0.close()

    esE = ExitStack(); _sv = c.es; c.es = esE
    w1T = c.sb(name + "_w1", [128, 8, 2816], BF16)
    w3T = c.sb(name + "_w3", [128, 8, 2816], BF16)
    w2T = c.sb(name + "_w2", [128, 22, 1024], BF16)
    h_all = c.sb(name + "_h", [128, 22, 512], BF16)
    tsil = [c.sb(name + f"_ts{i}", [128, 512], F32) for i in range(2)]
    delta = [c.sb(name + f"_d{i}", [128, 1024], F32) for i in range(2)]
    xg = _Ring(c, name + "_xg", 2, [128, 8, 512], BF16, kind="hw")
    ps_s1 = [c.ps(name + f"_ps1{i}", [128, 512], F32) for i in range(4)]
    ps_o = [c.ps(name + f"_pso{i}", [128, 1024], F32) for i in range(2)]
    c.es = _sv
    swx = c.sem(kind="sw")
    sevac = [c.sem(), c.sem()]
    spv = [c.sem(), c.sem()]
    sevo = [c.sem(), c.sem()]

    for e in range(NE):
        g.wait_ge(sp.h, sp.n)
        if e == 0:
            swx.inc(g.dma_start(out=w1T[:], in_=W["w1T"][0].rearrange("(a p) n -> p a n", p=128)), 16)
            swx.inc(g.dma_start(out=w3T[:], in_=W["w3T"][0].rearrange("(a p) n -> p a n", p=128)), 16)
        swx.inc(g.dma_start(out=w2T[:], in_=W["w2T"][e].rearrange("(a p) n -> p a n", p=128)), 16)
        swx.wait(t)
        sacc[0].wait(g); sacc[1].wait(g)
        for gi in range(8):
            bx = gi % 2
            if gi < 2:
                if e > 0 or gi > 0:
                    sy.wait_ge(sp.h, sp.n)
                xg.dma(8 * e + gi, sy.dma_start(
                    out=xg.buf(gi)[:],
                    in_=xntd[:, :, 512 * gi:512 * (gi + 1)].rearrange("k p j -> p k j")))
            xg.wait_loaded(t, 8 * e + gi)
            for m in range(22):
                par = m % 2
                pa, pb = ps_s1[2 * par], ps_s1[2 * par + 1]
                if m >= 2:
                    t.wait_ge(sevac[par].h, sevac[par].n)
                for k in range(8):
                    ia = t.matmul(pa[:], w1T[:, k, 128 * m:128 * (m + 1)],
                                  xg.buf(gi)[:, k, :],
                                  start=(k == 0), stop=(k == 7))
                    ib = t.matmul(pb[:], w3T[:, k, 128 * m:128 * (m + 1)],
                                  xg.buf(gi)[:, k, :],
                                  start=(k == 0), stop=(k == 7))
                sp.inc(ia); spv[par].inc(ib)
                s.wait_ge(sp.h, sp.n)
                s.wait_ge(sevac[par].h, sevac[par].n)
                sc.inc(s.activation(tsil[par][:], pa[:], AF.Silu))
                v.wait_ge(sc.h, sc.n)
                v.wait_ge(spv[par].h, spv[par].n)
                sevac[par].inc(v.tensor_tensor(h_all[:, m, :], tsil[par][:], pb[:], ALU.mult))
            if gi == 7 and e + 1 < NE:
                g.wait_ge(sevac[0].h, sevac[0].n)
                g.wait_ge(sevac[1].h, sevac[1].n)
                swx.inc(g.dma_start(out=w1T[:], in_=W["w1T"][e + 1].rearrange("(a p) n -> p a n", p=128)), 16)
                swx.inc(g.dma_start(out=w3T[:], in_=W["w3T"][e + 1].rearrange("(a p) n -> p a n", p=128)), 16)
            for tt in range(4):
                po = ps_o[tt % 2]
                tile = 4 * gi + tt
                if tt == 0:
                    t.wait_ge(sevac[0].h, sevac[0].n)
                    t.wait_ge(sevac[1].h, sevac[1].n)
                t.wait_ge(sevo[tt % 2].h, sevo[tt % 2].n)
                for m in range(22):
                    i1 = t.matmul(po[:, 0:512], h_all[:, m, 128 * tt:128 * (tt + 1)],
                                  w2T[:, m, 0:512], start=(m == 0), stop=(m == 21))
                    i2 = t.matmul(po[:, 512:1024], h_all[:, m, 128 * tt:128 * (tt + 1)],
                                  w2T[:, m, 512:1024], start=(m == 0), stop=(m == 21))
                sp.inc(i1); sp.inc(i2)
                sp.wait(s)
                db = delta[tt % 2]
                sacc[tt % 2].wait(s)
                sevo[tt % 2].inc(s.activation(db[:], po[:], AF.Copy, scale=cw[:, tile, e:e + 1]))
                g.wait_ge(sevo[tt % 2].h, sevo[tt % 2].n)
                sacc[tt % 2].inc(g.dma_start(out=Xout[tile], in_=db[:], accum_op=ALU.add), 16)
            if gi + 2 < 8:
                sy.wait_ge(sp.h, sp.n)
                xg.dma(8 * e + gi + 2, sy.dma_start(
                    out=xg.buf(gi)[:],
                    in_=xntd[:, :, 512 * (gi + 2):512 * (gi + 3)].rearrange("k p j -> p k j")))
    sacc[0].wait(g); sacc[1].wait(g); sacc[0].wait(sy); sacc[1].wait(sy)
    for smx in (sp, sv, sc, swx):
        for eng in (t, v, s, g, sy):
            smx.wait(eng)
    esE.close()
    return dict(sacc=sacc)

def _prep_weights_D(inputs):
    w1 = np.concatenate([inputs["L1_ffn_w1"], inputs["L1_ffn_new_w1"]], 0)
    w3 = np.concatenate([inputs["L1_ffn_w3"], inputs["L1_ffn_new_w3"]], 0)
    w2 = np.concatenate([inputs["L1_ffn_w2"], inputs["L1_ffn_new_w2"]], 0)
    gate = np.concatenate([inputs["L1_ffn_gate_weight"], inputs["L1_ffn_new_gate_weight"]], 0)
    gb = np.zeros((1, 16), np.float32)
    gb[0, 8] = inputs["L1_ffn_gate_bias"][0]
    gb[0, 9:] = -1e30
    gateTpad = np.zeros((1024, 16), np.float32)
    gateTpad[:, 0:9] = gate.T
    return dict(
        w1T=np.ascontiguousarray(np.transpose(w1, (0, 2, 1))),
        w3T=np.ascontiguousarray(np.transpose(w3, (0, 2, 1))),
        w2T=np.ascontiguousarray(np.transpose(w2, (0, 2, 1))),
        gateT=np.ascontiguousarray(gateTpad),
        gbias=gb,
    )

def build_D():
    nc = bass.Bass("TRN2", target_bir_lowering=False)
    X = nc.dram_tensor("X", [33, 128, D], F32, kind="ExternalInput").ap()
    XO = nc.dram_tensor("XO", [33, 128, D], F32, kind="ExternalOutput").ap()
    Wap = {k: nc.dram_tensor("W_" + k, list(s), F32, kind="ExternalInput").ap()
           for k, s in D_W_SHAPES.items()}
    xntd = nc.dram_tensor("S_xnT", [8, 128, NTOK], BF16, kind="Internal").ap()
    es = ExitStack()
    c = Ctx(nc, es)
    emit_phase_L1FFN(c, X, XO, Wap, xntd, "Dp")
    es.close()
    return nc

E_W_SHAPES = dict(WqkT=(1024, 2048), bqk=(1, 2048), lepw=(128, 8, 3), lepb=(128, 8),
                  kidx=(1, 2048), sgn=(1, 2048), mask67=(67, 8))
LN1E4 = float(np.log(10000.0))
NSEQ = 8192.0
E_SKIP = set()

def emit_phase_linattn(c, Xin, Xout, W, name, mode="full", ccg=None, ccpart=None,
                       hrowd=None, qeD=None, qrD=None, uTD=None):
    nc = c.nc
    t, v, s, g, sy = c.t, c.v, c.s, c.g, c.sy
    pre = (mode == "pre")

    WqkT = c.sb(name + "_wq", [128, 8, 2048], BF16)
    bqk = c.sb(name + "_bq", [1, 2048], BF16)
    onesr = c.sb(name + "_ones", [1, 128], BF16)
    ident = c.sb(name + "_id", [128, 128], BF16)
    cos_bc = c.sb(name + "_cos", [128, 2048], BF16)
    sin_bc = c.sb(name + "_sin", [128, 2048], BF16)
    es_bc = ExitStack(); _svb = c.es; c.es = es_bc
    r_th = c.sb(name + "_rth", [1, 2048], F32, side="right")
    r_sin = c.sb(name + "_rsin", [1, 2048], F32, side="right")
    r_cos = c.sb(name + "_rcos", [1, 2048], F32, side="right")
    r_sgn = c.sb(name + "_rsgn", [1, 2048], F32, side="right")
    c.es = _svb
    rowb = c.sb(name + "_rowb", [1, 2048], BF16)
    lepw = c.sb(name + "_lw", [128, 8, 3], F32)
    lepb = c.sb(name + "_lb", [128, 8], F32)
    m67 = c.sb(name + "_m67", [67, 8], F32)
    sg = c.sem(); sp = c.sem(); sv = c.sem(); sc = c.sem()
    sw = c.sem(kind="sw")

    c.const(EPS_LN); c.const(float(np.pi / 2)); c.const(1.0)
    c.sconst.wait(s)
    sw.inc(g.dma_start(out=WqkT[:], in_=W["WqkT"].rearrange("(a p) n -> p a n", p=128)), 16)
    sw.inc(g.dma_start(out=bqk[:], in_=W["bqk"]), 16)
    sw.inc(g.dma_start(out=r_th[:], in_=W["kidx"]), 16)
    sw.inc(g.dma_start(out=r_sgn[:], in_=W["sgn"]), 16)
    sw.inc(g.dma_start(out=lepw[:], in_=W["lepw"]), 16)
    sw.inc(g.dma_start(out=lepb[:], in_=W["lepb"]), 16)
    if not pre:
        sw.inc(g.dma_start(out=m67[:], in_=W["mask67"]), 16)
    sg.inc(g.memset(onesr[:], 1.0))
    emit_identity(c, ident, sg)

    _svb = c.es; c.es = es_bc
    ps_bc = c.ps(name + "_psbc", [128, 512], F32)
    c.es = _svb
    sw.wait(s)
    sc.inc(s.activation(r_th[:], r_th[:], AF.Exp, scale=-LN1E4))
    sc.wait(s)
    sc.inc(s.activation(r_sin[:], r_th[:], AF.Sin))
    sc.wait(s)
    sc.inc(s.activation(r_cos[:], r_th[:], AF.Sin, bias=float(np.pi / 2)))
    sc.wait(v); sw.wait(v)
    sv.inc(v.tensor_tensor(r_sin[:], r_sin[:], r_sgn[:], ALU.mult))
    for which, src_row, dst in ((0, r_cos, cos_bc), (1, r_sin, sin_bc)):
        sv.wait(v)
        sv.inc(v.tensor_copy(rowb[:], src_row[:]))
        for j in range(4):
            sv.wait(t); sg.wait(t)
            sp.inc(t.matmul(ps_bc[:], onesr[:], rowb[0:1, 512 * j:512 * (j + 1)],
                            start=True, stop=True))
            sp.wait(v)
            sv.inc(v.tensor_copy(dst[:, 512 * j:512 * (j + 1)], ps_bc[:]))

    if (not pre) and ("combine" in E_SKIP):
        kv_sb = c.sb(name + "_kv", [128, 1024], BF16)
        kvp = c.sb(name + "_kvp", [128, 1024], BF16)
        ksum_bc = c.sb(name + "_ksb", [128, 1024], BF16)
        halocols = c.sb(name + "_hc", [128, 2, 8], F32)
        sg.inc(g.memset(kv_sb[:], 0.001))
        sg.inc(g.memset(kvp[:], 0.001))
        sg.inc(g.memset(ksum_bc[:], 0.001))
        sg.inc(g.memset(halocols[:], 0.0))
        sg.wait(v); sg.wait(s); sg.wait(t)
    elif not pre:
        kv_sb = c.sb(name + "_kv", [128, 1024], BF16)
        kvp = c.sb(name + "_kvp", [128, 1024], BF16)
        ksum_bc = c.sb(name + "_ksb", [128, 1024], BF16)
        acc67 = [c.sb(name + f"_a67{i}", [67, 1024], F32) for i in range(2)]
        halocols = c.sb(name + "_hc", [128, 2, 8], F32)
        es_cc = ExitStack(); _sv = c.es; c.es = es_cc
        slot = _Ring(c, name + "_sl", 2, [67, 1024], F32, kind="hw")
        c.es = _sv
        sv2 = c.sem()
        sw.wait(v)
        for j in range(8):
            if j >= 2:
                sy.wait_ge(sv2.h, sv2.n)
            slot.dma(j, sy.dma_start(out=slot.buf(j)[:], in_=ccg[j]))
            slot.wait_loaded(v, j)
            dst, src_ = acc67[(j + 1) % 2], acc67[j % 2]
            if j == 0:
                sv2.inc(v.tensor_scalar(acc67[1][:], slot.buf(0)[:], m67[:, 0:1], None, ALU.mult))
            else:
                sv2.wait(v)
                sv2.inc(v.scalar_tensor_tensor(dst[:], slot.buf(j)[:], m67[:, j:j + 1],
                                               src_[:], ALU.mult, ALU.add))
        sv2.wait(v)
        sv2.inc(v.tensor_copy(kv_sb[0:64, :], acc67[0][0:64, :]))
        sv2.inc(v.tensor_copy(kv_sb[64:128, :], acc67[0][0:64, :]))
        sv2.wait(g)
        sg.inc(g.memset(kvp[:], 0.0))
        sg.wait(v)
        sv2.wait(v)
        for j_ in range(8):
            sv2.inc(v.tensor_copy(kvp[0:64, 128 * j_:128 * j_ + 64],
                                  kv_sb[0:64, 128 * j_:128 * j_ + 64]))
            sv2.inc(v.tensor_copy(kvp[64:128, 128 * j_ + 64:128 * (j_ + 1)],
                                  kv_sb[64:128, 128 * j_ + 64:128 * (j_ + 1)]))
        sv2.wait(v)
        sv2.inc(v.tensor_copy(rowb[:, 0:1024], acc67[0][64:65, :]))
        sv2.wait(t)
        for j in range(2):
            sp.inc(t.matmul(ps_bc[:], onesr[:], rowb[0:1, 512 * j:512 * (j + 1)],
                            start=True, stop=True))
            sp.wait(v)
            sv2.inc(v.tensor_copy(ksum_bc[:, 512 * j:512 * (j + 1)], ps_bc[:]))
            sv2.wait(t)
        sv2.wait(g)
        shh = c.sem(kind="sw")
        shh.inc(g.dma_start(out=hrowd[:], in_=acc67[0][65:67, :]), 16)
        shh.wait(g)
        with nc.allow_non_contiguous_dma(reason="2KB halo gather"):
            shh.inc(g.dma_start(out=halocols[:],
                                in_=hrowd[:].rearrange("r (m p) -> p r m", p=128)), 16)
        shh.wait(v); shh.wait(s)
        for smx in (sv2, shh):
            for eng in (t, v, s, g, sy):
                smx.wait(eng)
        es_cc.close()

    es_bc.close()
    es0 = ExitStack(); _sv = c.es; c.es = es0
    xin = [c.sb(name + f"_x{i}", [128, 1024], F32) for i in range(2)]
    ub = c.sb(name + "_u", [128, 1024], BF16)
    uT = [c.sb(name + f"_uT{i}", [128, 1024], BF16) for i in range(2)]
    ulastc = [c.sb(name + f"_ulc{i}", [128, 8], BF16) for i in range(2)]
    e0 = c.sb(name + "_e0", [128, 2048], F32)
    e1 = c.sb(name + "_e1", [128, 2048], F32)
    qke = c.sb(name + "_qke", [128, 2048], BF16)
    t1b = c.sb(name + "_t1", [128, 2048], BF16)
    t2b = c.sb(name + "_t2", [128, 2048], BF16)
    qkr = c.sb(name + "_qkr", [128, 2048], BF16)
    qz = c.sb(name + "_qz", [128, 1024], BF16)
    qzT = c.sb(name + "_qzT", [128, 1024], BF16)
    zt = c.sb(name + "_zt", [128, 1024], F32)
    zden = c.sb(name + "_zd", [128, 16], F32)
    pef = [c.sb(name + f"_pef{i}", [128, 1024], F32) for i in range(2)]
    pe_b = c.sb(name + "_peb", [128, 1024], BF16)
    x1s = [c.sb(name + f"_x1{i}", [128, 1024], F32) for i in range(2)]
    xo = _Ring(c, name + "_xo", 2, [128, 1024], F32, kind="sw")
    scratch = {k: c.sb(name + "_s" + k, [128, 1024] if k == "sq" else [128, 1], F32)
               for k in ("sq", "c0", "c1", "c2", "c3")}
    ps_tr = c.ps(name + "_pstr", [128, 1024], BF16)
    ps_qk = c.ps(name + "_psqk", [128, 2048], F32)
    if pre:
        ps_kv = c.ps(name + "_pskv", [64, 1024], F32)
        urow = c.sb(name + "_ur", [1, 2048], F32)
        ks_acc = [c.sb(name + f"_ka{i}", [1, 1024], F32) for i in range(2)]
    else:
        ps_at = c.ps(name + "_psat", [128, 1024], F32)
        ps_pe = c.ps(name + "_pspe", [128, 1024], BF16)
    c.es = _sv
    sq_ = [c.sem(kind="hw"), c.sem(kind="hw")]
    sst = [c.sem(kind="hw"), c.sem(kind="hw")]
    surow = c.sem(kind="sw")

    _sv = c.es; c.es = es0
    onescb = c.sb(name + "_oc", [128, 1], BF16)
    kvf = c.sb(name + "_kvf", [64, 1024], F32)
    ksf = c.sb(name + "_ksf", [1, 1024], F32)
    c.es = _sv
    sg.inc(g.memset(onescb[:], 1.0))
    if not pre:
        env_loc = dict(sp=sp, sv=sv, sc=sc, pef=pef, x1s=x1s, xo=xo, lepw=lepw,
                       pe_b=pe_b, ps_pe=ps_pe, ident=ident, Xout=Xout)

    if pre:
        st3 = [c.sem(kind="sw"), c.sem(kind="sw")]
        for ti in range(NT):
            bi = ti % 2
            if ti < 2:
                sq_[bi].inc(sy.dma_start(out=xin[bi][:], in_=Xin[ti]), 16)
            v.wait_ge(sq_[bi].h, sq_[bi].base + 16 * (ti // 2 + 1))
            s.wait_ge(sq_[bi].h, sq_[bi].base + 16 * (ti // 2 + 1))
            emit_ln_stats(c, xin[bi][:], sc, sv, scratch)
            sc.wait(v)
            v.wait_ge(sp.h, sp.n)
            surow.wait(v)
            st3[bi].wait(v)
            sv.inc(v.tensor_scalar(ub[:], xin[bi][:], scratch["c2"][:], scratch["c3"][:],
                                   ALU.subtract, ALU.mult))
            if ti == 0:
                sv.wait(g)
                surow.inc(g.dma_start(out=urow[0:1, 0:1024], in_=ub[0:1, :]), 16)
            sv.wait(t)
            if ti == 0:
                sg.wait(t); sw.wait(t)
            emit_transpose_tiles(c, ub, uT[bi], ident, ps_tr, sp, sv, 8)
            sv.wait(t)
            if ti >= 1:
                t.wait_ge(sv.h, sv.n)
            for nb in range(4):
                for k in range(8):
                    t.matmul(ps_qk[:, 512 * nb:512 * (nb + 1)], uT[bi][:, 128 * k:128 * (k + 1)],
                             WqkT[:, k, 512 * nb:512 * (nb + 1)], start=(k == 0), stop=False)
                sp.inc(t.matmul(ps_qk[:, 512 * nb:512 * (nb + 1)], onesr[:],
                                bqk[0:1, 512 * nb:512 * (nb + 1)], start=False, stop=True))
            sp.wait(v)
            sv.inc(v.tensor_scalar(e0[:], ps_qk[:], 0.0, None, ALU.min))
            sv.wait(s)
            sc.inc(s.activation(e1[:], e0[:], AF.Exp))
            sv.wait(v); sc.wait(v)
            sv.inc(v.scalar_tensor_tensor(e0[:], e0[:], -1.0, ps_qk[:], ALU.mult, ALU.add))
            sc.wait(v); sv.wait(v)
            sv.inc(v.tensor_add(qke[:], e0[:], e1[:]))
            sv.wait(v)
            sv.inc(v.tensor_tensor(t1b[:], qke[:], cos_bc[:], ALU.mult))
            qv = qke[:].rearrange("p (b s j) -> p b s j", s=2, j=32)
            sv_ = sin_bc[:].rearrange("p (b s j) -> p b s j", s=2, j=32)
            t2v = t2b[:].rearrange("p (b s j) -> p b s j", s=2, j=32)
            sv.wait(v)
            sv.inc(v.tensor_tensor(t2v[:, :, 0, :], qv[:, :, 1, :], sv_[:, :, 0, :], ALU.mult))
            sv.inc(v.tensor_tensor(t2v[:, :, 1, :], qv[:, :, 0, :], sv_[:, :, 1, :], ALU.mult))
            sv.wait(v)
            sv.inc(v.tensor_add(qkr[:], t1b[:], t2b[:]))
            sv.wait(g)
            st3[bi].inc(g.dma_start(out=uTD[ti], in_=uT[bi][:]), 16)
            st3[bi].inc(g.dma_start(out=qeD[ti], in_=qke[:, 0:1024]), 16)
            st3[bi].inc(g.dma_start(out=qrD[ti], in_=qkr[:, 0:1024]), 16)
            sv.wait(t)
            for h in range(16):
                ins = t.matmul(ps_kv[0:64, 64 * h:64 * (h + 1)],
                               qkr[:, 1024 + 64 * h:1024 + 64 * (h + 1)],
                               ub[:, 64 * h:64 * (h + 1)],
                               start=(ti == 0 and h % 8 == 0), stop=(ti == NT - 1 and h % 8 == 7))
            sp.inc(ins)
            i2 = t.matmul(ps_qk[0:1, 0:512], onescb[:], qke[:, 1024:1536], start=True, stop=True)
            sp.inc(i2)
            i3 = t.matmul(ps_qk[0:1, 512:1024], onescb[:], qke[:, 1536:2048], start=True, stop=True)
            sp.inc(i3)
            sp.wait(v)
            if ti == 0:
                sv.inc(v.tensor_copy(ks_acc[0][:], ps_qk[0:1, 0:1024]))
            else:
                sv.inc(v.tensor_tensor(ks_acc[ti % 2][:], ks_acc[(ti + 1) % 2][:],
                                       ps_qk[0:1, 0:1024], ALU.add))
            if ti == NT - 1:
                sv.wait(g)
                surow.inc(g.dma_start(out=urow[0:1, 1024:2048], in_=ub[127:128, :]), 16)
            if ti + 2 < NT:
                sy.wait_ge(sv.h, sv.n)
                sy.wait_ge(sp.h, sp.n)
                sq_[bi].inc(sy.dma_start(out=xin[bi][:], in_=Xin[ti + 2]), 16)
    else:
        _sv = c.es; c.es = es0
        qel = _Ring(c, name + "_qel", 2, [128, 1024], BF16, kind="hw")
        qrl = _Ring(c, name + "_qrl", 2, [128, 1024], BF16, kind="hw")
        utl = _Ring(c, name + "_utl", 2, [128, 1024], BF16, kind="hw")
        c.es = _sv
        for ti in range(NT):
            bi = ti % 2
            if ti < 2:
                sq_[bi].inc(sy.dma_start(out=xin[bi][:], in_=Xin[ti]), 16)
                qel.dma(ti, sy.dma_start(out=qel.buf(ti)[:], in_=qeD[ti]))
                qrl.dma(ti, sy.dma_start(out=qrl.buf(ti)[:], in_=qrD[ti]))
                utl.dma(ti, sy.dma_start(out=utl.buf(ti)[:], in_=uTD[ti]))
            v.wait_ge(sq_[bi].h, sq_[bi].base + 16 * (ti // 2 + 1))
            for r_ in (qel, qrl, utl):
                r_.wait_loaded(v, ti)
                r_.wait_loaded(t, ti)
            uTb = utl.buf(ti)
            sv.wait(v)
            sv.inc(v.tensor_tensor(zt[:], qel.buf(ti)[:], ksum_bc[:], ALU.mult))
            sv.wait(v)
            sv.inc(v.tensor_reduce(zden[:].rearrange("p (h j) -> p h j", j=1),
                                   zt[:].rearrange("p (h j) -> p h j", j=64), AX.X, ALU.add))
            sv.wait(v)
            sv.inc(v.tensor_scalar(zden[:], zden[:], 1.0 / NSEQ, 1e-6, ALU.mult, ALU.add))
            sv.wait(v)
            sv.inc(v.reciprocal(zden[:], zden[:]))
            sv.wait(v)
            sv.inc(v.tensor_scalar(zden[:], zden[:], 1.0 / NSEQ, None, ALU.mult))
            sv.wait(v)
            sv.inc(v.tensor_tensor(qz[:].rearrange("p (h d) -> p h d", d=64),
                                   qrl.buf(ti)[:].rearrange("p (h d) -> p h d", d=64),
                                   zden[:].unsqueeze(2).broadcast_to([128, 16, 64]), ALU.mult))
            sv.wait(t)
            if ti == 0:
                sg.wait(t); sw.wait(t)
            emit_transpose_tiles(c, qz, qzT, ident, ps_tr, sp, sv, 8)
            sv.wait(t)
            if ti >= 1:
                t.wait_ge(sv.h, sv.n)
            for j_ in range(8):
                ins = t.matmul(ps_at[:, 128 * j_:128 * (j_ + 1)],
                               qzT[:, 128 * j_:128 * (j_ + 1)],
                               kvp[:, 128 * j_:128 * (j_ + 1)],
                               start=(j_ % 4 == 0), stop=(j_ % 4 == 3))
                if j_ % 4 == 3:
                    sp.inc(ins)
            lcp = ulastc[(ti + 1) % 2]
            pf = pef[bi]
            for m in range(8):
                um = uTb[:, 128 * m:128 * (m + 1)]
                sv.wait(v); sc.wait(v)
                sv.inc(v.tensor_scalar(pf[:, 128 * m:128 * (m + 1)], um, lepw[:, m, 1:2], None, ALU.mult))
                sv.wait(v)
                sv.inc(v.scalar_tensor_tensor(pf[:, 128 * m:128 * m + 127], um[:, 1:128],
                                              lepw[:, m, 2:3], pf[:, 128 * m:128 * m + 127],
                                              ALU.mult, ALU.add))
                sv.wait(v)
                sv.inc(v.scalar_tensor_tensor(pf[:, 128 * m + 1:128 * (m + 1)], um[:, 0:127],
                                              lepw[:, m, 0:1], pf[:, 128 * m + 1:128 * (m + 1)],
                                              ALU.mult, ALU.add))
                sv.wait(v)
                lsrc = halocols[:, 1, m:m + 1] if ti == 0 else lcp[:, m:m + 1]
                sv.inc(v.scalar_tensor_tensor(pf[:, 128 * m:128 * m + 1], lsrc,
                                              lepw[:, m, 0:1], pf[:, 128 * m:128 * m + 1],
                                              ALU.mult, ALU.add))
                sv.wait(v)
                sv.inc(v.tensor_scalar(pf[:, 128 * m:128 * (m + 1)], pf[:, 128 * m:128 * (m + 1)],
                                       lepb[:, m:m + 1], None, ALU.add))
            sv.wait(v)
            for m in range(8):
                sv.inc(v.tensor_copy(ulastc[bi][:, m:m + 1], uTb[:, 128 * m + 127:128 * (m + 1)]))
            sp.wait(v)
            sv.wait(v)
            sv.inc(v.tensor_add(x1s[bi][:], ps_at[:], xin[bi][:]))
            if ti >= 1:
                _finish_lepe_tile(c, env_loc, ti - 1, uTb, None)
            if ti + 2 < NT:
                sy.wait_ge(sv.h, sv.n)
                sq_[bi].inc(sy.dma_start(out=xin[bi][:], in_=Xin[ti + 2]), 16)
                qel.dma(ti + 2, sy.dma_start(out=qel.buf(ti)[:], in_=qeD[ti + 2]))
                qrl.dma(ti + 2, sy.dma_start(out=qrl.buf(ti)[:], in_=qrD[ti + 2]))
                utl.dma(ti + 2, sy.dma_start(out=utl.buf(ti)[:], in_=uTD[ti + 2]))
    if not pre:
        _finish_lepe_tile(c, env_loc, NT - 1, None, halocols)
        for i_ in range(2):
            xo.sems[i_].wait(g); xo.sems[i_].wait(sy)
    else:
        sp.wait(v); sv.wait(v)
        sv.inc(v.tensor_copy(kvf[:], ps_kv[0:64, :]))
        sv.inc(v.tensor_copy(ksf[:], ks_acc[(NT - 1) % 2][:]))
        sv.wait(g)
        surow.wait(g)
        sfin = c.sem(kind="sw")
        sfin.inc(g.dma_start(out=ccpart[0:64, :], in_=kvf[:]), 16)
        sfin.inc(g.dma_start(out=ccpart[64:65, :], in_=ksf[:]), 16)
        sfin.inc(g.dma_start(out=ccpart[65:66, :], in_=urow[0:1, 0:1024]), 16)
        sfin.inc(g.dma_start(out=ccpart[66:67, :], in_=urow[0:1, 1024:2048]), 16)
        sfin.wait(g); sfin.wait(sy)
    es0.close()
    return None

def _finish_lepe_tile(c, L, tj, u_next, halocols):
    t, v, s, g, sy = c.t, c.v, c.s, c.g, c.sy
    sp, sv, sc = L["sp"], L["sv"], L["sc"]
    pef, x1s, xo, lepw = L["pef"], L["x1s"], L["xo"], L["lepw"]
    pe_b, ps_pe, ident, Xout = L["pe_b"], L["ps_pe"], L["ident"], L["Xout"]
    bj = tj % 2
    pf = pef[bj]
    for m in range(8):
        rsrc = (halocols[:, 0, m:m + 1] if u_next is None
                else u_next[:, 128 * m:128 * m + 1])
        sv.wait(v)
        sv.inc(v.scalar_tensor_tensor(pf[:, 128 * m + 127:128 * (m + 1)], rsrc,
                                      lepw[:, m, 2:3], pf[:, 128 * m + 127:128 * (m + 1)],
                                      ALU.mult, ALU.add))
    sv.wait(v)
    sv.inc(v.tensor_copy(pe_b[:], pf[:]))
    sv.wait(t)
    t.wait_ge(sv.h, sv.n)
    for m in range(8):
        ins = t.transpose(ps_pe[:, 128 * m:128 * (m + 1)], pe_b[:, 128 * m:128 * (m + 1)], ident[:])
    sp.inc(ins)
    sp.wait(v)
    ob = xo.buf(tj)
    v.wait_ge(xo.sem(tj).h, xo.sem(tj).base + 16 * (tj // 2))
    sv.inc(v.tensor_add(ob[:], x1s[bj][:], ps_pe[:]))
    sv.wait(g)
    xo.dma(tj, g.dma_start(out=Xout[tj], in_=ob[:]))


def _perm_qk():
    idx = np.zeros(2048, np.int64)
    for half in range(2):
        for h in range(16):
            for kk in range(32):
                for s_ in range(2):
                    new = half * 1024 + h * 64 + 32 * s_ + kk
                    old = half * 1024 + h * 64 + 2 * kk + s_
                    idx[new] = old
    return idx

def _prep_weights_E(inputs):
    perm = _perm_qk()
    WqkT = np.ascontiguousarray(inputs["L1_attn_W_qk"][perm].T)
    bqk = np.ascontiguousarray(inputs["L1_attn_b_qk"][perm][None, :])
    lepw = np.ascontiguousarray(
        inputs["L1_attn_lepe_w"][:, 0, :].reshape(8, 128, 3).transpose(1, 0, 2))
    lepb = np.ascontiguousarray(inputs["L1_attn_lepe_b"].reshape(8, 128).T)
    kidx = np.zeros((1, 2048), np.float32)
    sgn = np.zeros((1, 2048), np.float32)
    for half in range(2):
        for h in range(16):
            for s_ in range(2):
                for kk in range(32):
                    pos = half * 1024 + h * 64 + 32 * s_ + kk
                    kidx[0, pos] = (h * 32 + kk) / 512.0
                    sgn[0, pos] = -1.0 if s_ == 0 else 1.0
    return dict(WqkT=WqkT, bqk=bqk, lepw=lepw, lepb=lepb, kidx=kidx, sgn=sgn)

def _core_masks_E(cid):
    b = cid // 2
    m = np.zeros((67, 8), np.float32)
    m[0:65, 2 * b] = 1.0
    m[0:65, 2 * b + 1] = 1.0
    if cid % 2 == 0:
        m[65, cid + 1] = 1.0
    else:
        m[66, cid - 1] = 1.0
    return dict(mask67=m)

def build_E(mode="full"):
    nc = bass.Bass("TRN2", target_bir_lowering=False)
    X = nc.dram_tensor("X", [33, 128, D], F32, kind="ExternalInput").ap()
    Wap = {k: nc.dram_tensor("W_" + k, list(s), F32, kind="ExternalInput").ap()
           for k, s in E_W_SHAPES.items()}
    es = ExitStack()
    c = Ctx(nc, es)
    if mode == "pre":
        ccpart = nc.dram_tensor("CP", [67, 1024], F32, kind="ExternalOutput").ap()
        emit_phase_linattn(c, X, None, Wap, "Ep", mode="pre", ccpart=ccpart)
    else:
        XO = nc.dram_tensor("XO", [33, 128, D], F32, kind="ExternalOutput").ap()
        ccg = nc.dram_tensor("CG", [8, 67, 1024], F32, kind="ExternalInput").ap()
        hrowd = nc.dram_tensor("S_hrow", [2, 1024], F32, kind="Internal").ap()
        emit_phase_linattn(c, X, XO, Wap, "Ef", mode="full", ccg=ccg, hrowd=hrowd)
    es.close()
    return nc

def _global_barrier(c):
    for eng in (c.t, c.v, c.s, c.g, c.sy):
        for sm in c._active:
            if sm.n > 0:
                sm.wait(eng)

def build_full():
    nc = bass.Bass("TRN2", target_bir_lowering=False)
    X = nc.dram_tensor("X", [33, 128, D], F32, kind="ExternalInput").ap()
    XO = nc.dram_tensor("XO", [33, 128, D], F32, kind="ExternalOutput").ap()
    Xs = nc.dram_tensor("S_X", [33, 128, D], F32, kind="Internal").ap()
    WapA = {k: nc.dram_tensor("WA_" + k, list(s), F32, kind="ExternalInput").ap()
            for k, s in A_W_SHAPES.items()}
    WapB = {k: nc.dram_tensor("WB_" + k, list(s), F32, kind="ExternalInput").ap()
            for k, s in B_W_SHAPES.items()}
    WapD = {k: nc.dram_tensor("WD_" + k, list(s), F32, kind="ExternalInput").ap()
            for k, s in D_W_SHAPES.items()}
    WapE = {k: nc.dram_tensor("WE_" + k, list(s), F32, kind="ExternalInput").ap()
            for k, s in E_W_SHAPES.items()}
    DsB = {k: nc.dram_tensor("SB_" + k, list(sh), dt_, kind="Internal",
                             addr_space=("Shared" if k == "ccout" else "Local")).ap()
           for k, (sh, dt_) in B_SCRATCH.items()}
    xntd = nc.dram_tensor("S_xnT", [8, 128, NTOK], BF16, kind="Internal").ap()
    ccinE = nc.dram_tensor("S_ccinE", [67, 1024], F32, kind="Internal").ap()
    ccoutE = nc.dram_tensor("S_ccoutE", [536, 1024], F32, kind="Internal",
                            addr_space="Shared").ap()
    hrowd = nc.dram_tensor("S_hrow", [2, 1024], F32, kind="Internal").ap()
    qeD = nc.dram_tensor("S_qeD", [NT, 128, D], BF16, kind="Internal").ap()
    qrD = nc.dram_tensor("S_qrD", [NT, 128, D], BF16, kind="Internal").ap()
    uTD = nc.dram_tensor("S_uTD", [NT, 128, D], BF16, kind="Internal").ap()

    es = ExitStack()
    c = Ctx(nc, es)
    c.const(EPS_LN); c.const(1.0); c.const(float(np.pi / 2))

    def phase(fn):
        es_p = ExitStack()
        old = c.es
        c.es = es_p
        fn()
        c.es = old
        _global_barrier(c)
        es_p.close()
        c.release_sems()

    wstateA = emit_L0FFN_weights(c, WapA, "AW")
    phase(lambda: emit_phase_L0FFN(c, X, Xs, WapA, NT + 1, "A", wsem=wstateA))
    phase(lambda: emit_phase_mamba_main(c, Xs, Xs, WapB, DsB, "B", mode="cc"))
    phase(lambda: emit_phase_L0FFN(c, Xs, Xs, WapA, NT, "C", wsem=wstateA))
    phase(lambda: emit_phase_L1FFN(c, Xs, Xs, WapD, xntd, "Dp", inplace=True))
    phase(lambda: emit_phase_linattn(c, Xs, None, WapE, "Ep", mode="pre", ccpart=ccinE,
                                     qeD=qeD, qrD=qrD, uTD=uTD))
    def _ecc():
        scc = c.sem(kind="cc")
        scc.inc(c.g.collective_compute(
            "AllGather", ALU.bypass, replica_groups=[list(range(NCORES))],
            ins=[ccinE.opt()], outs=[ccoutE.opt()]), 1)
        scc.wait(c.g)
    phase(_ecc)
    ccgv = ccoutE.rearrange("(s p) f -> s p f", p=67)
    phase(lambda: emit_phase_linattn(c, Xs, Xs, WapE, "Ef", mode="full",
                                     ccg=ccgv, hrowd=hrowd, qeD=qeD, qrD=qrD, uTD=uTD))
    phase(lambda: emit_phase_L1FFN(c, Xs, XO, WapD, xntd, "Fp", inplace=False))
    es.close()
    return nc

def kernel(**inputs):
    x = np.ascontiguousarray(inputs["L0_attn_x"], np.float32)
    shards = shard_stream(x)
    WA = _prep_weights_A(inputs)
    WB = _prep_weights_B(inputs)
    WD = _prep_weights_D(inputs)
    WE = _prep_weights_E(inputs)
    cores = list(range(NCORES))
    nc = build_full()
    in_maps = []
    for cid in cores:
        m = dict(X=shards[cid])
        m.update({"WA_" + k: v for k, v in WA.items()})
        m.update({"WB_" + k: v for k, v in WB.items()})
        m.update({"WB_" + k: v for k, v in _core_masks_B(cid).items()})
        m.update({"WD_" + k: v for k, v in WD.items()})
        m.update({"WE_" + k: v for k, v in WE.items()})
        m.update({"WE_" + k: v for k, v in _core_masks_E(cid).items()})
        in_maps.append(m)
    res = run_bass_kernel_spmd(nc, in_maps, core_ids=cores)
    out = np.empty((4, 8192, D), np.float32)
    for cid in cores:
        b, h = cid // 2, cid % 2
        out[b, h * NTOK:(h + 1) * NTOK] = (
            np.asarray(res.results[cid]["XO"]).reshape(NT + 1, 128, D)[:NT].reshape(NTOK, D))
    return out


# version 09 (final)
# speedup vs baseline: 1.3794x; 1.0266x over previous
import math
from contextlib import ExitStack

import numpy as np
import concourse.bass as bass
import concourse.mybir as mybir
from concourse.bass_utils import run_bass_kernel_spmd

F32 = mybir.dt.float32
BF16 = mybir.dt.bfloat16
AF = mybir.ActivationFunctionType
ALU = mybir.AluOpType
AX = mybir.AxisListType

NCORES = 8
NTOK = 4096
NT = 32
D = 1024
EPS_LN = 1e-5

class Sem:
    def __init__(self, es, nc, name):
        self.h = es.enter_context(nc.semaphore(name))
        self.n = 0
        self.base = 0
    def inc(self, instr, k=1):
        instr.then_inc(self.h, k)
        self.n += k
        return instr
    def wait(self, eng, val=None):
        eng.wait_ge(self.h, self.n if val is None else val)

class Ctx:
    def __init__(self, nc, es):
        self.nc = nc
        self.es = es
        self.sem_es = es
        self.t = nc.tensor
        self.v = nc.vector
        self.s = nc.scalar
        self.g = nc.gpsimd
        self.sy = nc.sync
        self._semid = 0
        self._pools = {}
        self._active = []
        self._consts = {}
        self.sconst = self.sem("sconst")
        self.sconst.kind = "c"
    def sem(self, name=None, kind="c"):
        pool = self._pools.setdefault(kind, [])
        if pool:
            s = pool.pop()
        else:
            self._semid += 1
            s = Sem(self.sem_es, self.nc, name or f"{kind}{self._semid}")
            s.kind = kind
        s.base = s.n
        self._active.append(s)
        return s
    def release_sems(self):
        for s in self._active:
            if s is not self.sconst:
                self._pools.setdefault(s.kind, []).append(s)
        self._active = [s for s in self._active if s is self.sconst]
    def sb(self, name, shape, dtype, side=None):
        if side is None:
            return self.es.enter_context(self.nc.sbuf_tensor(name, shape, dtype))
        return self.es.enter_context(self.nc.sbuf_tensor(name, shape, dtype, side=side))
    def ps(self, name, shape, dtype):
        return self.es.enter_context(self.nc.psum_tensor(name, shape, dtype))
    def const(self, value, dtype=F32):
        key = (dtype, float(value))
        if key not in self._consts:
            tile = self.sem_es.enter_context(
                self.nc.sbuf_tensor(f"const{len(self._consts)}", [128, 1], dtype))
            self.sconst.inc(self.g.memset(tile[:], float(value)))
            self.nc.const_aps.register(dtype, float(value), tile[:])
            self._consts[key] = tile
        return float(value)

def emit_ln(c, x_ap, xn_bf, sc, sv, scratch):
    v, s = c.v, c.s
    sq, c0, c1_, c2, c3 = (scratch["sq"], scratch["c0"], scratch["c1"],
                           scratch["c2"], scratch["c3"])
    n = x_ap.shape[-1]
    sv.inc(v.tensor_reduce(c0[:], x_ap, AX.X, ALU.add))
    sc.inc(s.activation(sq[:, 0:n], x_ap, AF.Square))
    sc.wait(v)
    sv.inc(v.tensor_reduce(c1_[:], sq[:, 0:n], AX.X, ALU.add))
    sv.wait(v)
    sv.inc(v.tensor_scalar(c2[:], c0[:], 1.0 / n, None, ALU.mult))
    sv.inc(v.tensor_scalar(c1_[:], c1_[:], 1.0 / n, None, ALU.mult))
    sv.wait(v)
    sv.inc(v.tensor_tensor(c3[:], c2[:], c2[:], ALU.mult))
    sv.wait(v)
    sv.inc(v.tensor_tensor(c1_[:], c1_[:], c3[:], ALU.subtract))
    sv.wait(s)
    sc.inc(s.activation(c3[:], c1_[:], AF.Ln, bias=EPS_LN))
    sc.wait(s)
    sc.inc(s.activation(c3[:], c3[:], AF.Exp, scale=-0.5))
    sc.wait(v)
    sv.inc(v.tensor_scalar(xn_bf, x_ap, c2[:], c3[:], ALU.subtract, ALU.mult))
    return c3, c2


def emit_ln_stats(c, x_ap, sc, sv, scratch):
    v, s = c.v, c.s
    sq, c0, c1_, c2, c3 = (scratch["sq"], scratch["c0"], scratch["c1"],
                           scratch["c2"], scratch["c3"])
    n = x_ap.shape[-1]
    sv.wait(v); sv.wait(s); sc.wait(v); sc.wait(s)
    sv.inc(v.tensor_reduce(c0[:], x_ap, AX.X, ALU.add))
    sc.inc(s.activation(sq[:, 0:n], x_ap, AF.Square))
    sc.wait(v)
    sv.inc(v.tensor_reduce(c1_[:], sq[:, 0:n], AX.X, ALU.add))
    sv.wait(v)
    sv.inc(v.tensor_scalar(c2[:], c0[:], 1.0 / n, None, ALU.mult))
    sv.inc(v.tensor_scalar(c1_[:], c1_[:], 1.0 / n, None, ALU.mult))
    sv.wait(v)
    sv.inc(v.tensor_tensor(c3[:], c2[:], c2[:], ALU.mult))
    sv.wait(v)
    sv.inc(v.tensor_tensor(c1_[:], c1_[:], c3[:], ALU.subtract))
    sv.wait(s)
    sc.inc(s.activation(c3[:], c1_[:], AF.Ln, bias=EPS_LN))
    sc.wait(s)
    sc.inc(s.activation(c3[:], c3[:], AF.Exp, scale=-0.5))

def emit_ln_f32(c, x_ap, out_ap, sc, sv, scratch):
    emit_ln_stats(c, x_ap, sc, sv, scratch)
    sc.wait(c.v)
    sv.inc(c.v.tensor_scalar(out_ap, x_ap, scratch["c2"][:], scratch["c3"][:],
                             ALU.subtract, ALU.mult))

def emit_transpose_tiles(c, src_bf, dst_bf, ident, ps_tr, sp, sv, nk):
    t, v = c.t, c.v
    for k in range(nk):
        sp.inc(t.transpose(ps_tr[:, k * 128:(k + 1) * 128],
                           src_bf[:, k * 128:(k + 1) * 128], ident[:]))
    sp.wait(v)
    sv.inc(v.tensor_copy(dst_bf[:, 0:nk * 128], ps_tr[:, 0:nk * 128]))

def emit_identity(c, ident, sg):
    g = c.g
    sg.inc(g.memset(ident[:], 0.0))
    sg.wait(g)
    sg.inc(g.affine_select(out=ident[:], in_=ident[:], compare_op=ALU.not_equal,
                           fill=1.0, base=0, pattern=[[-1, ident.shape[1]]],
                           channel_multiplier=1))

def emit_L0FFN_weights(c, W, name):
    g = c.g
    bwt = c.sb(name + "_bwt", [128, 8, 1024], BF16)
    catT = c.sb(name + "_catT", [128, 8, 118], BF16)
    catB = c.sb(name + "_catB", [112, 1024], BF16)
    baseb = c.sb(name + "_bb", [1, 1024], BF16)
    routerb = c.sb(name + "_rb", [1, 6], BF16)
    onesr = c.sb(name + "_ones", [1, 128], BF16)
    ident = c.sb(name + "_id", [128, 128], BF16)
    sw = c.sem(kind="sw")
    sg = c.sem()
    sw.inc(g.dma_start(out=bwt[:], in_=W["baseWT"].rearrange("(a p) n -> p a n", p=128)), 16)
    sw.inc(g.dma_start(out=catT[:], in_=W["catT"].rearrange("(a p) n -> p a n", p=128)), 16)
    sw.inc(g.dma_start(out=catB[:], in_=W["catB"]), 16)
    sw.inc(g.dma_start(out=baseb[:], in_=W["baseb"]), 16)
    sw.inc(g.dma_start(out=routerb[:], in_=W["routerb"]), 16)
    sg.inc(g.memset(onesr[:], 1.0))
    emit_identity(c, ident, sg)
    return dict(bwt=bwt, catT=catT, catB=catB, baseb=baseb, routerb=routerb,
                onesr=onesr, ident=ident, sw=sw, sg=sg)

def emit_phase_L0FFN(c, Xin, Xout, W, ntiles, name, wsem=None):
    nc = c.nc
    t, v, s, g, sy = c.t, c.v, c.s, c.g, c.sy

    if wsem is None:
        wstate = emit_L0FFN_weights(c, W, name)
    else:
        wstate = wsem
    bwt, catT, catB, baseb, routerb, onesr, ident = (
        wstate["bwt"], wstate["catT"], wstate["catB"], wstate["baseb"],
        wstate["routerb"], wstate["onesr"], wstate["ident"])
    sw, sg = wstate["sw"], wstate["sg"]

    xin = [c.sb(name + f"_xin{i}", [128, 1024], F32) for i in range(2)]
    xn = c.sb(name + "_xn", [128, 1024], BF16)
    xnT = c.sb(name + "_xnT", [128, 1024], BF16)
    Hw = c.sb(name + "_Hw", [128, 112], BF16)
    HwT = c.sb(name + "_HwT", [112, 128], BF16)
    gate = c.sb(name + "_gate", [128, 6], F32)
    gw = c.sb(name + "_gw", [128, 6], F32)
    delta = c.sb(name + "_delta", [128, 1024], F32)
    xout = [c.sb(name + f"_xout{i}", [128, 1024], F32) for i in range(2)]
    scratch2 = [{k: c.sb(name + f"_{k}{i}", [128, 1024] if k == "sq" else [128, 1], F32)
                 for k in ("sq", "c0", "c1", "c2", "c3")} for i in range(2)]
    scratch = scratch2[0]
    ps_tr = c.ps(name + "_pstr", [128, 1024], BF16)
    ps_h = c.ps(name + "_psh", [128, 128], F32)
    ps_o = [c.ps(name + f"_pso{i}", [128, 1024], F32) for i in range(2)]
    ps_t2 = c.ps(name + "_pst2", [128, 128], BF16)

    c.const(EPS_LN)
    c.sconst.wait(c.s)
    sx = [c.sem(kind="hw"), c.sem(kind="hw")]
    sst = [c.sem(kind="hw"), c.sem(kind="hw")]
    sp = c.sem()
    sv = c.sem()
    sc = c.sem()
    sdone = c.sem()

    sx[0].inc(sy.dma_start(out=xin[0][:], in_=Xin[0]), 16)
    if ntiles > 1:
        sx[1].inc(sy.dma_start(out=xin[1][:], in_=Xin[1]), 16)

    for ti in range(ntiles):
        xb = xin[ti % 2]
        ob = xout[ti % 2]
        po = ps_o[ti % 2]
        scratch = scratch2[ti % 2]
        if ti == 0:
            for eng in (v, s):
                eng.wait_ge(sx[0].h, sx[0].base + 16)
            emit_ln_stats(c, xb[:], sc, sv, scratch)
        sc.wait(v)
        sv.inc(v.tensor_scalar(xn[:], xb[:], scratch["c2"][:], scratch["c3"][:],
                               ALU.subtract, ALU.mult))
        sc.wait(t)
        if ti == 0:
            sg.wait(t)
            sw.wait(t)
        sv.wait(t)
        emit_transpose_tiles(c, xn, xnT, ident, ps_tr, sp, sv, 8)
        sv.wait(t)
        if ti >= 2:
            t.wait_ge(sdone.h, sdone.base + ti - 1)
        for k in range(8):
            lhs = xnT[:, k * 128:(k + 1) * 128]
            t.matmul(po[:, 0:512], lhs, bwt[:, k, 0:512], start=(k == 0), stop=False)
            t.matmul(po[:, 512:1024], lhs, bwt[:, k, 512:1024], start=(k == 0), stop=False)
            sp.inc(t.matmul(ps_h[:, 0:118], lhs, catT[:, k, :], start=(k == 0), stop=False))
        t.matmul(po[:, 0:512], onesr[:], baseb[:, 0:512], start=False, stop=False)
        t.matmul(po[:, 512:1024], onesr[:], baseb[:, 512:1024], start=False, stop=False)
        sp.inc(t.matmul(ps_h[:, 112:118], onesr[:], routerb[:], start=False, stop=True))
        if ti + 1 < ntiles:
            nb_ = (ti + 1) % 2
            for eng in (v, s):
                eng.wait_ge(sx[nb_].h, sx[nb_].base + 16 * ((ti + 1) // 2 + 1))
            emit_ln_stats(c, xin[nb_][:], sc, sv, scratch2[nb_])
        sp.wait(v)
        sv.inc(v.tensor_reduce(scratch["c0"][:], ps_h[:, 112:118], AX.X, ALU.max))
        sv.wait(v)
        sv.inc(v.tensor_scalar(scratch["c1"][:], scratch["c0"][:], -1.0, None, ALU.mult))
        sv.wait(s)
        sp.wait(s)
        sc.inc(s.activation(gate[:], ps_h[:, 112:118], AF.Exp, bias=scratch["c1"][:]))
        sc.wait(v)
        sv.inc(v.tensor_reduce(scratch["c0"][:], gate[:], AX.X, ALU.add))
        sv.wait(v)
        sv.inc(v.reciprocal(scratch["c1"][:], scratch["c0"][:]))
        sv.wait(v)
        sv.inc(v.tensor_scalar(gate[:], gate[:], scratch["c1"][:], None, ALU.mult))
        sv.wait(v)
        sv.inc(v.tensor_reduce(scratch["c0"][:], gate[:], AX.X, ALU.max))
        sv.wait(v)
        sv.inc(v.tensor_scalar(gw[:], gate[:], scratch["c0"][:], None, ALU.is_ge))
        sv.wait(v)
        sv.inc(v.scalar_tensor_tensor(gw[:], gw[:], -1e30, gate[:], ALU.mult, ALU.add))
        sv.wait(v)
        sv.inc(v.tensor_reduce(scratch["c1"][:], gw[:], AX.X, ALU.max))
        sv.wait(v)
        sv.inc(v.tensor_scalar(gw[:], gate[:], scratch["c1"][:], None, ALU.is_ge))
        sv.wait(v)
        sv.inc(v.tensor_tensor(gw[:], gw[:], gate[:], ALU.mult))
        sv.wait(v)
        sv.inc(v.tensor_reduce(scratch["c2"][:], gw[:], AX.X, ALU.add))
        sv.wait(v)
        sv.inc(v.tensor_scalar(scratch["c2"][:], scratch["c2"][:], 1e-6, None, ALU.add))
        sv.wait(v)
        sv.inc(v.reciprocal(scratch["c2"][:], scratch["c2"][:]))
        sv.wait(v)
        sv.inc(v.tensor_scalar(gw[:], gw[:], scratch["c2"][:], None, ALU.mult))
        sv.wait(v)
        sv.inc(v.tensor_copy(Hw[:, 0:16], ps_h[:, 0:16]))
        for e in range(6):
            sv.inc(v.tensor_scalar(Hw[:, 16 + 16 * e:32 + 16 * e],
                                   ps_h[:, 16 + 16 * e:32 + 16 * e],
                                   gw[:, e:e + 1], None, ALU.mult))
        sv.wait(t)
        sp.inc(t.transpose(ps_t2[0:112, :], Hw[:], ident[:]))
        sp.wait(v)
        sv.inc(v.tensor_copy(HwT[:], ps_t2[0:112, :]))
        sv.wait(t)
        sp.inc(t.matmul(po[:, 0:512], HwT[:], catB[:, 0:512], start=False, stop=True))
        sp.inc(t.matmul(po[:, 512:1024], HwT[:], catB[:, 512:1024], start=False, stop=True))
        sp.wait(s)
        sc.inc(s.activation(delta[:], po[:], AF.Copy, scale=0.5))
        sc.wait(v)
        if ti >= 2:
            v.wait_ge(sst[ti % 2].h, sst[ti % 2].base + 16 * (ti // 2))
        sdone.inc(v.tensor_add(ob[:], delta[:], xb[:]))
        sy.wait_ge(sdone.h, sdone.base + ti + 1)
        sst[ti % 2].inc(sy.dma_start(out=Xout[ti], in_=ob[:]), 16)
        if ti + 2 < ntiles:
            sy.wait_ge(sdone.h, sdone.base + ti + 1)
            sx[ti % 2].inc(sy.dma_start(out=xin[ti % 2][:], in_=Xin[ti + 2]), 16)
    return dict(sst=sst, n=ntiles, wstate=wstate)

def _prep_weights_A(inputs):
    baseWT = np.ascontiguousarray(inputs["L0_ffn_base_W"].T)
    baseb = np.ascontiguousarray(inputs["L0_ffn_base_b"][None, :])
    routedA = inputs["L0_ffn_routed_A"].reshape(96, 1024)
    cat = np.concatenate([inputs["L0_ffn_shared_A"], routedA, inputs["L0_ffn_router_W"]], 0)
    catT = np.ascontiguousarray(cat.T)
    routedB = np.ascontiguousarray(np.transpose(inputs["L0_ffn_routed_B"], (0, 2, 1))).reshape(96, 1024)
    catB = np.ascontiguousarray(np.concatenate([inputs["L0_ffn_shared_B"].T, routedB], 0))
    routerb = np.ascontiguousarray(inputs["L0_ffn_router_b"][None, :])
    return dict(baseWT=baseWT, baseb=baseb, catT=catT, catB=catB, routerb=routerb)

def shard_stream(x):
    shards = []
    for cid in range(NCORES):
        b, h = cid // 2, cid % 2
        main = x[b, h * NTOK:(h + 1) * NTOK].reshape(NT, 128, D)
        halo = np.zeros((1, 128, D), x.dtype)
        if h == 1:
            halo[0, 0:3] = x[b, NTOK - 3:NTOK]
        shards.append(np.ascontiguousarray(np.concatenate([main, halo], 0)))
    return shards

A_W_SHAPES = dict(baseWT=(1024, 1024), baseb=(1, 1024), catT=(1024, 118),
                  catB=(112, 1024), routerb=(1, 6))

def build_A(ntiles=NT + 1):
    nc = bass.Bass("TRN2", target_bir_lowering=False)
    X = nc.dram_tensor("X", [ntiles, 128, D], F32, kind="ExternalInput").ap()
    XO = nc.dram_tensor("XO", [ntiles, 128, D], F32, kind="ExternalOutput").ap()
    Wap = {k: nc.dram_tensor("W_" + k, list(s), F32, kind="ExternalInput").ap()
           for k, s in A_W_SHAPES.items()}
    es = ExitStack()
    c = Ctx(nc, es)
    r = emit_phase_L0FFN(c, X, XO, Wap, ntiles, "A")
    for smm in r["sst"]:
        smm.wait(c.sy)
    es.close()
    return nc



B_W_SHAPES = dict(
    ipT=(1024, 4384),
    opT=(2048, 1024),
    lnw=(1, 1024), lnb=(1, 1024),
    convw=(2304, 4), convb=(128, 18),
    dtbias=(1, 32), alog=(1, 32), dvec=(1, 32),
    normw=(1, 2048),
    hmask=(128, 8),
    halom=(128, 1),
)
NCH = 32
B_SCRATCH = dict(u0=([33, 128, 1024], F32), xnT=([33, 128, 1024], BF16),
                 raw=([18, 128, 4099], F32), gate=([32, 128, 2048], F32),
                 act=([18, 128, 4096], BF16), y=([32, 128, 2048], F32),
                 ccin=([128, 2048], F32), ccout=([1024, 2048], F32))

def emit_phase_mamba(c, Xin, Xout, W, Ds, name="B", hgather=None, want_u0=True):
    nc = c.nc
    t, v, s, g, sy = c.t, c.v, c.s, c.g, c.sy
    u0d, xntd, rawd, gated, actd, yd = (Ds["u0"], Ds["xnT"], Ds["raw"],
                                        Ds["gate"], Ds["act"], Ds["y"])
    ccin, ccout = Ds["ccin"], Ds["ccout"]

    c.const(EPS_LN); c.const(1.0)
    c.sconst.wait(s)

    es_ip = ExitStack(); _sv = c.es; c.es = es_ip
    ipT = c.sb(name + "_ipT", [128, 8, 4384], BF16, side="right")
    c.es = _sv
    es_b0 = ExitStack(); _sv = c.es; c.es = es_b0
    onesr_dummy = None
    c.es = _sv
    onesr = c.sb(name + "_ones", [1, 128], BF16)
    onesrf = c.sb(name + "_onesf", [1, 128], F32)
    onescf = c.sb(name + "_onesc", [128, 1], F32)
    ident = c.sb(name + "_id", [128, 128], BF16)
    identf = c.sb(name + "_idf", [128, 128], F32)
    lmask = c.sb(name + "_lm", [128, 128], F32)
    _sv = c.es; c.es = es_b0
    lnw_bc = c.sb(name + "_lnw", [128, 1024], F32, side="right")
    lnb_bc = c.sb(name + "_lnb", [128, 1024], F32, side="right")
    rowtmp = c.sb(name + "_rowt", [1, 4224], F32, side="right")
    c.es = _sv
    normw_bc = c.sb(name + "_nw", [128, 2048], F32)
    dtb_bc = c.sb(name + "_dtb", [128, 32], F32)
    negA_bc = c.sb(name + "_negA", [128, 32], F32)
    D_bc = c.sb(name + "_Dbc", [128, 32], F32)
    convw_sb = c.sb(name + "_cw", [128, 18, 4], F32)
    convb_sb = c.sb(name + "_cb", [128, 18], F32)
    hmask_sb = c.sb(name + "_hm", [128, 8], F32)
    halom_sb = c.sb(name + "_ham", [128, 1], F32)
    dt_all = c.sb(name + "_dt", [128, NCH * 32], F32)
    s_all = c.sb(name + "_sa", [128, NCH * 32], F32)
    atot = c.sb(name + "_at", [1, NCH * 32], F32)
    pfx = c.sb(name + "_pfx", [1, NCH * 32], F32)
    H = [c.sb(name + f"_H{i}", [128, 2048], F32) for i in range(2)]
    Hb16 = c.sb(name + "_Hb", [128, 2048], BF16)
    Hinitb = c.sb(name + "_Hib", [128, 2048], BF16)

    ps_bc = c.ps(name + "_psbc", [128, 512], F32)
    ps_r = [c.ps(name + f"_psr{i}", [128, 512], F32) for i in range(2)]
    ps_tr = c.ps(name + "_pstr", [128, 1024], BF16)
    ps_big = c.ps(name + "_psbig", [128, 2048], F32)

    sg = c.sem(); sp = c.sem(); sv = c.sem(); sc = c.sem()
    sw = c.sem(kind="sw")
    sq = [c.sem(kind="hw") for _ in range(2)]
    so = [c.sem(kind="sw") for _ in range(4)]
    scc = c.sem(kind="cc")

    sw.inc(g.dma_start(out=ipT[:], in_=W["ipT"].rearrange("(a p) n -> p a n", p=128)), 16)
    sw.inc(g.dma_start(out=convw_sb[:], in_=W["convw"].rearrange("(m p) j -> p m j", p=128)), 16)
    sw.inc(g.dma_start(out=convb_sb[:], in_=W["convb"]), 16)
    sw.inc(g.dma_start(out=hmask_sb[:], in_=W["hmask"]), 16)
    sw.inc(g.dma_start(out=halom_sb[:], in_=W["halom"]), 16)
    sw.inc(g.dma_start(out=rowtmp[0:1, 0:1024], in_=W["lnw"]), 16)
    sw.inc(g.dma_start(out=rowtmp[0:1, 1024:2048], in_=W["lnb"]), 16)
    sw.inc(g.dma_start(out=rowtmp[0:1, 2048:2080], in_=W["dtbias"]), 16)
    sw.inc(g.dma_start(out=rowtmp[0:1, 2080:2112], in_=W["alog"]), 16)
    sw.inc(g.dma_start(out=rowtmp[0:1, 2112:2144], in_=W["dvec"]), 16)
    sw.inc(g.dma_start(out=rowtmp[0:1, 2144:4192], in_=W["normw"]), 16)
    sg.inc(g.memset(onesr[:], 1.0))
    sg.inc(g.memset(onesrf[:], 1.0))
    sg.inc(g.memset(onescf[:], 1.0))
    emit_identity(c, ident, sg)
    emit_identity(c, identf, sg)
    sg.inc(g.memset(lmask[:], 1.0))
    sg.wait(g)
    sg.inc(g.affine_select(out=lmask[:], in_=lmask[:], compare_op=ALU.is_ge,
                           fill=0.0, base=0, pattern=[[1, 128]], channel_multiplier=-1))
    sg.inc(g.memset(pfx[0:1, 0:32], 0.0))
    if hgather is None:
        sg.inc(g.memset(Hb16[:], 0.0))
        sg.inc(g.memset(H[1][:], 0.0))
    else:
        sw.wait(v)
        es_h = ExitStack(); _svh = c.es; c.es = es_h
        hsl0 = _Ring(c, name + "hg", 2, [128, 2048], F32, kind="hw")
        c.es = _svh
        sv0 = c.sem()
        for j in range(8):
            if j >= 2:
                sy.wait_ge(sv0.h, sv0.n)
            hsl0.dma(j, sy.dma_start(out=hsl0.buf(j)[:], in_=hgather[j]))
            hsl0.wait_loaded(v, j)
            dst, src_ = (H[1], H[0]) if j % 2 == 0 else (H[0], H[1])
            if j == 0:
                sv0.inc(v.tensor_scalar(H[1][:], hsl0.buf(0)[:], hmask_sb[:, 0:1], None, ALU.mult))
            else:
                sv0.wait(v)
                sv0.inc(v.scalar_tensor_tensor(dst[:], hsl0.buf(j)[:], hmask_sb[:, j:j + 1],
                                               src_[:], ALU.mult, ALU.add))
        sv0.wait(v)
        sv0.inc(v.tensor_copy(H[1][:], H[0][:]))
        sv0.wait(v)
        sv0.inc(v.tensor_copy(Hb16[:], H[1][:]))
        sv0.wait(t); sv0.wait(s); sv0.wait(g); sv0.wait(sy)
        es_h.close()

    sw.wait(s)
    sc.inc(s.activation(rowtmp[0:1, 2080:2112], rowtmp[0:1, 2080:2112], AF.Exp))

    def bcast(dst, col0, ncols, scale=None):
        for j0 in range(0, ncols, 512):
            w_ = min(512, ncols - j0)
            sv.wait(t); sc.wait(t)
            sp.inc(t.matmul(ps_bc[:, 0:w_], onesrf[:], rowtmp[0:1, col0 + j0:col0 + j0 + w_],
                            start=True, stop=True))
            sp.wait(v)
            if scale is None:
                sv.inc(v.tensor_copy(dst[:, j0:j0 + w_], ps_bc[:, 0:w_]))
            else:
                sv.inc(v.tensor_scalar(dst[:, j0:j0 + w_], ps_bc[:, 0:w_], scale, None, ALU.mult))
    sw.wait(t); sg.wait(t); sw.wait(v)
    bcast(lnw_bc, 0, 1024)
    bcast(lnb_bc, 1024, 1024)
    bcast(dtb_bc, 2048, 32)
    bcast(negA_bc, 2080, 32, scale=-1.0)
    bcast(D_bc, 2112, 32)
    bcast(normw_bc, 2144, 2048)

    scratch = {k: c.sb(name + "_" + k, [128, 2048] if k == "sq" else [128, 1], F32)
               for k in ("sq", "c0", "c1", "c2", "c3", "c4", "c5")}

    from contextlib import ExitStack as _ES
    es0 = _ES()
    c0ctx = c.es; c.es = es0
    xin = [c.sb(name + f"_b0x{i}", [128, 1024], F32) for i in range(2)]
    u0t = [c.sb(name + f"_b0u{i}", [128, 1024], F32) for i in range(2)]
    xc2 = c.sb(name + "_b0xc", [128, 1024], F32)
    xn1 = c.sb(name + "_b0n1", [128, 1024], F32)
    xnb = c.sb(name + "_b0xn", [128, 1024], BF16)
    xnT_t = [c.sb(name + f"_b0t{i}", [128, 1024], BF16) for i in range(2)]
    c.es = c0ctx

    for ti in range(33):
        bi = ti % 2
        if ti < 2:
            sq[bi].inc(sy.dma_start(out=xin[bi][:], in_=Xin[ti]), 16)
        v.wait_ge(sq[bi].h, sq[bi].base + 16 * (ti // 2 + 1))
        s.wait_ge(sq[bi].h, sq[bi].base + 16 * (ti // 2 + 1))
        emit_ln_f32(c, xin[bi][:], u0t[bi][:], sc, sv, scratch)
        if want_u0:
            sv.wait(g)
            so[bi].inc(g.dma_start(out=u0d[ti], in_=u0t[bi][:]), 16)
        emit_ln_stats(c, u0t[bi][:], sc, sv, scratch)
        sc.wait(v)
        sv.inc(v.tensor_scalar(xc2[:], u0t[bi][:], scratch["c2"][:], scratch["c3"][:],
                               ALU.subtract, ALU.mult))
        sv.wait(v)
        sv.inc(v.tensor_tensor(xn1[:], xc2[:], lnw_bc[:], ALU.mult))
        sv.wait(v)
        sv.inc(v.tensor_tensor(xnb[:], xn1[:], lnb_bc[:], ALU.add))
        sv.wait(t)
        emit_transpose_tiles(c, xnb, xnT_t[bi], ident, ps_tr, sp, sv, 8)
        sv.wait(g)
        so[2 + bi].inc(g.dma_start(out=xntd[ti], in_=xnT_t[bi][:]), 16)
        if ti + 2 < 33:
            sy.wait_ge(sv.h, sv.n)
            sy.wait_ge(so[bi].h, so[bi].n)
            sy.wait_ge(so[2 + bi].h, so[2 + bi].n)
            sq[bi].inc(sy.dma_start(out=xin[bi][:], in_=Xin[ti + 2]), 16)
    es0.close()
    env = dict(ipT=ipT, onesr=onesr, onesrf=onesrf, onescf=onescf,
               ident=ident, identf=identf, lmask=lmask,
               lnw_bc=lnw_bc, lnb_bc=lnb_bc, normw_bc=normw_bc,
               dtb_bc=dtb_bc, negA_bc=negA_bc, D_bc=D_bc,
               convw_sb=convw_sb, convb_sb=convb_sb, hmask_sb=hmask_sb,
               halom_sb=halom_sb, dt_all=dt_all, s_all=s_all, atot=atot, pfx=pfx,
               H=H, Hb16=Hb16, Hinitb=Hinitb, Wop=W["opT"],
               ps_bc=ps_bc, ps_r=ps_r, ps_tr=ps_tr, ps_big=ps_big,
               sw=sw, sg=sg, sp=sp, sv=sv, sc=sc, scc=scc,
               xntd=xntd, rawd=rawd, gated=gated, actd=actd, yd=yd, u0d=u0d,
               ccin=ccin, ccout=ccout, es_ip=es_ip, es_b0=es_b0,
               b0sems=list(sq) + list(so))
    return env

class _Ring:
    def __init__(self, c, name, n, shape, dtype, kind="hw"):
        self.c = c
        self.bufs = [c.sb(f"{name}{i}", shape, dtype) for i in range(n)]
        self.sems = [c.sem(kind=kind) for _ in range(n)]
        self.cnt = [0] * n
        self.n = n
    def buf(self, i):
        return self.bufs[i % self.n]
    def sem(self, i):
        return self.sems[i % self.n]
    def wait_loaded(self, eng, i):
        sm = self.sems[i % self.n]
        eng.wait_ge(sm.h, sm.base + 16 * (i // self.n + 1))
    def dma(self, i, instr):
        self.sems[i % self.n].inc(instr, 16)

def _emit_mamba_b1(c, env, want_z=True):
    t, v, s, g, sy = c.t, c.v, c.s, c.g, c.sy
    ipT, onesr = env["ipT"], env["onesr"]
    ps_big, ps_bc = env["ps_big"], env["ps_bc"]
    sp, sv, sc = env["sp"], env["sv"], env["sc"]
    xntd, rawd, gated = env["xntd"], env["rawd"], env["gated"]
    dt_all = env["dt_all"]
    es = ExitStack(); sv_es = c.es; c.es = es
    xg = _Ring(c, "b1xg", 2, [128, 4, 1024], BF16, kind="hw")
    stg = _Ring(c, "b1st", 4, [128, 512], F32, kind="sw")
    xh = c.sb("b1xh", [128, 1024], BF16)
    c.es = sv_es

    sqh = c.sem(kind="hw")
    sqh.inc(sy.dma_start(out=xh[:], in_=xntd[32]), 16)
    sqh.wait(t)
    hst = stg.buf(0)
    nmm = 0
    for m in range(18):
        pb = ps_big[:, 0:512] if (m % 2 == 0) else ps_big[:, 512:1024]
        if m >= 2:
            t.wait_ge(sv.h, sv.n)
        for k in range(8):
            ins = t.matmul(pb[:, 0:128], ipT[:, k, 2048 + 128 * m:2048 + 128 * (m + 1)],
                           xh[:, k * 128:(k + 1) * 128],
                           start=(k == 0), stop=(k == 7))
        sp.inc(ins)
        sp.wait(v)
        sv.inc(v.tensor_copy(hst[:, m * 3:m * 3 + 3], pb[:, 0:3]))
    sv.wait(g)
    stg.dma(0, g.dma_start(out=rawd[:, :, 0:3].rearrange("m p n -> p m n"),
                           in_=hst[:, 0:54].rearrange("p (m n) -> p m n", n=3)))

    si = 1
    for gi in range(8):
        bi = gi % 2
        if gi < 2:
            xg.dma(gi, sy.dma_start(out=xg.buf(gi)[:],
                                    in_=xntd[4 * gi:4 * gi + 4].rearrange("t p n -> p t n")))
        xg.wait_loaded(t, gi)
        for m in range(18):
            pb = ps_big[:, 0:512] if (m % 2 == 0) else ps_big[:, 512:1024]
            t.wait_ge(sv.h, sv.n)
            for k in range(8):
                ins = t.matmul(pb[:], ipT[:, k, 2048 + 128 * m:2048 + 128 * (m + 1)],
                               xg.buf(gi)[:, :, k * 128:(k + 1) * 128],
                               start=(k == 0), stop=(k == 7))
            sp.inc(ins)
            sp.wait(v)
            sb_ = stg.buf(si); sm = stg.sem(si)
            v.wait_ge(sm.h, sm.base + 16 * (si // 4))
            sv.inc(v.tensor_copy(sb_[:], pb[:]))
            sv.wait(g)
            stg.dma(si, g.dma_start(out=rawd[m, :, 3 + 512 * gi:3 + 512 * (gi + 1)], in_=sb_[:]))
            si += 1
        for tl in range(4 if want_z else 0):
            for nb in range(4):
                pz = ps_big[:, 1024:1536] if ((tl * 4 + nb) % 2 == 0) else ps_big[:, 1536:2048]
                t.wait_ge(sc.h, sc.n)
                for k in range(8):
                    ins = t.matmul(pz[:], xg.buf(gi)[:, tl, k * 128:(k + 1) * 128],
                                   ipT[:, k, 512 * nb:512 * (nb + 1)],
                                   start=(k == 0), stop=(k == 7))
                sp.inc(ins)
                sp.wait(s)
                sb_ = stg.buf(si); sm = stg.sem(si)
                s.wait_ge(sm.h, sm.base + 16 * (si // 4))
                sc.inc(s.activation(sb_[:], pz[:], AF.Silu))
                sc.wait(g)
                stg.dma(si, g.dma_start(out=gated[4 * gi + tl, :, 512 * nb:512 * (nb + 1)], in_=sb_[:]))
                si += 1
        for tl in range(4):
            t.wait_ge(sv.h, sv.n)
            for k in range(8):
                ins = t.matmul(ps_bc[:, 192:224], xg.buf(gi)[:, tl, k * 128:(k + 1) * 128],
                               ipT[:, k, 4352:4384], start=(k == 0), stop=(k == 7))
            sp.inc(ins)
            sp.wait(v)
            ti_ = 4 * gi + tl
            sv.inc(v.tensor_copy(dt_all[:, 32 * ti_:32 * (ti_ + 1)], ps_bc[:, 192:224]))
        if gi + 2 < 8:
            sy.wait_ge(sp.h, sp.n)
            xg.dma(gi + 2, sy.dma_start(out=xg.buf(gi)[:],
                                        in_=xntd[4 * (gi + 2):4 * (gi + 2) + 4].rearrange("t p n -> p t n")))
    for i in range(4):
        stg.sems[i].wait(g)
    es.close()
    return list(xg.sems) + list(stg.sems) + [sqh]

def _emit_mamba_conv(c, env):
    t, v, s, g, sy = c.t, c.v, c.s, c.g, c.sy
    sp, sv, sc = env["sp"], env["sv"], env["sc"]
    rawd, actd = env["rawd"], env["actd"]
    convw_sb, convb_sb, halom_sb = env["convw_sb"], env["convb_sb"], env["halom_sb"]
    es = ExitStack(); sv_es = c.es; c.es = es
    ext = _Ring(c, "cvext", 2, [128, 18, 515], F32, kind="hw")
    ast = _Ring(c, "cvast", 2, [128, 18 * 512], BF16, kind="sw")
    acc = [c.sb(f"cvacc{i}", [128, 512], F32) for i in range(2)]
    c.es = sv_es
    NCG = NCH // 4
    for cg in range(NCG):
        bi = cg % 2
        if cg < 2:
            ext.dma(cg, sy.dma_start(out=ext.buf(cg)[:],
                                     in_=rawd[:, :, 512 * cg:512 * cg + 515].rearrange("m p n -> p m n")))
        ext.wait_loaded(v, cg)
        if cg == 0:
            sv.inc(v.tensor_scalar(ext.buf(0)[:, :, 0:3], ext.buf(0)[:, :, 0:3],
                                   halom_sb[:, 0:1], None, ALU.mult))
            sv.wait(v)
        for m in range(18):
            sv.wait(v); sc.wait(v)
            sv.inc(v.tensor_scalar(acc[0][:], ext.buf(cg)[:, m, 0:512],
                                   convw_sb[:, m, 0:1], None, ALU.mult))
            sv.wait(v)
            sv.inc(v.scalar_tensor_tensor(acc[1][:], ext.buf(cg)[:, m, 1:513],
                                          convw_sb[:, m, 1:2], acc[0][:], ALU.mult, ALU.add))
            sv.wait(v)
            sv.inc(v.scalar_tensor_tensor(acc[0][:], ext.buf(cg)[:, m, 2:514],
                                          convw_sb[:, m, 2:3], acc[1][:], ALU.mult, ALU.add))
            sv.wait(v)
            sv.inc(v.scalar_tensor_tensor(acc[1][:], ext.buf(cg)[:, m, 3:515],
                                          convw_sb[:, m, 3:4], acc[0][:], ALU.mult, ALU.add))
            sv.wait(s)
            ab = ast.buf(cg)
            if m == 0:
                sm_ = ast.sem(cg)
                s.wait_ge(sm_.h, sm_.base + 16 * (cg // 2))
            sc.inc(s.activation(ab[:, m * 512:(m + 1) * 512], acc[1][:], AF.Silu,
                                bias=convb_sb[:, m:m + 1]))
        sc.wait(g)
        ast.dma(cg, g.dma_start(out=actd[:, :, 512 * cg:512 * (cg + 1)].rearrange("m p n -> p m n"),
                                in_=ast.buf(cg)[:].rearrange("p (m n) -> p m n", n=512)))
        if cg + 2 < NCG:
            sy.wait_ge(sc.h, sc.n)
            sy.wait_ge(sv.h, sv.n)
            ext.dma(cg + 2, sy.dma_start(out=ext.buf(cg)[:],
                                         in_=rawd[:, :, 512 * (cg + 2):512 * (cg + 2) + 515].rearrange("m p n -> p m n")))
    for i in range(2):
        ast.sems[i].wait(g)
    es.close()
    return list(ext.sems) + list(ast.sems)

def _emit_mamba_dt(c, env):
    t, v, s, g = c.t, c.v, c.s, c.g
    sp, sv, sc = env["sp"], env["sv"], env["sc"]
    dt_all, s_all, atot, pfx = env["dt_all"], env["s_all"], env["atot"], env["pfx"]
    dtb_bc, negA_bc, lmask = env["dtb_bc"], env["negA_bc"], env["lmask"]
    onescf, ps_bc = env["onescf"], env["ps_bc"]
    es = ExitStack(); sv_es = c.es; c.es = es
    ta = c.sb("dtta", [128, 32], F32)
    tb = c.sb("dttb", [128, 32], F32)
    tc_ = c.sb("dttc", [128, 32], F32)
    td = c.sb("dttd", [128, 32], F32)
    c.es = sv_es
    for cc in range(NCH):
        sl = slice(32 * cc, 32 * (cc + 1))
        sv.wait(v); sc.wait(v)
        sv.inc(v.tensor_tensor(ta[:], dt_all[:, sl], dtb_bc[:], ALU.add))
        sv.wait(s)
        sc.inc(s.activation(tb[:], ta[:], AF.Abs))
        sc.wait(s)
        sc.inc(s.activation(tc_[:], tb[:], AF.Exp, scale=-1.0))
        sc.wait(s)
        sc.inc(s.activation(tb[:], tc_[:], AF.Ln, bias=1.0))
        sc.wait(s)
        sc.inc(s.activation(tc_[:], ta[:], AF.Relu))
        sc.wait(v)
        sv.inc(v.tensor_tensor(dt_all[:, sl], tb[:], tc_[:], ALU.add))
        sv.wait(v)
        sv.inc(v.tensor_tensor(td[:], dt_all[:, sl], negA_bc[:], ALU.mult))
        sv.wait(t)
        sp.inc(t.matmul(ps_bc[:, 224:256], lmask[:], td[:], start=True, stop=True))
        sp.inc(t.matmul(ps_bc[0:1, 256:288], onescf[:], td[:], start=True, stop=True))
        sp.wait(v)
        sv.inc(v.tensor_copy(s_all[:, sl], ps_bc[:, 224:256]))
        sv.inc(v.tensor_copy(atot[0:1, sl], ps_bc[0:1, 256:288]))
        sv.wait(v)
        if cc + 1 < NCH:
            sv.inc(v.tensor_tensor(pfx[0:1, 32 * (cc + 1):32 * (cc + 2)],
                                   pfx[0:1, sl], atot[0:1, sl], ALU.add))
    es.close()
    return []


def _scan_state_update(c, env, cc, sl, L):
    t, v, s = c.t, c.v, c.s
    sp, sv, sc = env["sp"], env["sv"], env["sc"]
    ps_bc, ps_big = env["ps_bc"], env["ps_big"]
    onesrf = env["onesrf"]
    drow, dfac, b_tm, xtil, Htmp = L["drow"], L["dfac"], L["b_tm"], L["xtil"], L["Htmp"]
    hprev, hcur = L["hprev"], L["hcur"]
    Hb16 = env["Hb16"]
    sc.wait(t)
    sp.inc(t.matmul(ps_bc[:, 128:160], onesrf[:], drow[:], start=True, stop=True))
    sp.wait(v)
    sv.inc(v.tensor_copy(dfac[:], ps_bc[:, 128:160]))
    sv.wait(t)
    for j in range(4):
        sp.inc(t.matmul(ps_big[:, 512 * j:512 * (j + 1)], b_tm[:],
                        xtil[:, 512 * j:512 * (j + 1)], start=True, stop=True))
    sp.wait(v)
    sv.inc(v.tensor_tensor(Htmp[:].rearrange("p (h d) -> p h d", d=64),
                           hprev[:].rearrange("p (h d) -> p h d", d=64),
                           dfac[:].unsqueeze(2).broadcast_to([128, 32, 64]), ALU.mult))
    sv.wait(v)
    sv.inc(v.tensor_add(hcur[:], Htmp[:], ps_big[:]))
    sv.wait(v)
    sv.inc(v.tensor_copy(Hb16[:], hcur[:]))

def _emit_mamba_scan(c, env, want_y=True):
    t, v, s, g, sy = c.t, c.v, c.s, c.g, c.sy
    sp, sv, sc = env["sp"], env["sv"], env["sc"]
    actd, yd = env["actd"], env["yd"]
    s_all, dt_all, atot = env["s_all"], env["dt_all"], env["atot"]
    lmask, ident, identf = env["lmask"], env["ident"], env["identf"]
    onesrf, D_bc = env["onesrf"], env["D_bc"]
    H, Hb16 = env["H"], env["Hb16"]
    ps_bc, ps_r, ps_tr, ps_big = env["ps_bc"], env["ps_r"], env["ps_tr"], env["ps_big"]

    es = ExitStack(); sv_es = c.es; c.es = es
    actl = _Ring(c, "b3act", 2, [128, 18, 128], BF16, kind="hw")
    yst = _Ring(c, "b3yst", 2, [128, 2048], F32, kind="sw")
    x_tm = c.sb("b3xtm", [128, 2048], BF16)
    b_tm = c.sb("b3btm", [128, 128], BF16)
    xtil = c.sb("b3xtl", [128, 2048], BF16)
    gg = c.sb("b3g", [128, 128], F32)
    drow = c.sb("b3dr", [1, 32], F32)
    colb4 = [c.sb(f"b3cb{i}", [128, 512], F32) for i in range(2)]
    onesbig = c.sb("b3ob", [128, 512], F32)
    Es = c.sb("b3es", [128, 32], F32)
    e2 = c.sb("b3e2", [128, 32], F32)
    dfac = c.sb("b3df", [128, 32], F32)
    ehead = [c.sb(f"b3eh{i}", [128, 512], F32) for i in range(2)]
    ehead2 = [c.sb(f"b3eh2{i}", [128, 512], F32) for i in range(2)]
    mhead = [c.sb(f"b3mh{i}", [128, 512], BF16) for i in range(2)]
    Htmp = c.sb("b3htmp", [128, 2048], F32)
    ya = c.sb("b3ya", [128, 2048], F32)
    tmp32 = c.sb("b3t32", [128, 32], F32)
    c.es = sv_es
    sgo = c.sem()
    sgo.inc(c.g.memset(onesbig[:], 1.0))
    sgo.wait(c.v)

    for cc in range(NCH):
        sl = slice(32 * cc, 32 * (cc + 1))
        hprev = H[(cc + 1) % 2]
        hcur = H[cc % 2]
        if cc < 2:
            actl.dma(cc, sy.dma_start(out=actl.buf(cc)[:],
                                      in_=actd[:, :, 128 * cc:128 * (cc + 1)].rearrange("m p n -> p m n")))
        A = actl.buf(cc)
        sv.wait(s); sp.wait(s)
        if want_y:
            sc.inc(s.activation(Es[:], s_all[:, sl], AF.Exp))
        sv.wait(t); sc.wait(t)
        sp.inc(t.matmul(ps_bc[:, 128:160], onesrf[:], atot[0:1, sl], start=True, stop=True))
        sp.wait(v)
        sv.inc(v.tensor_tensor(tmp32[:], ps_bc[:, 128:160], s_all[:, sl], ALU.subtract))
        sv.wait(s)
        sc.inc(s.activation(tmp32[:], tmp32[:], AF.Exp))
        sc.wait(v)
        sv.inc(v.tensor_tensor(e2[:], tmp32[:], dt_all[:, sl], ALU.mult))
        sc.inc(s.activation(drow[:], atot[0:1, sl], AF.Exp))
        actl.wait_loaded(t, cc)
        for r in range(2):
            sv.wait(t)
            for j in range(8):
                sp.inc(t.transpose(ps_tr[:, j * 128:(j + 1) * 128],
                                   A[:, 8 * r + j, :], ident[:]))
            sp.wait(v)
            sv.inc(v.tensor_copy(x_tm[:, 1024 * r:1024 * (r + 1)], ps_tr[:]))
        sv.wait(t)
        sp.inc(t.transpose(ps_tr[:, 0:128], A[:, 16, :], ident[:]))
        sp.wait(v)
        sv.inc(v.tensor_copy(b_tm[:], ps_tr[:, 0:128]))
        sv.wait(v)
        sv.inc(v.tensor_tensor(xtil[:].rearrange("p (h d) -> p h d", d=64),
                               x_tm[:].rearrange("p (h d) -> p h d", d=64),
                               e2[:].unsqueeze(2).broadcast_to([128, 32, 64]), ALU.mult))
        if not want_y:
            _scan_state_update(c, env, cc, sl, locals())
            if cc + 2 < NCH:
                sy.wait_ge(sp.h, sp.n)
                actl.dma(cc + 2, sy.dma_start(out=actl.buf(cc)[:],
                                              in_=actd[:, :, 128 * (cc + 2):128 * (cc + 3)].rearrange("m p n -> p m n")))
            continue
        sv.wait(t)
        sp.inc(t.matmul(ps_bc[:, 0:128], A[:, 16, :], A[:, 17, :], start=True, stop=True))
        sp.wait(v)
        sv.inc(v.tensor_tensor(gg[:], ps_bc[:, 0:128], lmask[:], ALU.mult))
        sv.wait(t)
        for j in range(4):
            sp.inc(t.matmul(ps_big[:, 512 * j:512 * (j + 1)], A[:, 17, :],
                            Hb16[:, 512 * j:512 * (j + 1)], start=True, stop=True))
        sp.wait(v)
        ybuf = yst.buf(cc)
        v.wait_ge(yst.sem(cc).h, yst.sem(cc).base + 16 * (cc // 2))
        sv.inc(v.tensor_tensor(ybuf[:].rearrange("p (h d) -> p h d", d=64),
                               ps_big[:].rearrange("p (h d) -> p h d", d=64),
                               Es[:].unsqueeze(2).broadcast_to([128, 32, 64]), ALU.mult))
        sv.wait(t)
        for grp in range(8):
            par = grp % 2
            pr = ps_r[par]
            hsl_ = slice(32 * cc + 4 * grp, 32 * cc + 4 * grp + 4)
            s_v = s_all[:, hsl_].unsqueeze(2).broadcast_to([128, 4, 128])
            dt_v = dt_all[:, hsl_].unsqueeze(2).broadcast_to([128, 4, 128])
            gg_v = gg[:].unsqueeze(1).broadcast_to([128, 4, 128])
            cb4 = colb4[par]
            eb, eb2, mb = ehead[par], ehead2[par], mhead[par]
            if grp >= 2:
                v.wait_ge(sp.h, sp.n)
            sv.inc(v.tensor_tensor(cb4[:].rearrange("p (l n) -> p l n", n=128),
                                   s_v, onesbig[:].rearrange("p (l n) -> p l n", n=128),
                                   ALU.mult))
            if grp >= 2:
                t.wait_ge(sv.h, sv.n)
            sv.wait(t)
            for l4 in range(4):
                sp.inc(t.transpose(pr[:, 128 * l4:128 * (l4 + 1)],
                                   cb4[:, 128 * l4:128 * (l4 + 1)], identf[:]))
            sp.wait(v)
            ebv = eb[:].rearrange("p (l n) -> p l n", n=128)
            eb2v = eb2[:].rearrange("p (l n) -> p l n", n=128)
            mbv = mb[:].rearrange("p (l n) -> p l n", n=128)
            sv.inc(v.tensor_tensor(ebv, pr[:].rearrange("p (l n) -> p l n", n=128),
                                   s_v, ALU.min))
            sv.wait(v)
            sv.inc(v.tensor_tensor(ebv, ebv, s_v, ALU.subtract))
            sv.wait(s)
            sc.inc(s.activation(eb2[:], eb[:], AF.Exp))
            sc.wait(v)
            sv.inc(v.tensor_tensor(ebv, eb2v, dt_v, ALU.mult))
            sv.wait(v)
            sv.inc(v.tensor_tensor(mbv, ebv, gg_v, ALU.mult))
            sv.wait(t)
            for l4 in range(4):
                h = 4 * grp + l4
                sp.inc(t.matmul(ps_big[:, 64 * h:64 * (h + 1)], mb[:, 128 * l4:128 * (l4 + 1)],
                                x_tm[:, 64 * h:64 * (h + 1)],
                                start=(h % 8 == 0), stop=(h % 8 == 7)))
        sp.wait(v)
        sv.inc(v.tensor_add(ya[:], ybuf[:], ps_big[:]))
        sv.wait(v)
        sv.inc(v.tensor_tensor(ybuf[:].rearrange("p (h d) -> p h d", d=64),
                               x_tm[:].rearrange("p (h d) -> p h d", d=64),
                               D_bc[:].unsqueeze(2).broadcast_to([128, 32, 64]), ALU.mult))
        sv.wait(v)
        sv.inc(v.tensor_add(ya[:], ya[:], ybuf[:]))
        sv.wait(v)
        sv.inc(v.tensor_copy(ybuf[:], ya[:]))
        sv.wait(g)
        yst.dma(cc, g.dma_start(out=yd[cc], in_=ybuf[:]))
        _scan_state_update(c, env, cc, sl, locals())
        if cc + 2 < NCH:
            sy.wait_ge(sp.h, sp.n)
            actl.dma(cc + 2, sy.dma_start(out=actl.buf(cc)[:],
                                          in_=actd[:, :, 128 * (cc + 2):128 * (cc + 3)].rearrange("m p n -> p m n")))
    for i in range(2):
        yst.sems[i].wait(g)
    es.close()
    return list(actl.sems) + list(yst.sems)

def _emit_mamba_hexchange(c, env):
    t, v, s, g, sy = c.t, c.v, c.s, c.g, c.sy
    sp, sv, sc = env["sp"], env["sv"], env["sc"]
    H, Hinitb, hmask_sb = env["H"], env["Hinitb"], env["hmask_sb"]
    ccin, ccout = env["ccin"], env["ccout"]
    hfinal = H[(NCH - 1) % 2]
    scc = env["scc"]
    sst_cc = c.sem(kind="sw")
    sv.wait(g)
    sst_cc.inc(g.dma_start(out=ccin[:], in_=hfinal[:]), 16)
    sst_cc.wait(g)
    scc.inc(g.collective_compute(
        "AllGather", ALU.bypass, replica_groups=[list(range(NCORES))],
        ins=[ccin.opt()], outs=[ccout.opt()]), 1)
    scc.wait(sy)
    es = ExitStack(); sv_es = c.es; c.es = es
    hsl = _Ring(c, "hx", 2, [128, 2048], F32, kind="hw")
    Hinit = [c.sb(f"hxhi{i}", [128, 2048], F32) for i in range(2)]
    c.es = sv_es
    for j in range(8):
        if j >= 2:
            sy.wait_ge(sv.h, sv.n)
        hsl.dma(j, sy.dma_start(out=hsl.buf(j)[:], in_=ccout[128 * j:128 * (j + 1), :]))
        hsl.wait_loaded(v, j)
        if j == 0:
            sv.inc(v.tensor_scalar(Hinit[0][:], hsl.buf(0)[:], hmask_sb[:, 0:1], None, ALU.mult))
        else:
            sv.wait(v)
            sv.inc(v.scalar_tensor_tensor(Hinit[j % 2][:], hsl.buf(j)[:], hmask_sb[:, j:j + 1],
                                          Hinit[(j + 1) % 2][:], ALU.mult, ALU.add))
    sv.wait(v)
    sv.inc(v.tensor_copy(Hinitb[:], Hinit[7 % 2][:]))
    es.close()
    return list(hsl.sems)

def _emit_mamba_corr(c, env):
    t, v, s, g, sy = c.t, c.v, c.s, c.g, c.sy
    sp, sv, sc = env["sp"], env["sv"], env["sc"]
    actd, yd = env["actd"], env["yd"]
    s_all, pfx, onesrf = env["s_all"], env["pfx"], env["onesrf"]
    Hinitb = env["Hinitb"]
    ps_bc, ps_big = env["ps_bc"], env["ps_big"]
    es = ExitStack(); sv_es = c.es; c.es = es
    cl = _Ring(c, "b3cc", 2, [128, 128], BF16, kind="hw")
    dst = _Ring(c, "b3cd", 2, [128, 2048], F32, kind="sw")
    es2 = c.sb("b3ce", [128, 32], F32)
    c.es = sv_es
    for cc in range(NCH):
        if cc < 2:
            cl.dma(cc, sy.dma_start(out=cl.buf(cc)[:],
                                    in_=actd[17, :, 128 * cc:128 * (cc + 1)]))
        sv.wait(t); sc.wait(t)
        sp.inc(t.matmul(ps_bc[:, 128:160], onesrf[:], pfx[0:1, 32 * cc:32 * (cc + 1)],
                        start=True, stop=True))
        sp.wait(v)
        sv.inc(v.tensor_tensor(es2[:], ps_bc[:, 128:160], s_all[:, 32 * cc:32 * (cc + 1)], ALU.add))
        sv.wait(s)
        sc.inc(s.activation(es2[:], es2[:], AF.Exp))
        cl.wait_loaded(t, cc)
        sc.wait(t)
        for j in range(4):
            sp.inc(t.matmul(ps_big[:, 512 * j:512 * (j + 1)], cl.buf(cc)[:],
                            Hinitb[:, 512 * j:512 * (j + 1)], start=True, stop=True))
        sp.wait(v); sc.wait(v)
        db = dst.buf(cc)
        v.wait_ge(dst.sem(cc).h, dst.sem(cc).base + 16 * (cc // 2))
        sv.inc(v.tensor_tensor(db[:].rearrange("p (h d) -> p h d", d=64),
                               ps_big[:].rearrange("p (h d) -> p h d", d=64),
                               es2[:].unsqueeze(2).broadcast_to([128, 32, 64]), ALU.mult))
        sv.wait(g)
        dst.dma(cc, g.dma_start(out=yd[cc], in_=db[:], accum_op=ALU.add))
        if cc + 2 < NCH:
            sy.wait_ge(sp.h, sp.n)
            cl.dma(cc + 2, sy.dma_start(out=cl.buf(cc)[:],
                                        in_=actd[17, :, 128 * (cc + 2):128 * (cc + 3)]))
    for i in range(2):
        dst.sems[i].wait(g)
    es.close()
    return list(cl.sems) + list(dst.sems)

def _emit_mamba_out(c, env, Xin, Xout):
    t, v, s, g, sy = c.t, c.v, c.s, c.g, c.sy
    sp, sv, sc = env["sp"], env["sv"], env["sc"]
    yd, gated, u0d = env["yd"], env["gated"], env["u0d"]
    normw_bc, ident = env["normw_bc"], env["ident"]
    ps_tr, ps_big = env["ps_tr"], env["ps_big"]
    es = ExitStack(); sv_es = c.es; c.es = es
    opT = c.sb("b4opT", [128, 16, 1024], BF16)
    swo = c.sem(kind="sw")
    swo.inc(g.dma_start(out=opT[:], in_=env["Wop"].rearrange("(a p) n -> p a n", p=128)), 16)
    swo.wait(t)
    yl = _Ring(c, "b4y", 2, [128, 2048], F32, kind="hw")
    gl = _Ring(c, "b4g", 2, [128, 2048], F32, kind="hw")
    ul = _Ring(c, "b4u", 2, [128, 1024], F32, kind="hw")
    xl = _Ring(c, "b4x", 2, [128, 1024], F32, kind="hw")
    y2 = c.sb("b4y2", [128, 2048], F32)
    y4 = c.sb("b4y4", [128, 2048], BF16)
    y_fm = c.sb("b4yfm", [128, 2048], BF16)
    xo = _Ring(c, "b4o", 2, [128, 1024], F32, kind="sw")
    scr = {k: c.sb("b4" + k, [128, 2048] if k == "sq" else [128, 1], F32)
           for k in ("sq", "c0", "c1", "c2", "c3")}
    c.es = sv_es
    for ti in range(NT):
        bi = ti % 2
        if ti < 2:
            yl.dma(ti, sy.dma_start(out=yl.buf(ti)[:], in_=yd[ti]))
            gl.dma(ti, sy.dma_start(out=gl.buf(ti)[:], in_=gated[ti]))
            ul.dma(ti, sy.dma_start(out=ul.buf(ti)[:], in_=u0d[ti]))
            xl.dma(ti, sy.dma_start(out=xl.buf(ti)[:], in_=Xin[ti]))
        for r_ in (yl, gl, ul, xl):
            r_.wait_loaded(v, ti)
        sv.wait(v)
        sv.inc(v.tensor_tensor(y2[:], yl.buf(ti)[:], gl.buf(ti)[:], ALU.mult))
        sv.wait(s)
        sc.inc(s.activation(scr["sq"][:], y2[:], AF.Square))
        sc.wait(v)
        sv.inc(v.tensor_reduce(scr["c0"][:], scr["sq"][:], AX.X, ALU.add))
        sv.wait(v)
        sv.inc(v.tensor_scalar(scr["c1"][:], scr["c0"][:], 1.0 / 2048, None, ALU.mult))
        sv.wait(s)
        sc.inc(s.activation(scr["c2"][:], scr["c1"][:], AF.Ln, bias=EPS_LN))
        sc.wait(s)
        sc.inc(s.activation(scr["c2"][:], scr["c2"][:], AF.Exp, scale=-0.5))
        sc.wait(v)
        sv.inc(v.tensor_scalar(y2[:], y2[:], scr["c2"][:], None, ALU.mult))
        sv.wait(v)
        sv.inc(v.tensor_tensor(y4[:], y2[:], normw_bc[:], ALU.mult))
        for r in range(2):
            sv.wait(t)
            for j in range(8):
                sp.inc(t.transpose(ps_tr[:, j * 128:(j + 1) * 128],
                                   y4[:, 1024 * r + 128 * j:1024 * r + 128 * (j + 1)], ident[:]))
            sp.wait(v)
            sv.inc(v.tensor_copy(y_fm[:, 1024 * r:1024 * (r + 1)], ps_tr[:]))
        sv.wait(t)
        if ti >= 1:
            t.wait_ge(sv.h, sv.n)
        for k in range(16):
            i1 = t.matmul(ps_big[:, 0:512], y_fm[:, 128 * k:128 * (k + 1)],
                          opT[:, k, 0:512], start=(k == 0), stop=(k == 15))
            i2 = t.matmul(ps_big[:, 512:1024], y_fm[:, 128 * k:128 * (k + 1)],
                          opT[:, k, 512:1024], start=(k == 0), stop=(k == 15))
        sp.inc(i1); sp.inc(i2)
        sp.wait(v)
        ob = xo.buf(ti)
        v.wait_ge(xo.sem(ti).h, xo.sem(ti).base + 16 * (ti // 2))
        sv.inc(v.tensor_add(y2[:, 0:1024], ps_big[:, 0:1024], ul.buf(ti)[:]))
        sv.wait(v)
        sv.inc(v.tensor_add(ob[:], y2[:, 0:1024], xl.buf(ti)[:]))
        sv.wait(g)
        xo.dma(ti, g.dma_start(out=Xout[ti], in_=ob[:]))
        if ti + 2 < NT:
            sy.wait_ge(sv.h, sv.n)
            yl.dma(ti + 2, sy.dma_start(out=yl.buf(ti)[:], in_=yd[ti + 2]))
            gl.dma(ti + 2, sy.dma_start(out=gl.buf(ti)[:], in_=gated[ti + 2]))
            ul.dma(ti + 2, sy.dma_start(out=ul.buf(ti)[:], in_=u0d[ti + 2]))
            xl.dma(ti + 2, sy.dma_start(out=xl.buf(ti)[:], in_=Xin[ti + 2]))
    for i in range(2):
        xo.sems[i].wait(g)
    es.close()
    return list(xo.sems) + list(yl.sems) + list(gl.sems) + list(ul.sems) + list(xl.sems)

def _barrier_all(c, sems):
    for eng in (c.t, c.v, c.s, c.g, c.sy):
        for sm in sems:
            sm.wait(eng)

def emit_phase_mamba_main(c, Xin, Xout, W, Ds, name="B", mode="cc",
                          hgather=None, hout=None):
    env = emit_phase_mamba(c, Xin, Xout, W, Ds, name, hgather=hgather,
                           want_u0=(mode != "pre"))
    base = [env["sp"], env["sv"], env["sc"], env["sw"], env["sg"], env["scc"]]
    _barrier_all(c, base + env["b0sems"])
    env["es_b0"].close()
    r = _emit_mamba_b1(c, env, want_z=(mode != "pre"))
    _barrier_all(c, base + r)
    env["es_ip"].close()
    r = _emit_mamba_conv(c, env)
    _barrier_all(c, base + r)
    r = _emit_mamba_dt(c, env)
    _barrier_all(c, base + r)
    r = _emit_mamba_scan(c, env, want_y=(mode != "pre"))
    _barrier_all(c, base + r)
    if mode == "pre":
        sfin = c.sem(kind="sw")
        sfin.inc(c.g.dma_start(out=hout, in_=env["H"][1][:]), 16)
        sfin.wait(c.g)
        return env
    if mode == "cc":
        r = _emit_mamba_hexchange(c, env)
        _barrier_all(c, base + r)
        r = _emit_mamba_corr(c, env)
        _barrier_all(c, base + r)
    r = _emit_mamba_out(c, env, Xin, Xout)
    _barrier_all(c, base + r)
    return env

def _prep_weights_B(inputs):
    return dict(
        ipT=np.ascontiguousarray(inputs["L0_attn_in_proj_w"].T),
        opT=np.ascontiguousarray(inputs["L0_attn_out_proj_w"].T),
        lnw=np.ascontiguousarray(inputs["L0_attn_ln_w"][None, :]),
        lnb=np.ascontiguousarray(inputs["L0_attn_ln_b"][None, :]),
        convw=np.ascontiguousarray(inputs["L0_attn_conv_w"]),
        convb=np.ascontiguousarray(inputs["L0_attn_conv_b"].reshape(18, 128).T),
        dtbias=np.ascontiguousarray(inputs["L0_attn_dt_bias"][None, :]),
        alog=np.ascontiguousarray(inputs["L0_attn_A_log"][None, :]),
        dvec=np.ascontiguousarray(inputs["L0_attn_D"][None, :]),
        normw=np.ascontiguousarray(inputs["L0_attn_norm_w"][None, :]),
    )

def _core_masks_B(cid):
    hmask = np.zeros((128, 8), np.float32)
    halom = np.zeros((128, 1), np.float32)
    if cid % 2 == 1:
        hmask[:, cid - 1] = 1.0
        halom[:] = 1.0
    return dict(hmask=hmask, halom=halom)

def build_B(mode="cc"):
    nc = bass.Bass("TRN2", target_bir_lowering=False)
    X = nc.dram_tensor("X", [33, 128, D], F32, kind="ExternalInput").ap()
    XO = nc.dram_tensor("XO", [33, 128, D], F32, kind="ExternalOutput").ap()
    Wap = {k: nc.dram_tensor("W_" + k, list(s), F32, kind="ExternalInput").ap()
           for k, s in B_W_SHAPES.items()}
    Ds = {k: nc.dram_tensor("S_" + k, list(sh), dt_, kind="Internal",
                            addr_space=("Shared" if k == "ccout" else "Local")).ap()
          for k, (sh, dt_) in B_SCRATCH.items()}
    hg = hout = None
    if mode == "pre":
        hout = nc.dram_tensor("HF", [128, 2048], F32, kind="ExternalOutput").ap()
    if mode == "full":
        hg = nc.dram_tensor("HG", [8, 128, 2048], F32, kind="ExternalInput").ap()
    es = ExitStack()
    c = Ctx(nc, es)
    emit_phase_mamba_main(c, X, XO, Wap, Ds, "B", mode=mode, hgather=hg, hout=hout)
    es.close()
    return nc

D_W_SHAPES = dict(w1T=(9, 1024, 2816), w3T=(9, 1024, 2816), w2T=(9, 2816, 1024),
                  gateT=(1024, 16), gbias=(1, 16))

def emit_phase_L1FFN(c, Xin, Xout, W, xntd, name, inplace=False):
    nc = c.nc
    t, v, s, g, sy = c.t, c.v, c.s, c.g, c.sy
    NE = 9

    gateT = c.sb(name + "_gT", [128, 8, 16], BF16)
    gbias = c.sb(name + "_gb", [1, 16], BF16)
    onesr = c.sb(name + "_ones", [1, 128], BF16)
    ident = c.sb(name + "_id", [128, 128], BF16)
    cw = c.sb(name + "_cw", [128, NT, 16], F32)
    sg = c.sem(); sp = c.sem(); sv = c.sem(); sc = c.sem()
    sw = c.sem(kind="sw")
    sacc = [c.sem(kind="sw"), c.sem(kind="sw")]

    c.const(EPS_LN)
    c.sconst.wait(s)
    sw.inc(g.dma_start(out=gateT[:], in_=W["gateT"].rearrange("(a p) n -> p a n", p=128)), 16)
    sw.inc(g.dma_start(out=gbias[:], in_=W["gbias"]), 16)
    sg.inc(g.memset(onesr[:], 1.0))
    emit_identity(c, ident, sg)

    es0 = ExitStack(); _sv = c.es; c.es = es0
    xin = [c.sb(name + f"_p0x{i}", [128, 1024], F32) for i in range(2)]
    xnb = c.sb(name + "_p0n", [128, 1024], BF16)
    xnT = [c.sb(name + f"_p0t{i}", [128, 1024], BF16) for i in range(2)]
    gsc = c.sb(name + "_p0g", [128, 16], F32)
    gw2 = c.sb(name + "_p0w", [128, 16], F32)
    scratch2 = [{k: c.sb(name + f"_p0{k}{i}", [128, 1024] if k == "sq" else [128, 1], F32)
                 for k in ("sq", "c0", "c1", "c2", "c3")} for i in range(2)]
    scratch = scratch2[0]
    ps_tr = c.ps(name + "_p0ptr", [128, 1024], BF16)
    ps_sc = c.ps(name + "_p0psc", [128, 16], F32)
    c.es = _sv
    sq_ = [c.sem(kind="hw"), c.sem(kind="hw")]
    so_ = [c.sem(kind="sw"), c.sem(kind="sw")]

    sq_[0].inc(sy.dma_start(out=xin[0][:], in_=Xin[0]), 16)
    sq_[1].inc(sy.dma_start(out=xin[1][:], in_=Xin[1]), 16)
    for ti in range(NT):
        bi = ti % 2
        scratch = scratch2[bi]
        if ti == 0:
            v.wait_ge(sq_[0].h, sq_[0].base + 16)
            s.wait_ge(sq_[0].h, sq_[0].base + 16)
            emit_ln_stats(c, xin[0][:], sc, sv, scratch)
        sc.wait(v)
        sv.inc(v.tensor_scalar(xnb[:], xin[bi][:], scratch["c2"][:], scratch["c3"][:],
                               ALU.subtract, ALU.mult))
        if not inplace:
            sv.wait(g)
            so_[bi].inc(g.dma_start(out=Xout[ti], in_=xin[bi][:]), 16)
        sv.wait(t)
        if ti == 0:
            sg.wait(t); sw.wait(t)
        emit_transpose_tiles(c, xnb, xnT[bi], ident, ps_tr, sp, sv, 8)
        sv.wait(g)
        so_[bi].inc(g.dma_start(
            out=xntd[:, :, 128 * ti:128 * (ti + 1)].rearrange("k p j -> p k j"),
            in_=xnT[bi][:].rearrange("p (k j) -> p k j", j=128)), 16)
        if ti + 1 < NT:
            nb_ = (ti + 1) % 2
            v.wait_ge(sq_[nb_].h, sq_[nb_].base + 16 * ((ti + 1) // 2 + 1))
            s.wait_ge(sq_[nb_].h, sq_[nb_].base + 16 * ((ti + 1) // 2 + 1))
            emit_ln_stats(c, xin[nb_][:], sc, sv, scratch2[nb_])
        if ti + 2 < NT:
            sy.wait_ge(sv.h, sv.n)
            sy.wait_ge(so_[bi].h, so_[bi].n)
            sq_[bi].inc(sy.dma_start(out=xin[bi][:], in_=Xin[ti + 2]), 16)
    _sv = c.es; c.es = es0
    xnt_r = _Ring(c, name + "_xr", 2, [128, 8, 128], BF16, kind="hw")
    c.es = _sv
    so_[0].wait(sy); so_[1].wait(sy)
    for ti in range(NT):
        bi = ti % 2
        if ti < 2:
            xnt_r.dma(ti, sy.dma_start(
                out=xnt_r.buf(ti)[:],
                in_=xntd[:, :, 128 * ti:128 * (ti + 1)].rearrange("k p j -> p k j")))
        xnt_r.wait_loaded(t, ti)
        sv.wait(t); sc.wait(t)
        for k in range(8):
            t.matmul(ps_sc[:], xnt_r.buf(ti)[:, k, :], gateT[:, k, :],
                     start=(k == 0), stop=False)
        sp.inc(t.matmul(ps_sc[:], onesr[:], gbias[:].broadcast_to([1, 16]), start=False, stop=True))
        sp.wait(s)
        sc.inc(s.activation(gsc[:], ps_sc[:], AF.Sigmoid))
        sc.wait(v)
        sv.inc(v.tensor_reduce(scratch["c0"][:], gsc[:, 0:9], AX.X, ALU.max))
        sv.wait(v)
        sv.inc(v.tensor_scalar(gw2[:, 0:9], gsc[:, 0:9], scratch["c0"][:], None, ALU.is_ge))
        sv.wait(v)
        sv.inc(v.scalar_tensor_tensor(gw2[:, 0:9], gw2[:, 0:9], -1e30, gsc[:, 0:9],
                                      ALU.mult, ALU.add))
        sv.wait(v)
        sv.inc(v.tensor_reduce(scratch["c1"][:], gw2[:, 0:9], AX.X, ALU.max))
        sv.wait(v)
        sv.inc(v.tensor_scalar(gw2[:, 0:9], gsc[:, 0:9], scratch["c1"][:], None, ALU.is_ge))
        sv.wait(v)
        sv.inc(v.tensor_tensor(gw2[:, 0:9], gw2[:, 0:9], gsc[:, 0:9], ALU.mult))
        sv.wait(v)
        sv.inc(v.tensor_reduce(scratch["c2"][:], gw2[:, 0:9], AX.X, ALU.add))
        sv.wait(v)
        sv.inc(v.tensor_scalar(scratch["c2"][:], scratch["c2"][:], 1e-20, None, ALU.add))
        sv.wait(v)
        sv.inc(v.reciprocal(scratch["c2"][:], scratch["c2"][:]))
        sv.wait(v)
        sv.inc(v.tensor_scalar(scratch["c2"][:], scratch["c2"][:], 0.5, None, ALU.mult))
        sv.wait(v)
        sv.inc(v.tensor_scalar(cw[:, ti, 0:9], gw2[:, 0:9], scratch["c2"][:], None, ALU.mult))
        if ti + 2 < NT:
            sy.wait_ge(sv.h, sv.n)
            xnt_r.dma(ti + 2, sy.dma_start(
                out=xnt_r.buf(ti)[:],
                in_=xntd[:, :, 128 * (ti + 2):128 * (ti + 3)].rearrange("k p j -> p k j")))
    for smx in sq_ + so_:
        smx.wait(g); smx.wait(sy); smx.wait(t); smx.wait(v); smx.wait(s)
    for smx in (sp, sv, sc, sw, sg):
        for eng in (t, v, s, g, sy):
            smx.wait(eng)
    es0.close()

    esE = ExitStack(); _sv = c.es; c.es = esE
    w1T = c.sb(name + "_w1", [128, 8, 2816], BF16)
    w3T = c.sb(name + "_w3", [128, 8, 2816], BF16)
    w2T = c.sb(name + "_w2", [128, 22, 1024], BF16)
    h_all = c.sb(name + "_h", [128, 22, 512], BF16)
    tsil = [c.sb(name + f"_ts{i}", [128, 512], F32) for i in range(2)]
    delta = [c.sb(name + f"_d{i}", [128, 1024], F32) for i in range(2)]
    xg = _Ring(c, name + "_xg", 2, [128, 8, 512], BF16, kind="hw")
    ps_s1 = [c.ps(name + f"_ps1{i}", [128, 512], F32) for i in range(4)]
    ps_o = [c.ps(name + f"_pso{i}", [128, 1024], F32) for i in range(2)]
    c.es = _sv
    swx = c.sem(kind="sw")
    sw2 = c.sem(kind="sw")
    sevac = [c.sem(), c.sem()]
    spv = [c.sem(), c.sem()]
    sevo = [c.sem(), c.sem()]

    for e in range(NE):
        g.wait_ge(sp.h, sp.n)
        if e == 0:
            swx.inc(g.dma_start(out=w1T[:], in_=W["w1T"][0].rearrange("(a p) n -> p a n", p=128)), 16)
            swx.inc(g.dma_start(out=w3T[:], in_=W["w3T"][0].rearrange("(a p) n -> p a n", p=128)), 16)
        sw2.inc(g.dma_start(out=w2T[:], in_=W["w2T"][e].rearrange("(a p) n -> p a n", p=128)), 16)
        swx.wait(t)
        sacc[0].wait(g); sacc[1].wait(g)
        for gi in range(8):
            bx = gi % 2
            if gi < 2:
                if e > 0 or gi > 0:
                    sy.wait_ge(sp.h, sp.n)
                xg.dma(8 * e + gi, sy.dma_start(
                    out=xg.buf(gi)[:],
                    in_=xntd[:, :, 512 * gi:512 * (gi + 1)].rearrange("k p j -> p k j")))
            xg.wait_loaded(t, 8 * e + gi)
            for m in range(22):
                par = m % 2
                pa, pb = ps_s1[2 * par], ps_s1[2 * par + 1]
                if m >= 2:
                    t.wait_ge(sevac[par].h, sevac[par].n)
                for k in range(8):
                    ia = t.matmul(pa[:], w1T[:, k, 128 * m:128 * (m + 1)],
                                  xg.buf(gi)[:, k, :],
                                  start=(k == 0), stop=(k == 7))
                    ib = t.matmul(pb[:], w3T[:, k, 128 * m:128 * (m + 1)],
                                  xg.buf(gi)[:, k, :],
                                  start=(k == 0), stop=(k == 7))
                sp.inc(ia); spv[par].inc(ib)
                s.wait_ge(sp.h, sp.n)
                s.wait_ge(sevac[par].h, sevac[par].n)
                sc.inc(s.activation(tsil[par][:], pa[:], AF.Silu))
                v.wait_ge(sc.h, sc.n)
                v.wait_ge(spv[par].h, spv[par].n)
                sevac[par].inc(v.tensor_tensor(h_all[:, m, :], tsil[par][:], pb[:], ALU.mult))
            if gi == 7 and e + 1 < NE:
                g.wait_ge(sevac[0].h, sevac[0].n)
                g.wait_ge(sevac[1].h, sevac[1].n)
                swx.inc(g.dma_start(out=w1T[:], in_=W["w1T"][e + 1].rearrange("(a p) n -> p a n", p=128)), 16)
                swx.inc(g.dma_start(out=w3T[:], in_=W["w3T"][e + 1].rearrange("(a p) n -> p a n", p=128)), 16)
            for tt in range(4):
                po = ps_o[tt % 2]
                tile = 4 * gi + tt
                if tt == 0:
                    t.wait_ge(sevac[0].h, sevac[0].n)
                    t.wait_ge(sevac[1].h, sevac[1].n)
                    if gi == 0:
                        sw2.wait(t)
                t.wait_ge(sevo[tt % 2].h, sevo[tt % 2].n)
                for m in range(22):
                    i1 = t.matmul(po[:, 0:512], h_all[:, m, 128 * tt:128 * (tt + 1)],
                                  w2T[:, m, 0:512], start=(m == 0), stop=(m == 21))
                    i2 = t.matmul(po[:, 512:1024], h_all[:, m, 128 * tt:128 * (tt + 1)],
                                  w2T[:, m, 512:1024], start=(m == 0), stop=(m == 21))
                sp.inc(i1); sp.inc(i2)
                sp.wait(s)
                db = delta[tt % 2]
                sacc[tt % 2].wait(s)
                sevo[tt % 2].inc(s.activation(db[:], po[:], AF.Copy, scale=cw[:, tile, e:e + 1]))
                g.wait_ge(sevo[tt % 2].h, sevo[tt % 2].n)
                sacc[tt % 2].inc(g.dma_start(out=Xout[tile], in_=db[:], accum_op=ALU.add), 16)
            if gi + 2 < 8:
                sy.wait_ge(sp.h, sp.n)
                xg.dma(8 * e + gi + 2, sy.dma_start(
                    out=xg.buf(gi)[:],
                    in_=xntd[:, :, 512 * (gi + 2):512 * (gi + 3)].rearrange("k p j -> p k j")))
    sacc[0].wait(g); sacc[1].wait(g); sacc[0].wait(sy); sacc[1].wait(sy)
    for smx in (sp, sv, sc, swx):
        for eng in (t, v, s, g, sy):
            smx.wait(eng)
    esE.close()
    return dict(sacc=sacc)

def _prep_weights_D(inputs):
    w1 = np.concatenate([inputs["L1_ffn_w1"], inputs["L1_ffn_new_w1"]], 0)
    w3 = np.concatenate([inputs["L1_ffn_w3"], inputs["L1_ffn_new_w3"]], 0)
    w2 = np.concatenate([inputs["L1_ffn_w2"], inputs["L1_ffn_new_w2"]], 0)
    gate = np.concatenate([inputs["L1_ffn_gate_weight"], inputs["L1_ffn_new_gate_weight"]], 0)
    gb = np.zeros((1, 16), np.float32)
    gb[0, 8] = inputs["L1_ffn_gate_bias"][0]
    gb[0, 9:] = -1e30
    gateTpad = np.zeros((1024, 16), np.float32)
    gateTpad[:, 0:9] = gate.T
    return dict(
        w1T=np.ascontiguousarray(np.transpose(w1, (0, 2, 1))),
        w3T=np.ascontiguousarray(np.transpose(w3, (0, 2, 1))),
        w2T=np.ascontiguousarray(np.transpose(w2, (0, 2, 1))),
        gateT=np.ascontiguousarray(gateTpad),
        gbias=gb,
    )

def build_D():
    nc = bass.Bass("TRN2", target_bir_lowering=False)
    X = nc.dram_tensor("X", [33, 128, D], F32, kind="ExternalInput").ap()
    XO = nc.dram_tensor("XO", [33, 128, D], F32, kind="ExternalOutput").ap()
    Wap = {k: nc.dram_tensor("W_" + k, list(s), F32, kind="ExternalInput").ap()
           for k, s in D_W_SHAPES.items()}
    xntd = nc.dram_tensor("S_xnT", [8, 128, NTOK], BF16, kind="Internal").ap()
    es = ExitStack()
    c = Ctx(nc, es)
    emit_phase_L1FFN(c, X, XO, Wap, xntd, "Dp")
    es.close()
    return nc

E_W_SHAPES = dict(WqkT=(1024, 2048), bqk=(1, 2048), lepw=(128, 8, 3), lepb=(128, 8),
                  kidx=(1, 2048), sgn=(1, 2048), mask67=(67, 8))
LN1E4 = float(np.log(10000.0))
NSEQ = 8192.0
E_SKIP = set()

def emit_phase_linattn(c, Xin, Xout, W, name, mode="full", ccg=None, ccpart=None,
                       hrowd=None, qeD=None, qrD=None, uTD=None):
    nc = c.nc
    t, v, s, g, sy = c.t, c.v, c.s, c.g, c.sy
    pre = (mode == "pre")

    WqkT = c.sb(name + "_wq", [128, 8, 2048], BF16)
    bqk = c.sb(name + "_bq", [1, 2048], BF16)
    onesr = c.sb(name + "_ones", [1, 128], BF16)
    ident = c.sb(name + "_id", [128, 128], BF16)
    cos_bc = c.sb(name + "_cos", [128, 2048], BF16)
    sin_bc = c.sb(name + "_sin", [128, 2048], BF16)
    es_bc = ExitStack(); _svb = c.es; c.es = es_bc
    r_th = c.sb(name + "_rth", [1, 2048], F32, side="right")
    r_sin = c.sb(name + "_rsin", [1, 2048], F32, side="right")
    r_cos = c.sb(name + "_rcos", [1, 2048], F32, side="right")
    r_sgn = c.sb(name + "_rsgn", [1, 2048], F32, side="right")
    c.es = _svb
    rowb = c.sb(name + "_rowb", [1, 2048], BF16)
    lepw = c.sb(name + "_lw", [128, 8, 3], F32)
    lepb = c.sb(name + "_lb", [128, 8], F32)
    m67 = c.sb(name + "_m67", [67, 8], F32)
    sg = c.sem(); sp = c.sem(); sv = c.sem(); sc = c.sem()
    sw = c.sem(kind="sw")

    c.const(EPS_LN); c.const(float(np.pi / 2)); c.const(1.0)
    c.sconst.wait(s)
    sw.inc(g.dma_start(out=WqkT[:], in_=W["WqkT"].rearrange("(a p) n -> p a n", p=128)), 16)
    sw.inc(g.dma_start(out=bqk[:], in_=W["bqk"]), 16)
    sw.inc(g.dma_start(out=r_th[:], in_=W["kidx"]), 16)
    sw.inc(g.dma_start(out=r_sgn[:], in_=W["sgn"]), 16)
    sw.inc(g.dma_start(out=lepw[:], in_=W["lepw"]), 16)
    sw.inc(g.dma_start(out=lepb[:], in_=W["lepb"]), 16)
    if not pre:
        sw.inc(g.dma_start(out=m67[:], in_=W["mask67"]), 16)
    sg.inc(g.memset(onesr[:], 1.0))
    emit_identity(c, ident, sg)

    _svb = c.es; c.es = es_bc
    ps_bc = c.ps(name + "_psbc", [128, 512], F32)
    c.es = _svb
    sw.wait(s)
    sc.inc(s.activation(r_th[:], r_th[:], AF.Exp, scale=-LN1E4))
    sc.wait(s)
    sc.inc(s.activation(r_sin[:], r_th[:], AF.Sin))
    sc.wait(s)
    sc.inc(s.activation(r_cos[:], r_th[:], AF.Sin, bias=float(np.pi / 2)))
    sc.wait(v); sw.wait(v)
    sv.inc(v.tensor_tensor(r_sin[:], r_sin[:], r_sgn[:], ALU.mult))
    for which, src_row, dst in ((0, r_cos, cos_bc), (1, r_sin, sin_bc)):
        sv.wait(v)
        sv.inc(v.tensor_copy(rowb[:], src_row[:]))
        for j in range(4):
            sv.wait(t); sg.wait(t)
            sp.inc(t.matmul(ps_bc[:], onesr[:], rowb[0:1, 512 * j:512 * (j + 1)],
                            start=True, stop=True))
            sp.wait(v)
            sv.inc(v.tensor_copy(dst[:, 512 * j:512 * (j + 1)], ps_bc[:]))

    if (not pre) and ("combine" in E_SKIP):
        kv_sb = c.sb(name + "_kv", [128, 1024], BF16)
        kvp = c.sb(name + "_kvp", [128, 1024], BF16)
        ksum_bc = c.sb(name + "_ksb", [128, 1024], BF16)
        halocols = c.sb(name + "_hc", [128, 2, 8], F32)
        sg.inc(g.memset(kv_sb[:], 0.001))
        sg.inc(g.memset(kvp[:], 0.001))
        sg.inc(g.memset(ksum_bc[:], 0.001))
        sg.inc(g.memset(halocols[:], 0.0))
        sg.wait(v); sg.wait(s); sg.wait(t)
    elif not pre:
        kv_sb = c.sb(name + "_kv", [128, 1024], BF16)
        kvp = c.sb(name + "_kvp", [128, 1024], BF16)
        ksum_bc = c.sb(name + "_ksb", [128, 1024], BF16)
        acc67 = [c.sb(name + f"_a67{i}", [67, 1024], F32) for i in range(2)]
        halocols = c.sb(name + "_hc", [128, 2, 8], F32)
        es_cc = ExitStack(); _sv = c.es; c.es = es_cc
        slot = _Ring(c, name + "_sl", 2, [67, 1024], F32, kind="hw")
        c.es = _sv
        sv2 = c.sem()
        sw.wait(v)
        for j in range(8):
            if j >= 2:
                sy.wait_ge(sv2.h, sv2.n)
            slot.dma(j, sy.dma_start(out=slot.buf(j)[:], in_=ccg[j]))
            slot.wait_loaded(v, j)
            dst, src_ = acc67[(j + 1) % 2], acc67[j % 2]
            if j == 0:
                sv2.inc(v.tensor_scalar(acc67[1][:], slot.buf(0)[:], m67[:, 0:1], None, ALU.mult))
            else:
                sv2.wait(v)
                sv2.inc(v.scalar_tensor_tensor(dst[:], slot.buf(j)[:], m67[:, j:j + 1],
                                               src_[:], ALU.mult, ALU.add))
        sv2.wait(v)
        sv2.inc(v.tensor_copy(kv_sb[0:64, :], acc67[0][0:64, :]))
        sv2.inc(v.tensor_copy(kv_sb[64:128, :], acc67[0][0:64, :]))
        sv2.wait(g)
        sg.inc(g.memset(kvp[:], 0.0))
        sg.wait(v)
        sv2.wait(v)
        for j_ in range(8):
            sv2.inc(v.tensor_copy(kvp[0:64, 128 * j_:128 * j_ + 64],
                                  kv_sb[0:64, 128 * j_:128 * j_ + 64]))
            sv2.inc(v.tensor_copy(kvp[64:128, 128 * j_ + 64:128 * (j_ + 1)],
                                  kv_sb[64:128, 128 * j_ + 64:128 * (j_ + 1)]))
        sv2.wait(v)
        sv2.inc(v.tensor_copy(rowb[:, 0:1024], acc67[0][64:65, :]))
        sv2.wait(t)
        for j in range(2):
            sp.inc(t.matmul(ps_bc[:], onesr[:], rowb[0:1, 512 * j:512 * (j + 1)],
                            start=True, stop=True))
            sp.wait(v)
            sv2.inc(v.tensor_copy(ksum_bc[:, 512 * j:512 * (j + 1)], ps_bc[:]))
            sv2.wait(t)
        sv2.wait(g)
        shh = c.sem(kind="sw")
        shh.inc(g.dma_start(out=hrowd[:], in_=acc67[0][65:67, :]), 16)
        shh.wait(g)
        with nc.allow_non_contiguous_dma(reason="2KB halo gather"):
            shh.inc(g.dma_start(out=halocols[:],
                                in_=hrowd[:].rearrange("r (m p) -> p r m", p=128)), 16)
        shh.wait(v); shh.wait(s)
        for smx in (sv2, shh):
            for eng in (t, v, s, g, sy):
                smx.wait(eng)
        es_cc.close()

    es_bc.close()
    es0 = ExitStack(); _sv = c.es; c.es = es0
    xin = [c.sb(name + f"_x{i}", [128, 1024], F32) for i in range(2)]
    ub = c.sb(name + "_u", [128, 1024], BF16)
    uT = [c.sb(name + f"_uT{i}", [128, 1024], BF16) for i in range(2)]
    ulastc = [c.sb(name + f"_ulc{i}", [128, 8], BF16) for i in range(2)]
    e0 = c.sb(name + "_e0", [128, 2048], F32)
    e1 = c.sb(name + "_e1", [128, 2048], F32)
    qke = c.sb(name + "_qke", [128, 2048], BF16)
    t1b = c.sb(name + "_t1", [128, 2048], BF16)
    t2b = c.sb(name + "_t2", [128, 2048], BF16)
    qkr = c.sb(name + "_qkr", [128, 2048], BF16)
    qz = c.sb(name + "_qz", [128, 1024], BF16)
    qzT = c.sb(name + "_qzT", [128, 1024], BF16)
    zt = c.sb(name + "_zt", [128, 1024], F32)
    zden = c.sb(name + "_zd", [128, 16], F32)
    pef = [c.sb(name + f"_pef{i}", [128, 1024], F32) for i in range(2)]
    pe_b = c.sb(name + "_peb", [128, 1024], BF16)
    x1s = [c.sb(name + f"_x1{i}", [128, 1024], F32) for i in range(2)]
    xo = _Ring(c, name + "_xo", 2, [128, 1024], F32, kind="sw")
    scratch = {k: c.sb(name + "_s" + k, [128, 1024] if k == "sq" else [128, 1], F32)
               for k in ("sq", "c0", "c1", "c2", "c3")}
    ps_tr = c.ps(name + "_pstr", [128, 1024], BF16)
    ps_qk = c.ps(name + "_psqk", [128, 2048], F32)
    if pre:
        ps_kv = c.ps(name + "_pskv", [64, 1024], F32)
        urow = c.sb(name + "_ur", [1, 2048], F32)
        ks_acc = [c.sb(name + f"_ka{i}", [1, 1024], F32) for i in range(2)]
    else:
        ps_at = c.ps(name + "_psat", [128, 1024], F32)
        ps_pe = c.ps(name + "_pspe", [128, 1024], BF16)
    c.es = _sv
    sq_ = [c.sem(kind="hw"), c.sem(kind="hw")]
    sst = [c.sem(kind="hw"), c.sem(kind="hw")]
    surow = c.sem(kind="sw")

    _sv = c.es; c.es = es0
    onescb = c.sb(name + "_oc", [128, 1], BF16)
    kvf = c.sb(name + "_kvf", [64, 1024], F32)
    ksf = c.sb(name + "_ksf", [1, 1024], F32)
    c.es = _sv
    sg.inc(g.memset(onescb[:], 1.0))
    if not pre:
        env_loc = dict(sp=sp, sv=sv, sc=sc, pef=pef, x1s=x1s, xo=xo, lepw=lepw,
                       pe_b=pe_b, ps_pe=ps_pe, ident=ident, Xout=Xout)

    if pre:
        st3 = [c.sem(kind="sw"), c.sem(kind="sw")]
        for ti in range(NT):
            bi = ti % 2
            if ti < 2:
                sq_[bi].inc(sy.dma_start(out=xin[bi][:], in_=Xin[ti]), 16)
            v.wait_ge(sq_[bi].h, sq_[bi].base + 16 * (ti // 2 + 1))
            s.wait_ge(sq_[bi].h, sq_[bi].base + 16 * (ti // 2 + 1))
            emit_ln_stats(c, xin[bi][:], sc, sv, scratch)
            sc.wait(v)
            v.wait_ge(sp.h, sp.n)
            surow.wait(v)
            st3[bi].wait(v)
            sv.inc(v.tensor_scalar(ub[:], xin[bi][:], scratch["c2"][:], scratch["c3"][:],
                                   ALU.subtract, ALU.mult))
            if ti == 0:
                sv.wait(g)
                surow.inc(g.dma_start(out=urow[0:1, 0:1024], in_=ub[0:1, :]), 16)
            sv.wait(t)
            if ti == 0:
                sg.wait(t); sw.wait(t)
            emit_transpose_tiles(c, ub, uT[bi], ident, ps_tr, sp, sv, 8)
            sv.wait(t)
            if ti >= 1:
                t.wait_ge(sv.h, sv.n)
            for nb in range(4):
                for k in range(8):
                    t.matmul(ps_qk[:, 512 * nb:512 * (nb + 1)], uT[bi][:, 128 * k:128 * (k + 1)],
                             WqkT[:, k, 512 * nb:512 * (nb + 1)], start=(k == 0), stop=False)
                sp.inc(t.matmul(ps_qk[:, 512 * nb:512 * (nb + 1)], onesr[:],
                                bqk[0:1, 512 * nb:512 * (nb + 1)], start=False, stop=True))
            sp.wait(v)
            sv.inc(v.tensor_scalar(e0[:], ps_qk[:], 0.0, None, ALU.min))
            sv.wait(s)
            sc.inc(s.activation(e1[:], e0[:], AF.Exp))
            sv.wait(v); sc.wait(v)
            sv.inc(v.scalar_tensor_tensor(e0[:], e0[:], -1.0, ps_qk[:], ALU.mult, ALU.add))
            sc.wait(v); sv.wait(v)
            sv.inc(v.tensor_add(qke[:], e0[:], e1[:]))
            sv.wait(v)
            sv.inc(v.tensor_tensor(t1b[:], qke[:], cos_bc[:], ALU.mult))
            qv = qke[:].rearrange("p (b s j) -> p b s j", s=2, j=32)
            sv_ = sin_bc[:].rearrange("p (b s j) -> p b s j", s=2, j=32)
            t2v = t2b[:].rearrange("p (b s j) -> p b s j", s=2, j=32)
            sv.wait(v)
            sv.inc(v.tensor_tensor(t2v[:, :, 0, :], qv[:, :, 1, :], sv_[:, :, 0, :], ALU.mult))
            sv.inc(v.tensor_tensor(t2v[:, :, 1, :], qv[:, :, 0, :], sv_[:, :, 1, :], ALU.mult))
            sv.wait(v)
            sv.inc(v.tensor_add(qkr[:], t1b[:], t2b[:]))
            sv.wait(g)
            st3[bi].inc(g.dma_start(out=uTD[ti], in_=uT[bi][:]), 16)
            st3[bi].inc(g.dma_start(out=qeD[ti], in_=qke[:, 0:1024]), 16)
            st3[bi].inc(g.dma_start(out=qrD[ti], in_=qkr[:, 0:1024]), 16)
            sv.wait(t)
            for h in range(16):
                ins = t.matmul(ps_kv[0:64, 64 * h:64 * (h + 1)],
                               qkr[:, 1024 + 64 * h:1024 + 64 * (h + 1)],
                               ub[:, 64 * h:64 * (h + 1)],
                               start=(ti == 0 and h % 8 == 0), stop=(ti == NT - 1 and h % 8 == 7))
            sp.inc(ins)
            i2 = t.matmul(ps_qk[0:1, 0:512], onescb[:], qke[:, 1024:1536], start=True, stop=True)
            sp.inc(i2)
            i3 = t.matmul(ps_qk[0:1, 512:1024], onescb[:], qke[:, 1536:2048], start=True, stop=True)
            sp.inc(i3)
            sp.wait(v)
            if ti == 0:
                sv.inc(v.tensor_copy(ks_acc[0][:], ps_qk[0:1, 0:1024]))
            else:
                sv.inc(v.tensor_tensor(ks_acc[ti % 2][:], ks_acc[(ti + 1) % 2][:],
                                       ps_qk[0:1, 0:1024], ALU.add))
            if ti == NT - 1:
                sv.wait(g)
                surow.inc(g.dma_start(out=urow[0:1, 1024:2048], in_=ub[127:128, :]), 16)
            if ti + 2 < NT:
                sy.wait_ge(sv.h, sv.n)
                sy.wait_ge(sp.h, sp.n)
                sq_[bi].inc(sy.dma_start(out=xin[bi][:], in_=Xin[ti + 2]), 16)
    else:
        _sv = c.es; c.es = es0
        qel = _Ring(c, name + "_qel", 2, [128, 1024], BF16, kind="hw")
        qrl = _Ring(c, name + "_qrl", 2, [128, 1024], BF16, kind="hw")
        utl = _Ring(c, name + "_utl", 2, [128, 1024], BF16, kind="hw")
        c.es = _sv
        for ti in range(NT):
            bi = ti % 2
            if ti < 2:
                sq_[bi].inc(sy.dma_start(out=xin[bi][:], in_=Xin[ti]), 16)
                qel.dma(ti, sy.dma_start(out=qel.buf(ti)[:], in_=qeD[ti]))
                qrl.dma(ti, sy.dma_start(out=qrl.buf(ti)[:], in_=qrD[ti]))
                utl.dma(ti, sy.dma_start(out=utl.buf(ti)[:], in_=uTD[ti]))
            v.wait_ge(sq_[bi].h, sq_[bi].base + 16 * (ti // 2 + 1))
            for r_ in (qel, qrl, utl):
                r_.wait_loaded(v, ti)
                r_.wait_loaded(t, ti)
            uTb = utl.buf(ti)
            sv.wait(v)
            sv.inc(v.tensor_tensor(zt[:], qel.buf(ti)[:], ksum_bc[:], ALU.mult))
            sv.wait(v)
            sv.inc(v.tensor_reduce(zden[:].rearrange("p (h j) -> p h j", j=1),
                                   zt[:].rearrange("p (h j) -> p h j", j=64), AX.X, ALU.add))
            sv.wait(v)
            sv.inc(v.tensor_scalar(zden[:], zden[:], 1.0 / NSEQ, 1e-6, ALU.mult, ALU.add))
            sv.wait(v)
            sv.inc(v.reciprocal(zden[:], zden[:]))
            sv.wait(v)
            sv.inc(v.tensor_scalar(zden[:], zden[:], 1.0 / NSEQ, None, ALU.mult))
            sv.wait(v)
            sv.inc(v.tensor_tensor(qz[:].rearrange("p (h d) -> p h d", d=64),
                                   qrl.buf(ti)[:].rearrange("p (h d) -> p h d", d=64),
                                   zden[:].unsqueeze(2).broadcast_to([128, 16, 64]), ALU.mult))
            sv.wait(t)
            if ti == 0:
                sg.wait(t); sw.wait(t)
            emit_transpose_tiles(c, qz, qzT, ident, ps_tr, sp, sv, 8)
            sv.wait(t)
            if ti >= 1:
                t.wait_ge(sv.h, sv.n)
            for j_ in range(8):
                ins = t.matmul(ps_at[:, 128 * j_:128 * (j_ + 1)],
                               qzT[:, 128 * j_:128 * (j_ + 1)],
                               kvp[:, 128 * j_:128 * (j_ + 1)],
                               start=(j_ % 4 == 0), stop=(j_ % 4 == 3))
                if j_ % 4 == 3:
                    sp.inc(ins)
            lcp = ulastc[(ti + 1) % 2]
            pf = pef[bi]
            for m in range(8):
                um = uTb[:, 128 * m:128 * (m + 1)]
                sv.wait(v); sc.wait(v)
                sv.inc(v.tensor_scalar(pf[:, 128 * m:128 * (m + 1)], um, lepw[:, m, 1:2], None, ALU.mult))
                sv.wait(v)
                sv.inc(v.scalar_tensor_tensor(pf[:, 128 * m:128 * m + 127], um[:, 1:128],
                                              lepw[:, m, 2:3], pf[:, 128 * m:128 * m + 127],
                                              ALU.mult, ALU.add))
                sv.wait(v)
                sv.inc(v.scalar_tensor_tensor(pf[:, 128 * m + 1:128 * (m + 1)], um[:, 0:127],
                                              lepw[:, m, 0:1], pf[:, 128 * m + 1:128 * (m + 1)],
                                              ALU.mult, ALU.add))
                sv.wait(v)
                lsrc = halocols[:, 1, m:m + 1] if ti == 0 else lcp[:, m:m + 1]
                sv.inc(v.scalar_tensor_tensor(pf[:, 128 * m:128 * m + 1], lsrc,
                                              lepw[:, m, 0:1], pf[:, 128 * m:128 * m + 1],
                                              ALU.mult, ALU.add))
                sv.wait(v)
                sv.inc(v.tensor_scalar(pf[:, 128 * m:128 * (m + 1)], pf[:, 128 * m:128 * (m + 1)],
                                       lepb[:, m:m + 1], None, ALU.add))
            sv.wait(v)
            for m in range(8):
                sv.inc(v.tensor_copy(ulastc[bi][:, m:m + 1], uTb[:, 128 * m + 127:128 * (m + 1)]))
            sp.wait(v)
            sv.wait(v)
            sv.inc(v.tensor_add(x1s[bi][:], ps_at[:], xin[bi][:]))
            if ti >= 1:
                _finish_lepe_tile(c, env_loc, ti - 1, uTb, None)
            if ti + 2 < NT:
                sy.wait_ge(sv.h, sv.n)
                sq_[bi].inc(sy.dma_start(out=xin[bi][:], in_=Xin[ti + 2]), 16)
                qel.dma(ti + 2, sy.dma_start(out=qel.buf(ti)[:], in_=qeD[ti + 2]))
                qrl.dma(ti + 2, sy.dma_start(out=qrl.buf(ti)[:], in_=qrD[ti + 2]))
                utl.dma(ti + 2, sy.dma_start(out=utl.buf(ti)[:], in_=uTD[ti + 2]))
    if not pre:
        _finish_lepe_tile(c, env_loc, NT - 1, None, halocols)
        for i_ in range(2):
            xo.sems[i_].wait(g); xo.sems[i_].wait(sy)
    else:
        sp.wait(v); sv.wait(v)
        sv.inc(v.tensor_copy(kvf[:], ps_kv[0:64, :]))
        sv.inc(v.tensor_copy(ksf[:], ks_acc[(NT - 1) % 2][:]))
        sv.wait(g)
        surow.wait(g)
        sfin = c.sem(kind="sw")
        sfin.inc(g.dma_start(out=ccpart[0:64, :], in_=kvf[:]), 16)
        sfin.inc(g.dma_start(out=ccpart[64:65, :], in_=ksf[:]), 16)
        sfin.inc(g.dma_start(out=ccpart[65:66, :], in_=urow[0:1, 0:1024]), 16)
        sfin.inc(g.dma_start(out=ccpart[66:67, :], in_=urow[0:1, 1024:2048]), 16)
        sfin.wait(g); sfin.wait(sy)
    es0.close()
    return None

def _finish_lepe_tile(c, L, tj, u_next, halocols):
    t, v, s, g, sy = c.t, c.v, c.s, c.g, c.sy
    sp, sv, sc = L["sp"], L["sv"], L["sc"]
    pef, x1s, xo, lepw = L["pef"], L["x1s"], L["xo"], L["lepw"]
    pe_b, ps_pe, ident, Xout = L["pe_b"], L["ps_pe"], L["ident"], L["Xout"]
    bj = tj % 2
    pf = pef[bj]
    for m in range(8):
        rsrc = (halocols[:, 0, m:m + 1] if u_next is None
                else u_next[:, 128 * m:128 * m + 1])
        sv.wait(v)
        sv.inc(v.scalar_tensor_tensor(pf[:, 128 * m + 127:128 * (m + 1)], rsrc,
                                      lepw[:, m, 2:3], pf[:, 128 * m + 127:128 * (m + 1)],
                                      ALU.mult, ALU.add))
    sv.wait(v)
    sv.inc(v.tensor_copy(pe_b[:], pf[:]))
    sv.wait(t)
    t.wait_ge(sv.h, sv.n)
    for m in range(8):
        ins = t.transpose(ps_pe[:, 128 * m:128 * (m + 1)], pe_b[:, 128 * m:128 * (m + 1)], ident[:])
    sp.inc(ins)
    sp.wait(v)
    ob = xo.buf(tj)
    v.wait_ge(xo.sem(tj).h, xo.sem(tj).base + 16 * (tj // 2))
    sv.inc(v.tensor_add(ob[:], x1s[bj][:], ps_pe[:]))
    sv.wait(g)
    xo.dma(tj, g.dma_start(out=Xout[tj], in_=ob[:]))


def _perm_qk():
    idx = np.zeros(2048, np.int64)
    for half in range(2):
        for h in range(16):
            for kk in range(32):
                for s_ in range(2):
                    new = half * 1024 + h * 64 + 32 * s_ + kk
                    old = half * 1024 + h * 64 + 2 * kk + s_
                    idx[new] = old
    return idx

def _prep_weights_E(inputs):
    perm = _perm_qk()
    WqkT = np.ascontiguousarray(inputs["L1_attn_W_qk"][perm].T)
    bqk = np.ascontiguousarray(inputs["L1_attn_b_qk"][perm][None, :])
    lepw = np.ascontiguousarray(
        inputs["L1_attn_lepe_w"][:, 0, :].reshape(8, 128, 3).transpose(1, 0, 2))
    lepb = np.ascontiguousarray(inputs["L1_attn_lepe_b"].reshape(8, 128).T)
    kidx = np.zeros((1, 2048), np.float32)
    sgn = np.zeros((1, 2048), np.float32)
    for half in range(2):
        for h in range(16):
            for s_ in range(2):
                for kk in range(32):
                    pos = half * 1024 + h * 64 + 32 * s_ + kk
                    kidx[0, pos] = (h * 32 + kk) / 512.0
                    sgn[0, pos] = -1.0 if s_ == 0 else 1.0
    return dict(WqkT=WqkT, bqk=bqk, lepw=lepw, lepb=lepb, kidx=kidx, sgn=sgn)

def _core_masks_E(cid):
    b = cid // 2
    m = np.zeros((67, 8), np.float32)
    m[0:65, 2 * b] = 1.0
    m[0:65, 2 * b + 1] = 1.0
    if cid % 2 == 0:
        m[65, cid + 1] = 1.0
    else:
        m[66, cid - 1] = 1.0
    return dict(mask67=m)

def build_E(mode="full"):
    nc = bass.Bass("TRN2", target_bir_lowering=False)
    X = nc.dram_tensor("X", [33, 128, D], F32, kind="ExternalInput").ap()
    Wap = {k: nc.dram_tensor("W_" + k, list(s), F32, kind="ExternalInput").ap()
           for k, s in E_W_SHAPES.items()}
    es = ExitStack()
    c = Ctx(nc, es)
    if mode == "pre":
        ccpart = nc.dram_tensor("CP", [67, 1024], F32, kind="ExternalOutput").ap()
        emit_phase_linattn(c, X, None, Wap, "Ep", mode="pre", ccpart=ccpart)
    else:
        XO = nc.dram_tensor("XO", [33, 128, D], F32, kind="ExternalOutput").ap()
        ccg = nc.dram_tensor("CG", [8, 67, 1024], F32, kind="ExternalInput").ap()
        hrowd = nc.dram_tensor("S_hrow", [2, 1024], F32, kind="Internal").ap()
        emit_phase_linattn(c, X, XO, Wap, "Ef", mode="full", ccg=ccg, hrowd=hrowd)
    es.close()
    return nc

def _global_barrier(c):
    for eng in (c.t, c.v, c.s, c.g, c.sy):
        for sm in c._active:
            if sm.n > 0:
                sm.wait(eng)

def build_full():
    nc = bass.Bass("TRN2", target_bir_lowering=False)
    X = nc.dram_tensor("X", [33, 128, D], F32, kind="ExternalInput").ap()
    XO = nc.dram_tensor("XO", [33, 128, D], F32, kind="ExternalOutput").ap()
    Xs = nc.dram_tensor("S_X", [33, 128, D], F32, kind="Internal").ap()
    WapA = {k: nc.dram_tensor("WA_" + k, list(s), F32, kind="ExternalInput").ap()
            for k, s in A_W_SHAPES.items()}
    WapB = {k: nc.dram_tensor("WB_" + k, list(s), F32, kind="ExternalInput").ap()
            for k, s in B_W_SHAPES.items()}
    WapD = {k: nc.dram_tensor("WD_" + k, list(s), F32, kind="ExternalInput").ap()
            for k, s in D_W_SHAPES.items()}
    WapE = {k: nc.dram_tensor("WE_" + k, list(s), F32, kind="ExternalInput").ap()
            for k, s in E_W_SHAPES.items()}
    DsB = {k: nc.dram_tensor("SB_" + k, list(sh), dt_, kind="Internal",
                             addr_space=("Shared" if k == "ccout" else "Local")).ap()
           for k, (sh, dt_) in B_SCRATCH.items()}
    xntd = nc.dram_tensor("S_xnT", [8, 128, NTOK], BF16, kind="Internal").ap()
    ccinE = nc.dram_tensor("S_ccinE", [67, 1024], F32, kind="Internal").ap()
    ccoutE = nc.dram_tensor("S_ccoutE", [536, 1024], F32, kind="Internal",
                            addr_space="Shared").ap()
    hrowd = nc.dram_tensor("S_hrow", [2, 1024], F32, kind="Internal").ap()
    qeD = nc.dram_tensor("S_qeD", [NT, 128, D], BF16, kind="Internal").ap()
    qrD = nc.dram_tensor("S_qrD", [NT, 128, D], BF16, kind="Internal").ap()
    uTD = nc.dram_tensor("S_uTD", [NT, 128, D], BF16, kind="Internal").ap()

    es = ExitStack()
    c = Ctx(nc, es)
    c.const(EPS_LN); c.const(1.0); c.const(float(np.pi / 2))

    def phase(fn):
        es_p = ExitStack()
        old = c.es
        c.es = es_p
        fn()
        c.es = old
        _global_barrier(c)
        es_p.close()
        c.release_sems()

    wstateA = emit_L0FFN_weights(c, WapA, "AW")
    phase(lambda: emit_phase_L0FFN(c, X, Xs, WapA, NT + 1, "A", wsem=wstateA))
    phase(lambda: emit_phase_mamba_main(c, Xs, Xs, WapB, DsB, "B", mode="cc"))
    phase(lambda: emit_phase_L0FFN(c, Xs, Xs, WapA, NT, "C", wsem=wstateA))
    phase(lambda: emit_phase_L1FFN(c, Xs, Xs, WapD, xntd, "Dp", inplace=True))
    phase(lambda: emit_phase_linattn(c, Xs, None, WapE, "Ep", mode="pre", ccpart=ccinE,
                                     qeD=qeD, qrD=qrD, uTD=uTD))
    def _ecc():
        scc = c.sem(kind="cc")
        scc.inc(c.g.collective_compute(
            "AllGather", ALU.bypass, replica_groups=[list(range(NCORES))],
            ins=[ccinE.opt()], outs=[ccoutE.opt()]), 1)
        scc.wait(c.g)
    phase(_ecc)
    ccgv = ccoutE.rearrange("(s p) f -> s p f", p=67)
    phase(lambda: emit_phase_linattn(c, Xs, Xs, WapE, "Ef", mode="full",
                                     ccg=ccgv, hrowd=hrowd, qeD=qeD, qrD=qrD, uTD=uTD))
    phase(lambda: emit_phase_L1FFN(c, Xs, XO, WapD, xntd, "Fp", inplace=False))
    es.close()
    return nc

def kernel(**inputs):
    x = np.ascontiguousarray(inputs["L0_attn_x"], np.float32)
    shards = shard_stream(x)
    WA = _prep_weights_A(inputs)
    WB = _prep_weights_B(inputs)
    WD = _prep_weights_D(inputs)
    WE = _prep_weights_E(inputs)
    cores = list(range(NCORES))
    nc = build_full()
    in_maps = []
    for cid in cores:
        m = dict(X=shards[cid])
        m.update({"WA_" + k: v for k, v in WA.items()})
        m.update({"WB_" + k: v for k, v in WB.items()})
        m.update({"WB_" + k: v for k, v in _core_masks_B(cid).items()})
        m.update({"WD_" + k: v for k, v in WD.items()})
        m.update({"WE_" + k: v for k, v in WE.items()})
        m.update({"WE_" + k: v for k, v in _core_masks_E(cid).items()})
        in_maps.append(m)
    res = run_bass_kernel_spmd(nc, in_maps, core_ids=cores)
    out = np.empty((4, 8192, D), np.float32)
    for cid in cores:
        b, h = cid // 2, cid % 2
        out[b, h * NTOK:(h + 1) * NTOK] = (
            np.asarray(res.results[cid]["XO"]).reshape(NT + 1, 128, D)[:NT].reshape(NTOK, D))
    return out
